# version 04
# baseline (speedup 1.0000x reference)
import math
import numpy as np
import ml_dtypes
from contextlib import ExitStack

import concourse.bass as bass
import concourse.mybir as mybir
from concourse.bass_utils import run_bass_kernel_spmd

F32 = mybir.dt.float32
BF16 = mybir.dt.bfloat16
AF = mybir.ActivationFunctionType
ALU = mybir.AluOpType
AX = mybir.AxisListType
BF = ml_dtypes.bfloat16

S = 2048
DM = 1024
NT = 16
NCH = 8
EPS = 1e-5

class Sem:
    def __init__(self, handle, name):
        self.h = handle
        self.name = name
        self.n = 0

    def inc(self, ins, k=1):
        ins.then_inc(self.h, k)
        self.n += k
        return (self, self.n)

    def tok(self):
        return (self, self.n)


def wait(eng, *toks):
    for t in toks:
        if t is None:
            continue
        sem, val = t
        if val > 0:
            eng.wait_ge(sem.h, val)


class Prog:
    def __init__(self):
        self.nc = bass.Bass("TRN2", target_bir_lowering=False)
        self.st = ExitStack()
        self.sems = {}
        self.consts_ready = None

    def sem(self, name):
        if name not in self.sems:
            h = self.st.enter_context(self.nc.semaphore(name))
            self.sems[name] = Sem(h, name)
        return self.sems[name]

    def sb(self, name, shape, dt):
        return self.st.enter_context(self.nc.sbuf_tensor(name, shape, dt))

    def psum(self, name, shape, dt=F32):
        return self.st.enter_context(self.nc.psum_tensor(name, shape, dt))

    def din(self, name, shape, dt=F32):
        return self.nc.dram_tensor(name, shape, dt, kind="ExternalInput").ap()

    def dout(self, name, shape, dt=F32):
        return self.nc.dram_tensor(name, shape, dt, kind="ExternalOutput").ap()

    def dtmp(self, name, shape, dt=F32):
        return self.nc.dram_tensor(name, shape, dt).ap()

    def close(self):
        self.st.close()
        return self.nc

    def init_consts(self):
        nc = self.nc
        self.ones_bf = self.sb("ones_bf", [128, 1], BF16)
        self.ones_f32col = self.sb("ones_f32col", [128, 1], F32)
        self.ones_row_f = self.sb("ones_row_f", [1, 128], F32)
        self.ones_row_bf16 = self.sb("ones_row_bf16", [1, 128], BF16)
        self.cpool = self.sb("cpool", [128, 4], F32)
        g = self.sem("g")
        with nc.Block() as blk:
            @blk.gpsimd
            def _(gpsimd):
                gpsimd.memset(self.ones_bf[:], 1.0)
                gpsimd.memset(self.ones_f32col[:], 1.0)
                gpsimd.memset(self.ones_row_f[:], 1.0)
                gpsimd.memset(self.ones_row_bf16[:], 1.0)
                gpsimd.memset(self.cpool[:, 0:1], 1e-5)
                gpsimd.memset(self.cpool[:, 1:2], 24.0)
                gpsimd.memset(self.cpool[:, 2:3], float(math.log(0.125)))
                t = g.inc(gpsimd.memset(self.cpool[:, 3:4], -10.0))
        self.consts_ready = g.tok()


def chunk(buf, c, width=S):
    return buf[:, c * width:(c + 1) * width]


def emit_big_load(p, dst_sb, src_ap, cast=False, deps=(), sem=None):
    if not hasattr(p, "_ldctr"):
        p._ldctr = 0
        p.load_toks = []
    if sem is not None:
        ld = sem
    else:
        ld = p.sem("uld%d" % p._ldctr)
        p._ldctr += 1
    with p.nc.Block() as blk:
        if cast:
            @blk.gpsimd
            def _(gpsimd):
                wait(gpsimd, *deps)
                ld.inc(gpsimd.dma_start(out=dst_sb, in_=src_ap), 16)
        else:
            @blk.sync
            def _(sync):
                wait(sync, *deps)
                ld.inc(sync.dma_start(out=dst_sb, in_=src_ap), 16)
    p.load_toks.append(ld.tok())
    return ld.tok()


def emit_layernorm(p, SRC, LNB, tag, deps=(), affine=None, scratch=None):
    nc = p.nc
    sV, sT, sS, sG = p.sem("v"), p.sem("t"), p.sem("s"), p.sem("g")
    st2 = ExitStack()
    if scratch is None:
        SQ = st2.enter_context(nc.sbuf_tensor(tag + "_sq", [128, 2 * 512], BF16))[:]
        ROWS = st2.enter_context(nc.sbuf_tensor(tag + "_rows", [1, 3 * S], F32))[:]
        BCA = st2.enter_context(nc.sbuf_tensor(tag + "_bca", [128, S], F32))[:]
        BCB = st2.enter_context(nc.sbuf_tensor(tag + "_bcb", [128, S], F32))[:]
        TMP = st2.enter_context(nc.sbuf_tensor(tag + "_tmp", [128, 2 * 512], BF16))[:]
    else:
        SQ, ROWS, BCA, BCB, TMP = scratch
    stStat = ExitStack()
    PS = stStat.enter_context(nc.psum_tensor(tag + "_ps", [1, S], F32))
    PQ = stStat.enter_context(nc.psum_tensor(tag + "_pq", [1, S], F32))

    v0, t0, s0 = sV.n, sT.n, sS.n
    with nc.Block() as blk:
        @blk.vector
        def _(vector):
            wait(vector, *deps)
            for c in range(NCH):
                sV.inc(vector.tensor_copy(out=chunk(LNB, c), in_=chunk(SRC, c)))

        @blk.scalar
        def _(scalar):
            for i in range(32):
                c, sb = divmod(i, 4)
                wait(scalar, (sV, v0 + c + 1))
                if i >= 2:
                    wait(scalar, (sT, t0 + (i - 2) + 1))
                sS.inc(scalar.activation(
                    out=SQ[:, (i % 2) * 512:(i % 2) * 512 + 512],
                    in_=LNB[:, c * S + 512 * sb: c * S + 512 * (sb + 1)],
                    func=AF.Square))

        @blk.tensor
        def _(tensor):
            wait(tensor, p.consts_ready)
            for i in range(32):
                c, sb = divmod(i, 4)
                wait(tensor, (sV, v0 + c + 1), (sS, s0 + i + 1))
                tensor.matmul(out=PS[0:1, 512 * sb:512 * (sb + 1)],
                              lhsT=p.ones_bf[:],
                              rhs=LNB[:, c * S + 512 * sb: c * S + 512 * (sb + 1)],
                              start=(c == 0), stop=(c == NCH - 1))
                mm = tensor.matmul(out=PQ[0:1, 512 * sb:512 * (sb + 1)],
                                   lhsT=p.ones_bf[:],
                                   rhs=SQ[:, (i % 2) * 512:(i % 2) * 512 + 512],
                                   start=(c == 0), stop=(c == NCH - 1))
                sT.inc(mm)

    with nc.Block() as blk:
        @blk.vector
        def _(vector):
            wait(vector, (sT, t0 + 32))
            vector.tensor_scalar_mul(out=ROWS[0:1, 0:S], in0=PS[0:1, :], scalar1=1.0 / DM)
            sV.inc(vector.tensor_scalar_mul(out=ROWS[0:1, 2 * S:3 * S], in0=PQ[0:1, :],
                                            scalar1=1.0 / DM))
        vA = sV.n

        @blk.scalar
        def _(scalar):
            wait(scalar, (sV, vA))
            sS.inc(scalar.activation(out=ROWS[0:1, S:2 * S], in_=ROWS[0:1, 0:S],
                                     func=AF.Square))
        sA = sS.n

        @blk.gpsimd
        def _(gpsimd):
            wait(gpsimd, (sS, sA))
            sG.inc(gpsimd.tensor_sub(out=ROWS[0:1, 2 * S:3 * S],
                                     in0=ROWS[0:1, 2 * S:3 * S],
                                     in1=ROWS[0:1, S:2 * S]))
        gA = sG.n

    with nc.Block() as blk:
        @blk.scalar
        def _(scalar):
            wait(scalar, (sG, gA))
            sS.inc(scalar.activation(out=ROWS[0:1, S:2 * S], in_=ROWS[0:1, 2 * S:3 * S],
                                     func=AF.Sqrt, bias=p.cpool[0:1, 0:1]))
        sB = sS.n

        @blk.vector
        def _(vector):
            wait(vector, (sS, sB))
            trec = sV.inc(vector.reciprocal(out=ROWS[0:1, S:2 * S], in_=ROWS[0:1, S:2 * S]))
            wait(vector, trec)
            sV.inc(vector.tensor_mul(out=ROWS[0:1, 2 * S:3 * S], in0=ROWS[0:1, 0:S],
                                     in1=ROWS[0:1, S:2 * S]))
        vB = sV.n

    stStat.close()
    PB = st2.enter_context(nc.psum_tensor(tag + "_pb", [128, 512], F32))
    PC = st2.enter_context(nc.psum_tensor(tag + "_pc", [128, 512], F32))
    tB = sT.n
    vC = sV.n
    with nc.Block() as blk:
        @blk.tensor
        def _(tensor):
            wait(tensor, (sV, vB))
            for sb in range(4):
                if sb >= 1:
                    wait(tensor, (sV, vC + 2 * (sb - 1) + 2))
                sT.inc(tensor.matmul(out=PB[:, 0:512], lhsT=p.ones_row_f[:],
                                     rhs=ROWS[0:1, S + 512 * sb:S + 512 * (sb + 1)],
                                     start=True, stop=True))
                sT.inc(tensor.matmul(out=PC[:, 0:512], lhsT=p.ones_row_f[:],
                                     rhs=ROWS[0:1, 2 * S + 512 * sb:2 * S + 512 * (sb + 1)],
                                     start=True, stop=True))

        @blk.vector
        def _(vector):
            for sb in range(4):
                wait(vector, (sT, tB + 2 * sb + 1))
                sV.inc(vector.tensor_copy(out=BCA[:, 512 * sb:512 * (sb + 1)], in_=PB[:, 0:512]))
                wait(vector, (sT, tB + 2 * sb + 2))
                sV.inc(vector.tensor_copy(out=BCB[:, 512 * sb:512 * (sb + 1)], in_=PC[:, 0:512]))

    vD = sV.n
    gD = sG.n
    with nc.Block() as blk:
        @blk.vector
        def _(vector):
            wait(vector, (sV, vD))
            for i in range(32):
                c, sb = divmod(i, 4)
                if i >= 2:
                    wait(vector, (sG, gD + (i - 2) + 1))
                sV.inc(vector.tensor_mul(
                    out=TMP[:, (i % 2) * 512:(i % 2) * 512 + 512],
                    in0=LNB[:, c * S + 512 * sb: c * S + 512 * (sb + 1)],
                    in1=BCA[:, 512 * sb:512 * (sb + 1)]))

        @blk.gpsimd
        def _(gpsimd):
            for i in range(32):
                c, sb = divmod(i, 4)
                wait(gpsimd, (sV, vD + i + 1))
                sG.inc(gpsimd.tensor_sub(
                    out=LNB[:, c * S + 512 * sb: c * S + 512 * (sb + 1)],
                    in0=TMP[:, (i % 2) * 512:(i % 2) * 512 + 512],
                    in1=BCB[:, 512 * sb:512 * (sb + 1)]))

    done = sG.tok()
    if affine is not None:
        Wc, Bc = affine
        gE = sG.n - 32
        with nc.Block() as blk:
            @blk.vector
            def _(vector):
                for c in range(NCH):
                    wait(vector, (sG, gE + 4 * (c + 1)))
                    sV.inc(vector.tensor_scalar(out=chunk(LNB, c), in0=chunk(LNB, c),
                                                scalar1=Wc[:, c:c + 1], scalar2=Bc[:, c:c + 1],
                                                op0=ALU.mult, op1=ALU.add))
        done = sV.tok()
    st2.close()
    return done


def emit_store_xT(p, X, dst_ap, deps):
    if not hasattr(p, "_ldctr"):
        p._ldctr = 0
    gd = p.sem("uld%d" % p._ldctr)
    p._ldctr += 1
    with p.nc.Block() as blk:
        @blk.gpsimd
        def _(gpsimd):
            wait(gpsimd, *deps)
            t = gd.inc(gpsimd.dma_start(
                out=dst_ap.rearrange("(c p) s -> p c s", p=128),
                in_=X[:].rearrange("p (c s) -> p c s", c=NCH)), 16)
    return gd.tok()


def emit_final_wait(p, toks):
    all_toks = list(toks) + getattr(p, "load_toks", [])
    with p.nc.Block() as blk:
        @blk.sync
        def _(sync):
            wait(sync, *all_toks)


def emit_l0_ffn(p, X, deps):
    nc = p.nc
    sV, sT, sS, sG = p.sem("v"), p.sem("t"), p.sem("s"), p.sem("g")
    st2 = ExitStack()
    sbuf = lambda n, sh, dt: st2.enter_context(nc.sbuf_tensor(n, sh, dt))
    psum = lambda n, sh: st2.enter_context(nc.psum_tensor(n, sh, F32))

    wt = p.din("l0f_wt", [1024, 1024])
    wpt = p.din("l0f_wpt", [1024, 256])
    bp = p.din("l0f_bp", [128, 2])
    simc = p.din("l0f_sim", [128, 8])
    temp = p.din("l0f_temp", [1, 1])
    wdt = p.din("l0f_wdt", [128, 120])
    wut = p.din("l0f_wut", [15, 1024])

    LNB = sbuf("bf_lnb", [128, NCH * S], BF16)
    WT16 = sbuf("bf_wt", [128, NCH * 1024], BF16)
    WPT16 = sbuf("bf_wpt", [128, NCH * 256], BF16)
    BPC = sbuf("bf_bp", [128, 2], F32)
    SIMC = sbuf("bf_simc", [128, 8], F32)
    SIMSQ = sbuf("bf_simsq", [128, 8], F32)
    SIMN = sbuf("bf_simn", [128, 8], BF16)
    TEMP = sbuf("bf_temp", [1, 1], F32)
    WDT16 = sbuf("bf_wdt", [128, 120], BF16)
    WUT16 = sbuf("bf_wut", [15, 1024], BF16)
    tok_wt = emit_big_load(p, WT16[:].rearrange("p (c f) -> p c f", c=NCH),
                           wt.rearrange("(c p) f -> p c f", p=128), cast=True,
                           deps=deps)
    tok_wpt = emit_big_load(p, WPT16[:].rearrange("p (c f) -> p c f", c=NCH),
                            wpt.rearrange("(c p) f -> p c f", p=128), cast=True,
                            deps=deps)
    tok_wdt = emit_big_load(p, WDT16[:], wdt, cast=True, deps=deps)
    tok_wut = emit_big_load(p, WUT16[:], wut, cast=True, deps=deps)
    tok_bp = emit_big_load(p, BPC[:], bp, deps=deps)
    tok_sim = emit_big_load(p, SIMC[:], simc, deps=deps)
    tok_temp = emit_big_load(p, TEMP[:], temp, deps=deps)

    tok_ln = emit_layernorm(p, X, LNB, "bf_ln", deps=deps)

    PRJ = sbuf("bf_prj", [128, 2 * S], BF16)
    PSQ = sbuf("bf_psq", [128, 2 * 512], BF16)
    R1 = sbuf("bf_r1", [1, S], F32)
    R4 = sbuf("bf_r4", [1, 16], F32)
    IV4 = sbuf("bf_iv4", [128, 4], F32)
    INVC = sbuf("bf_invc", [128, 16], F32)
    GT = sbuf("bf_gt", [128, 64], F32)
    GM = sbuf("bf_gm", [128, 16], F32)
    GMK = sbuf("bf_gmk", [128, 64], F32)
    RETT = sbuf("bf_rett", [4, S], BF16)
    RETX = sbuf("bf_retx", [15, S], BF16)
    DT = sbuf("bf_dt", [15, S], BF16)

    stPN = ExitStack()
    PN = stPN.enter_context(nc.psum_tensor("bf_pn", [1, S], F32))
    stPP = ExitStack()
    PP = stPP.enter_context(nc.psum_tensor("bf_pp", [128, 512], F32))
    PP2 = stPP.enter_context(nc.psum_tensor("bf_pp2", [128, 512], F32))
    t0 = sT.n
    s0 = sS.n
    sched = []
    for i in range(8):
        sched.append(("P", i))
        if i >= 2:
            sched.append(("N", i - 2))
    sched += [("N", 6), ("N", 7)]
    t_at = {}
    for k, ev in enumerate(sched):
        t_at[ev] = t0 + k + 1
    with nc.Block() as blk:
        @blk.tensor
        def _(tensor):
            wait(tensor, tok_ln, tok_wpt, tok_bp)
            for ev in sched:
                kind, i = ev
                if kind == "P":
                    pc, sb = divmod(i, 4)
                    PPx = PP if i % 2 == 0 else PP2
                    if i >= 2:
                        wait(tensor, (sS, s0 + 2 * (i - 2) + 2))
                    for dc in range(NCH):
                        mm = tensor.matmul(
                            out=PPx[:, 0:512],
                            lhsT=WPT16[:, dc * 256 + 128 * pc: dc * 256 + 128 * (pc + 1)],
                            rhs=LNB[:, dc * S + 512 * sb: dc * S + 512 * (sb + 1)],
                            start=(dc == 0), stop=(dc == NCH - 1))
                    sT.inc(mm)
                else:
                    pc, sb = divmod(i, 4)
                    wait(tensor, (sS, s0 + 2 * i + 2))
                    mm = tensor.matmul(out=PN[0:1, 512 * sb:512 * (sb + 1)],
                                       lhsT=p.ones_bf[:],
                                       rhs=PSQ[:, (i % 2) * 512:(i % 2) * 512 + 512],
                                       start=(pc == 0), stop=(pc == 1))
                    sT.inc(mm)

        @blk.scalar
        def _(scalar):
            for i in range(8):
                pc, sb = divmod(i, 4)
                wait(scalar, (sT, t_at[("P", i)]))
                if i >= 2:
                    wait(scalar, (sT, t_at[("N", i - 2)]))
                tevac = sS.inc(scalar.activation(
                    out=PRJ[:, pc * S + 512 * sb: pc * S + 512 * (sb + 1)],
                    in_=(PP if i % 2 == 0 else PP2)[:, 0:512],
                    func=AF.Identity, bias=BPC[:, pc:pc + 1]))
                wait(scalar, tevac)
                sS.inc(scalar.activation(
                    out=PSQ[:, (i % 2) * 512:(i % 2) * 512 + 512],
                    in_=PRJ[:, pc * S + 512 * sb: pc * S + 512 * (sb + 1)],
                    func=AF.Square))
    tok_projmm = sT.tok()
    stPP.close()

    sB0 = sS.n
    with nc.Block() as blk:
        @blk.scalar
        def _(scalar):
            wait(scalar, tok_projmm)
            sS.inc(scalar.activation(out=R1[0:1, :], in_=PN[0:1, :], func=AF.Sqrt))

        @blk.vector
        def _(vector):
            wait(vector, (sS, sB0 + 1))
            tmx = sV.inc(vector.tensor_scalar_max(out=R1[0:1, :], in0=R1[0:1, :], scalar1=1e-12))
            wait(vector, tmx)
            sV.inc(vector.reciprocal(out=R1[0:1, :], in_=R1[0:1, :]))
    tok_invn = sV.tok()
    stPN.close()

    stPIC = ExitStack()
    PIC = stPIC.enter_context(nc.psum_tensor("bf_pic", [128, 16], F32))
    tI = sT.n
    with nc.Block() as blk:
        @blk.tensor
        def _(tensor):
            wait(tensor, tok_invn)
            for t in range(16):
                tr = tensor.transpose(out=PIC[:, t:t + 1], in_=R1[0:1, 128 * t:128 * (t + 1)],
                                      identity=p.ones_row_f[0:1, 0:1])
            sT.inc(tr)

        @blk.vector
        def _(vector):
            wait(vector, (sT, tI + 1))
            sV.inc(vector.tensor_copy(out=INVC[:], in_=PIC[:]))
    tok_invc = sV.tok()
    stPIC.close()

    sC0, vC0, tC0, gC0 = sS.n, sV.n, sT.n, sG.n
    stS8 = ExitStack()
    PS8 = stS8.enter_context(nc.psum_tensor("bf_ps8", [1, 16], F32))
    PIV = stS8.enter_context(nc.psum_tensor("bf_piv", [128, 4], F32))
    with nc.Block() as blk:
        @blk.scalar
        def _(scalar):
            wait(scalar, tok_sim)
            sS.inc(scalar.activation(out=SIMSQ[:], in_=SIMC[:], func=AF.Square))
            wait(scalar, tok_temp, (sG, gC0 + 1))
            sS.inc(scalar.activation(out=TEMP[:], in_=TEMP[:], func=AF.Exp))

        @blk.gpsimd
        def _(gpsimd):
            wait(gpsimd, tok_temp)
            sG.inc(gpsimd.tensor_scalar_min(out=TEMP[:], in0=TEMP[:],
                                            scalar1=float(np.log(1.0 / 0.01))))

        @blk.tensor
        def _(tensor):
            wait(tensor, (sS, sC0 + 1))
            sT.inc(tensor.matmul(out=PS8[0:1, 0:8], lhsT=p.ones_f32col[:],
                                 rhs=SIMSQ[:], start=True, stop=True))

        @blk.vector
        def _(vector):
            wait(vector, (sT, tC0 + 1))
            tcp = sV.inc(vector.tensor_copy(out=R4[0:1, 8:16], in_=PS8[0:1, 0:8]))
            wait(vector, tcp)
            sV.inc(vector.tensor_add(out=R4[0:1, 0:4], in0=R4[0:1, 8:12], in1=R4[0:1, 12:16]))
    sD0, vD0, tD0 = sS.n, sV.n, sT.n
    with nc.Block() as blk:
        @blk.scalar
        def _(scalar):
            wait(scalar, (sV, vD0))
            sS.inc(scalar.activation(out=R4[0:1, 4:8], in_=R4[0:1, 0:4], func=AF.Sqrt))

        @blk.vector
        def _(vector):
            wait(vector, (sS, sD0 + 1))
            t1 = sV.inc(vector.tensor_scalar_max(out=R4[0:1, 4:8], in0=R4[0:1, 4:8], scalar1=1e-12))
            wait(vector, t1)
            t2 = sV.inc(vector.reciprocal(out=R4[0:1, 4:8], in_=R4[0:1, 4:8]))
            wait(vector, t2)
            sV.inc(vector.tensor_scalar_mul(out=R4[0:1, 4:8], in0=R4[0:1, 4:8],
                                            scalar1=TEMP[0:1, 0:1]))

        @blk.tensor
        def _(tensor):
            wait(tensor, (sV, vD0 + 3))
            sT.inc(tensor.matmul(out=PIV[:], lhsT=p.ones_row_f[:], rhs=R4[0:1, 4:8],
                                 start=True, stop=True))

        @blk.scalar
        def _(scalar):
            wait(scalar, (sT, tD0 + 1), (sS, sD0 + 1))
            sS.inc(scalar.activation(out=IV4[:], in_=PIV[:], func=AF.Copy))
    sE0 = sS.n
    with nc.Block() as blk:
        @blk.vector
        def _(vector):
            wait(vector, (sS, sE0))
            sV.inc(vector.tensor_tensor(
                out=SIMN[:].rearrange("p (c e) -> p c e", c=2),
                in0=SIMC[:].rearrange("p (c e) -> p c e", c=2),
                in1=bass.AP(IV4, 0, [[4, 128], [0, 2], [1, 4]]),
                op=ALU.mult))
    tok_simn = sV.tok()
    stS8.close()

    stPG = ExitStack()
    PG = stPG.enter_context(nc.psum_tensor("bf_pg", [128, 64], F32))
    tG0 = sT.n
    with nc.Block() as blk:
        @blk.tensor
        def _(tensor):
            wait(tensor, tok_simn, tok_projmm)
            for t in range(16):
                for pc in range(2):
                    mm = tensor.matmul(out=PG[:, 4 * t:4 * (t + 1)],
                                       lhsT=PRJ[:, pc * S + 128 * t: pc * S + 128 * (t + 1)],
                                       rhs=SIMN[:, 4 * pc:4 * (pc + 1)],
                                       start=(pc == 0), stop=(pc == 1))
            sT.inc(mm)

        @blk.vector
        def _(vector):
            wait(vector, (sT, tG0 + 1), tok_invc)
            t1 = sV.inc(vector.tensor_tensor(
                out=GT[:].rearrange("p (t e) -> p t e", t=16),
                in0=PG[:].rearrange("p (t e) -> p t e", t=16),
                in1=bass.AP(INVC, 0, [[16, 128], [1, 16], [0, 4]]),
                op=ALU.mult))
            wait(vector, t1)
            t2 = sV.inc(vector.tensor_reduce(out=GM[:], in_=GT[:].rearrange("p (t e) -> p t e", t=16),
                                             axis=AX.X, op=ALU.max, negate=True))
            wait(vector, t2)
            sV.inc(vector.tensor_tensor(
                out=GT[:].rearrange("p (t e) -> p t e", t=16),
                in0=GT[:].rearrange("p (t e) -> p t e", t=16),
                in1=bass.AP(GM, 0, [[16, 128], [1, 16], [0, 4]]),
                op=ALU.add))
        vG0 = sV.n

        @blk.scalar
        def _(scalar):
            wait(scalar, (sV, vG0))
            sS.inc(scalar.activation(out=GT[:], in_=GT[:], func=AF.Exp))
        sG0 = sS.n

    vH0 = sV.n
    sH0 = sS.n
    with nc.Block() as blk:
        @blk.vector
        def _(vector):
            wait(vector, (sS, sG0))
            t1 = sV.inc(vector.tensor_reduce(out=GM[:], in_=GT[:].rearrange("p (t e) -> p t e", t=16),
                                             axis=AX.X, op=ALU.add))
            wait(vector, t1)
            t2 = sV.inc(vector.reciprocal(out=GM[:], in_=GM[:]))
            wait(vector, t2)
            t3 = sV.inc(vector.tensor_tensor(
                out=GT[:].rearrange("p (t e) -> p t e", t=16),
                in0=GT[:].rearrange("p (t e) -> p t e", t=16),
                in1=bass.AP(GM, 0, [[16, 128], [1, 16], [0, 4]]),
                op=ALU.mult))
            wait(vector, t3)
            t4 = sV.inc(vector.tensor_reduce(out=GM[:], in_=GT[:].rearrange("p (t e) -> p t e", t=16),
                                             axis=AX.X, op=ALU.max))
            wait(vector, t4)
            t5 = sV.inc(vector.tensor_tensor(
                out=GMK[:].rearrange("p (t e) -> p t e", t=16),
                in0=GT[:].rearrange("p (t e) -> p t e", t=16),
                in1=bass.AP(GM, 0, [[16, 128], [1, 16], [0, 4]]),
                op=ALU.is_ge))
            wait(vector, t5)
            sV.inc(vector.tensor_mul(out=GT[:], in0=GT[:], in1=GMK[:]))
    tok_ret = sV.tok()
    stPG.close()

    stPRT = ExitStack()
    PRT = stPRT.enter_context(nc.psum_tensor("bf_prt", [4, 128], F32))
    PRT2 = stPRT.enter_context(nc.psum_tensor("bf_prt2", [4, 128], F32))
    tR0 = sT.n
    vR0 = sV.n
    with nc.Block() as blk:
        @blk.tensor
        def _(tensor):
            wait(tensor, tok_ret, p.tok_idf)
            for t in range(16):
                if t >= 2:
                    wait(tensor, (sV, vR0 + (t - 2) + 1))
                sT.inc(tensor.transpose(out=(PRT if t % 2 == 0 else PRT2)[:],
                                        in_=GT[:, 4 * t:4 * (t + 1)],
                                        identity=p.IDF[:]))

        @blk.vector
        def _(vector):
            for t in range(16):
                wait(vector, (sT, tR0 + t + 1))
                sV.inc(vector.tensor_copy(out=RETT[0:4, 128 * t:128 * (t + 1)],
                                          in_=(PRT if t % 2 == 0 else PRT2)[:]))
    tok_rett = sV.tok()

    stPRT.close()
    stPRX = ExitStack()
    PRX = stPRX.enter_context(nc.psum_tensor("bf_prx", [15, 512], F32))
    tX0 = sT.n
    vX0 = sV.n
    with nc.Block() as blk:
        @blk.tensor
        def _(tensor):
            wait(tensor, tok_rett, p.tok_e415)
            for sb in range(4):
                if sb >= 1:
                    wait(tensor, (sV, vX0 + sb))
                sT.inc(tensor.matmul(out=PRX[:], lhsT=p.E415[:],
                                     rhs=RETT[0:4, 512 * sb:512 * (sb + 1)],
                                     start=True, stop=True))

        @blk.vector
        def _(vector):
            for sb in range(4):
                wait(vector, (sT, tX0 + sb + 1))
                sV.inc(vector.tensor_copy(out=RETX[0:15, 512 * sb:512 * (sb + 1)], in_=PRX[:]))
    tok_retx = sV.tok()

    stPRX.close()
    stPD = ExitStack()
    PD = stPD.enter_context(nc.psum_tensor("bf_pd", [15, 512], F32))
    tD1 = sT.n
    vD1 = sV.n
    with nc.Block() as blk:
        @blk.tensor
        def _(tensor):
            wait(tensor, tok_wdt, tok_retx)
            for sb in range(4):
                if sb >= 1:
                    wait(tensor, (sV, vD1 + sb))
                for dc in range(NCH):
                    mm = tensor.matmul(out=PD[:],
                                       lhsT=WDT16[:, 15 * dc:15 * (dc + 1)],
                                       rhs=LNB[:, dc * S + 512 * sb: dc * S + 512 * (sb + 1)],
                                       start=(dc == 0), stop=(dc == NCH - 1))
                sT.inc(mm)

        @blk.vector
        def _(vector):
            for sb in range(4):
                wait(vector, (sT, tD1 + sb + 1))
                sV.inc(vector.tensor_tensor(out=DT[0:15, 512 * sb:512 * (sb + 1)], in0=PD[:],
                                            in1=RETX[0:15, 512 * sb:512 * (sb + 1)],
                                            op=ALU.mult))
    tok_dt = sV.tok()

    stPD.close()
    stPO = ExitStack()
    PO = stPO.enter_context(nc.psum_tensor("bf_po", [128, 512], F32))
    PO2 = stPO.enter_context(nc.psum_tensor("bf_po2", [128, 512], F32))
    tF0 = sT.n
    vF0 = sV.n
    with nc.Block() as blk:
        @blk.tensor
        def _(tensor):
            wait(tensor, tok_wt, tok_wut, tok_dt)
            i = 0
            for oc in range(NCH):
                for sb in range(4):
                    POx = PO if i % 2 == 0 else PO2
                    if i >= 2:
                        wait(tensor, (sV, vF0 + (i - 2) + 1))
                    for dc in range(NCH):
                        tensor.matmul(
                            out=POx[:, 0:512],
                            lhsT=WT16[:, dc * 1024 + 128 * oc: dc * 1024 + 128 * (oc + 1)],
                            rhs=LNB[:, dc * S + 512 * sb: dc * S + 512 * (sb + 1)],
                            start=(dc == 0), stop=False)
                    mm = tensor.matmul(out=POx[:, 0:512],
                                       lhsT=WUT16[0:15, 128 * oc:128 * (oc + 1)],
                                       rhs=DT[0:15, 512 * sb:512 * (sb + 1)],
                                       start=False, stop=True)
                    sT.inc(mm)
                    i += 1

        @blk.vector
        def _(vector):
            i = 0
            for oc in range(NCH):
                for sb in range(4):
                    wait(vector, (sT, tF0 + i + 1))
                    dst = X[:, oc * S + 512 * sb: oc * S + 512 * (sb + 1)]
                    sV.inc(vector.scalar_tensor_tensor(
                        out=dst, in0=(PO if i % 2 == 0 else PO2)[:, 0:512],
                        scalar=1.0, in1=dst, op0=ALU.mult, op1=ALU.add))
                    i += 1
    tok_done = sV.tok()
    stPO.close()
    st2.close()
    return tok_done



def emit_spill(p, X, dram_ap, deps):
    ld = p.sem("uld%d" % p._ldctr); p._ldctr += 1
    with p.nc.Block() as blk:
        @blk.gpsimd
        def _(gpsimd):
            wait(gpsimd, *deps)
            ld.inc(gpsimd.dma_start(out=dram_ap.rearrange("(c p) s -> p c s", p=128),
                                    in_=X[:].rearrange("p (c s) -> p c s", c=NCH)), 16)
    p.load_toks.append(ld.tok())
    return ld.tok()


def emit_restore(p, X, dram_ap, deps):
    ld = p.sem("uld%d" % p._ldctr); p._ldctr += 1
    with p.nc.Block() as blk:
        @blk.sync
        def _(sync):
            wait(sync, *deps)
            ld.inc(sync.dma_start(out=X[:].rearrange("p (c s) -> p c s", c=NCH),
                                  in_=dram_ap.rearrange("(c p) s -> p c s", p=128)), 16)
    p.load_toks.append(ld.tok())
    return ld.tok()


def emit_l0_attn(p, X, deps, stop_at=None, dbg_raw_ot=False):
    nc = p.nc
    sV, sT, sS, sG = p.sem("v"), p.sem("t"), p.sem("s"), p.sem("g")
    st2 = ExitStack()
    sbuf = lambda n, sh, dt: st2.enter_context(nc.sbuf_tensor(n, sh, dt))

    w1t = p.din("l0a_w1t", [1024, 1024])
    w2t = p.din("l0a_w2t", [1024, 1024])
    w3t = p.din("l0a_w3t", [1024, 1024])
    wot = p.din("l0a_wot", [1024, 1024])
    cost = p.din("l0a_cos", [128, 2048])
    sint = p.din("l0a_sina", [128, 2048])
    xsp = p.dtmp("l0a_xsp", [1024, S])

    LNB = sbuf("af_lnb", [128, NCH * S], BF16)
    A16 = sbuf("af_a16", [128, NT * 1024], BF16)
    B16 = sbuf("af_b16", [128, NCH * S], BF16)
    C16 = sbuf("af_c16", [128, NCH * S], BF16)
    TMPA = sbuf("af_tmpa", [128, 2 * 1024], BF16)
    TMPB = sbuf("af_tmpb", [128, 2 * 1024], BF16)
    SQH = sbuf("af_sqh", [128, 2048], BF16)
    XB = X[:].bitcast(BF16)
    XF = X[:]
    D16 = XB
    WTB = XB[:, 16384:24576]
    COS = XF[:, 12288:14336]
    SIN = XF[:, 14336:16384]
    LNBF = LNB[:].bitcast(F32)
    LNBB = LNB[:]
    EXPT = LNBB[:, 0:1024]
    BCS = LNBF[:, 512:1024]
    SR = LNBF[0:1, 1024:1536]
    RLN = LNBF[0:1, 1536:2048]
    ISR = LNBF[0:1, 2048:2560]
    MQ = LNBF[0:1, 2560:2592]
    NMB = LNBF[:, 2592:2600]
    WOB = LNBB[:, 6144:14336]
    B16F = B16[:].bitcast(F32)
    C16F = C16[:].bitcast(F32)
    ln_scratch = (A16[:, 0:1024], C16F[0:1, 0:6144], B16F[:, 0:2048],
                  B16F[:, 2048:4096], A16[:, 1024:2048])

    tok_ln = emit_layernorm(p, X, LNB, "af_ln", deps=deps, scratch=ln_scratch)
    if stop_at == "ln":
        st2.pop_all()
        return ("dump16", LNB[:], tok_ln)
    tok_spill = emit_spill(p, X, xsp, [tok_ln])
    tok_cos = emit_big_load(p, COS, cost, deps=[tok_spill])
    tok_sin = emit_big_load(p, SIN, sint, deps=[tok_spill])

    stPT = ExitStack()
    PT = stPT.enter_context(nc.psum_tensor("af_pt", [128, 512], F32))
    PT2 = stPT.enter_context(nc.psum_tensor("af_pt2", [128, 512], F32))
    PTRa = stPT.enter_context(nc.psum_tensor("af_ptra", [128, 128], BF16))
    PTRb = stPT.enter_context(nc.psum_tensor("af_ptrb", [128, 128], BF16))

    def project_tok(wsrc, tok_w, qscale, prior_tok):
        t0, s0 = sT.n, sS.n
        evac_toks = []
        with nc.Block() as blk:
            @blk.tensor
            def _(tensor):
                wait(tensor, tok_ln, tok_w, prior_tok)
                for i in range(32):
                    t, half = divmod(i, 2)
                    PTx = PT if i % 2 == 0 else PT2
                    if i >= 2:
                        wait(tensor, (sS, s0 + (i - 2) + 1))
                    for ec in range(NCH):
                        mm = tensor.matmul(
                            out=PTx[:, 0:512],
                            lhsT=LNB[:, ec * S + 128 * t: ec * S + 128 * (t + 1)],
                            rhs=WTB[:, ec * 1024 + 512 * half: ec * 1024 + 512 * (half + 1)],
                            start=(ec == 0), stop=(ec == NCH - 1))
                    sT.inc(mm)

            @blk.scalar
            def _(scalar):
                for i in range(32):
                    t, half = divmod(i, 2)
                    wait(scalar, (sT, t0 + i + 1))
                    tk = sS.inc(scalar.activation(
                        out=A16[:, t * 1024 + 512 * half: t * 1024 + 512 * (half + 1)],
                        in_=(PT if i % 2 == 0 else PT2)[:, 0:512],
                        func=AF.Identity, scale=qscale))
                    evac_toks.append(tk)
        return evac_toks, sS.tok()

    def rope_inplace(tok_proj_evacs):
        v0, g0 = sV.n, sG.n
        add_toks = []
        with nc.Block() as blk:
            @blk.vector
            def _(vector):
                wait(vector, tok_cos, tok_sin)
                for t in range(NT):
                    tv = t % 2
                    wait(vector, tok_proj_evacs[2 * t + 1])
                    if t >= 2:
                        wait(vector, (sV, v0 + 3 * (t - 2) + 3))
                    tm = sV.inc(vector.tensor_mul(
                        out=TMPA[:, (t % 2) * 1024:(t % 2) * 1024 + 1024],
                        in0=A16[:, t * 1024:(t + 1) * 1024],
                        in1=COS[:, tv * 1024:(tv + 1) * 1024]))
                    swap = bass.AP(A16, t * 1024 + 1,
                                   [[NT * 1024, 128], [2, 512], [-1, 2]])
                    tw = sV.inc(vector.tensor_mul(
                        out=TMPB[:, (t % 2) * 1024:(t % 2) * 1024 + 1024],
                        in0=swap.rearrange("p a b -> p (a b)") if False else swap,
                        in1=SIN[:, tv * 1024:(tv + 1) * 1024].rearrange("p (a b) -> p a b", b=2)))
                    wait(vector, tw)
                    tk = sV.inc(vector.tensor_add(
                        out=A16[:, t * 1024:(t + 1) * 1024],
                        in0=TMPA[:, (t % 2) * 1024:(t % 2) * 1024 + 1024],
                        in1=TMPB[:, (t % 2) * 1024:(t % 2) * 1024 + 1024]))
                    add_toks.append(tk)
        return add_toks

    def transpose_to(dst, truej, tile_toks):
        t0, v0 = sT.n, sV.n
        with nc.Block() as blk:
            @blk.tensor
            def _(tensor):
                wait(tensor, p.tok_idb)
                for i in range(128):
                    t, c = divmod(i, 8)
                    wait(tensor, tile_toks[t])
                    if i >= 2:
                        wait(tensor, (sV, v0 + (i - 2) + 1))
                    sT.inc(tensor.transpose(
                        out=(PTRa if i % 2 == 0 else PTRb)[:],
                        in_=A16[:, t * 1024 + 128 * c: t * 1024 + 128 * (c + 1)],
                        identity=p.IDB[:]))

            @blk.vector
            def _(vector):
                for i in range(128):
                    t, c = divmod(i, 8)
                    h, trel = divmod(t, 2)
                    wait(vector, (sT, t0 + i + 1))
                    if truej:
                        dstap = bass.AP(dst, h * 2048 + 1024 * trel + c,
                                        [[NCH * S, 128], [8, 128]])
                    else:
                        dstap = dst[:, h * 2048 + 256 * c + 128 * trel:
                                    h * 2048 + 256 * c + 128 * trel + 128]
                    sV.inc(vector.tensor_copy(
                        out=dstap, in_=(PTRa if i % 2 == 0 else PTRb)[:]))
        return sV.tok()

    tok_w = emit_big_load(p, WTB.rearrange("p (c f) -> p c f", c=NCH),
                          w1t.rearrange("(c p) f -> p c f", p=128), cast=True,
                          deps=[tok_spill])
    ev, tok_pq = project_tok(w1t, tok_w, 1.0 / math.sqrt(8.0), None)
    if stop_at == "qproj":
        st2.pop_all()
        return ("dump16", A16[:], tok_pq)
    rtoks = rope_inplace(ev)
    if stop_at == "qrope":
        st2.pop_all()
        return ("dump16", A16[:], rtoks[-1])
    tok_qf = transpose_to(B16, True, rtoks)
    if stop_at == "qf":
        st2.pop_all()
        return ("dump16", B16[:], tok_qf)
    tok_w = emit_big_load(p, WTB.rearrange("p (c f) -> p c f", c=NCH),
                          w2t.rearrange("(c p) f -> p c f", p=128), cast=True,
                          deps=[tok_pq])
    ev, tok_pk = project_tok(w2t, tok_w, 1.0, tok_qf)
    rtoks = rope_inplace(ev)
    tok_kf = transpose_to(C16, False, rtoks)
    if stop_at == "kf":
        st2.pop_all()
        return ("dump16", C16[:], tok_kf)
    tok_w = emit_big_load(p, WTB.rearrange("p (c f) -> p c f", c=NCH),
                          w3t.rearrange("(c p) f -> p c f", p=128), cast=True,
                          deps=[tok_pk])
    ev, tok_pv = project_tok(w3t, tok_w, 1.0, tok_kf)
    tok_v = tok_pv
    if stop_at == "v":
        st2.pop_all()
        return ("dump16", A16[:], tok_v)

    stPT.close()
    stNR = ExitStack()
    PNR = stNR.enter_context(nc.psum_tensor("af_pnr", [1, S], F32))
    tN0, sN0, vN0 = sT.n, sS.n, sV.n
    with nc.Block() as blk:
        @blk.scalar
        def _(scalar):
            wait(scalar, tok_qf, tok_kf)
            for it in range(16):
                src = B16 if it < 8 else C16
                h = it % 8
                if it > 0:
                    wait(scalar, (sT, tN0 + 4 * it))
                sS.inc(scalar.activation(out=SQH[:], in_=src[:, h * S:(h + 1) * S],
                                         func=AF.Square))

        @blk.tensor
        def _(tensor):
            wait(tensor, p.consts_ready)
            for it in range(16):
                wait(tensor, (sS, sN0 + it + 1))
                if it > 0:
                    wait(tensor, (sV, vN0 + it))
                for sb in range(4):
                    mm = tensor.matmul(out=PNR[0:1, 512 * sb:512 * (sb + 1)],
                                       lhsT=p.ones_bf[:],
                                       rhs=SQH[:, 512 * sb:512 * (sb + 1)],
                                       start=True, stop=True)
                    sT.inc(mm)

        @blk.vector
        def _(vector):
            for it in range(16):
                wait(vector, (sT, tN0 + 4 * it + 4))
                sV.inc(vector.tensor_reduce(out=MQ[0:1, it:it + 1], in_=PNR[0:1, :],
                                            axis=AX.X, op=ALU.max))
    vN1, sN1, tN1 = sV.n, sS.n, sT.n
    stBQ = ExitStack()
    PBQ = stBQ.enter_context(nc.psum_tensor("af_pbq", [128, 8], F32))
    with nc.Block() as blk:
        @blk.vector
        def _(vector):
            wait(vector, (sV, vN1))
            tk = sV.inc(vector.tensor_mul(out=MQ[0:1, 16:24], in0=MQ[0:1, 0:8],
                                          in1=MQ[0:1, 8:16]))
        @blk.scalar
        def _(scalar):
            wait(scalar, (sV, vN1 + 1))
            sS.inc(scalar.activation(out=MQ[0:1, 24:32], in_=MQ[0:1, 16:24],
                                     func=AF.Sqrt))
        @blk.gpsimd
        def _(gpsimd):
            wait(gpsimd, (sS, sN1 + 1))
            sG.inc(gpsimd.tensor_scalar_mul(out=MQ[0:1, 24:32], in0=MQ[0:1, 24:32],
                                            scalar1=-1.0))
    gN1 = sG.n
    with nc.Block() as blk:
        @blk.tensor
        def _(tensor):
            wait(tensor, (sG, gN1))
            sT.inc(tensor.matmul(out=PBQ[:], lhsT=p.ones_row_f[:], rhs=MQ[0:1, 24:32],
                                 start=True, stop=True))
        @blk.scalar
        def _(scalar):
            wait(scalar, (sT, tN1 + 1))
            sS.inc(scalar.activation(out=NMB[:], in_=PBQ[:], func=AF.Copy))
    tok_nmb = sS.tok()
    stBQ.close()
    stNR.close()
    if stop_at == "nmb":
        st2.pop_all()
        return ("dumpf", NMB, tok_nmb)

    stAT = ExitStack()
    OT = stAT.enter_context(nc.psum_tensor("af_ot", [128, S], F32))
    STa = stAT.enter_context(nc.psum_tensor("af_sta", [128, 512], F32))
    STb = stAT.enter_context(nc.psum_tensor("af_stb", [128, 512], F32))
    SIG = stAT.enter_context(nc.psum_tensor("af_sig", [1, 512], F32))
    BCp = stAT.enter_context(nc.psum_tensor("af_bcp", [128, 512], F32))

    t0, s0, v0 = sT.n, sS.n, sV.n
    def t_s(h, qb, k):
        base = t0 + (h * 4 + qb) * 33
        return base + (1 if k == 0 else 2 * k)

    def t_sig(h, qb, k):
        base = t0 + (h * 4 + qb) * 33
        return base + (2 * k + 3 if k < 15 else 32)

    def t_bc(h, qb):
        return t0 + (h * 4 + qb) * 33 + 33

    def s_exp(h, qb, k):
        return s0 + (h * 4 + qb) * 18 + k + 1

    def s_row(h, qb, j):
        return s0 + (h * 4 + qb) * 18 + 16 + j

    def v_sig(h, qb):
        return v0 + (h * 4 + qb) * 3 + 1

    def v_bcs(h, qb):
        return v0 + (h * 4 + qb) * 3 + 2

    def v_ot(h, qb):
        return v0 + (h * 4 + qb) * 3 + 3

    with nc.Block() as blk:
        @blk.tensor
        def _(tensor):
            wait(tensor, tok_nmb, tok_v)
            for h in range(8):
                for qb in range(4):
                    idx = h * 4 + qb
                    if idx > 0:
                        wait(tensor, (sV, v_sig(h if qb else h - 1, qb - 1 if qb else 3)))
                    if h > 0:
                        wait(tensor, (sV, v_ot(h - 1, qb)))
                    for k in range(17):
                        if k < 16:
                            g_ = idx * 16 + k
                            if g_ >= 2:
                                i2, k2 = divmod(g_ - 2, 16)
                                wait(tensor, (sS, s_exp(i2 // 4, i2 % 4, k2)))
                            STx = STa if k % 2 == 0 else STb
                            sT.inc(tensor.matmul(
                                out=STx[:, 0:512],
                                lhsT=C16[:, h * S + 128 * k: h * S + 128 * (k + 1)],
                                rhs=B16[:, h * S + 512 * qb: h * S + 512 * (qb + 1)],
                                start=True, stop=True))
                        if k >= 1:
                            kk = k - 1
                            wait(tensor, (sS, s_exp(h, qb, kk)))
                            tile = 2 * h + (kk % 2)
                            jlo = kk // 2
                            tensor.matmul(
                                out=OT[:, 512 * qb:512 * (qb + 1)],
                                lhsT=A16[:, tile * 1024 + 128 * jlo:
                                         tile * 1024 + 128 * (jlo + 1)],
                                rhs=EXPT[:, (kk % 2) * 512:(kk % 2) * 512 + 512],
                                start=(kk == 0), stop=(kk == 15))
                            sT.inc(tensor.matmul(
                                out=SIG[0:1, 0:512],
                                lhsT=p.ones_bf[:],
                                rhs=EXPT[:, (kk % 2) * 512:(kk % 2) * 512 + 512],
                                start=(kk == 0), stop=(kk == 15)))
                    wait(tensor, (sS, s_row(h, qb, 2)))
                    if idx > 0:
                        wait(tensor, (sV, v_bcs(h if qb else h - 1, qb - 1 if qb else 3)))
                    sT.inc(tensor.matmul(out=BCp[:, 0:512], lhsT=p.ones_row_f[:],
                                         rhs=ISR[0:1, 0:512], start=True, stop=True))

        @blk.scalar
        def _(scalar):
            for h in range(8):
                for qb in range(4):
                    for k in range(16):
                        wait(scalar, (sT, t_s(h, qb, k)))
                        g_ = (h * 4 + qb) * 16 + k
                        if g_ >= 2:
                            i2, k2 = divmod(g_ - 2, 16)
                            wait(scalar, (sT, t_sig(i2 // 4, i2 % 4, k2)))
                        sS.inc(scalar.activation(
                            out=EXPT[:, (k % 2) * 512:(k % 2) * 512 + 512],
                            in_=(STa if k % 2 == 0 else STb)[:, 0:512],
                            func=AF.Exp, bias=NMB[:, h:h + 1]))
                    wait(scalar, (sV, v_sig(h, qb)))
                    if h * 4 + qb > 0:
                        pv = (h, qb - 1) if qb else (h - 1, 3)
                        wait(scalar, (sT, t_bc(*pv)))
                    t1 = sS.inc(scalar.activation(out=RLN[0:1, 0:512], in_=SR[0:1, 0:512],
                                                  func=AF.Ln))
                    wait(scalar, t1)
                    sS.inc(scalar.activation(out=ISR[0:1, 0:512], in_=RLN[0:1, 0:512],
                                             func=AF.Exp, scale=-1.0,
                                             bias=p.cpool[0:1, 1:2]))

        @blk.gpsimd
        def _(gpsimd):
            if dbg_raw_ot == "rows":
                wait(gpsimd, (sV, v0 + 1))
                gpsimd.tensor_copy(out=XF[0:1, 12288:12800], in_=SR[0:1, 0:512])
                wait(gpsimd, (sS, s0 + 18))
                gpsimd.tensor_copy(out=XF[0:1, 12800:13312], in_=RLN[0:1, 0:512])
                sG.inc(gpsimd.tensor_copy(out=XF[0:1, 13312:13824], in_=ISR[0:1, 0:512]))

        @blk.vector
        def _(vector):
            wait(vector, tok_spill)
            for h in range(8):
                for qb in range(4):
                    wait(vector, (sT, t_sig(h, qb, 15)))
                    if h * 4 + qb > 0:
                        pv = (h, qb - 1) if qb else (h - 1, 3)
                        wait(vector, (sS, s_row(*pv, 1)))
                    sV.inc(vector.tensor_scalar_mul(out=SR[0:1, 0:512],
                                                    in0=SIG[0:1, 0:512],
                                                    scalar1=26489122129.84347))
                    wait(vector, (sT, t_bc(h, qb)))
                    tb = sV.inc(vector.tensor_copy(out=BCS[:], in_=BCp[:, 0:512]))
                    wait(vector, tb)
                    if dbg_raw_ot == "bcs":
                        sV.inc(vector.tensor_copy(
                            out=D16[:, h * S + 512 * qb: h * S + 512 * (qb + 1)],
                            in_=BCS[:]))
                    elif dbg_raw_ot:
                        sV.inc(vector.tensor_copy(
                            out=D16[:, h * S + 512 * qb: h * S + 512 * (qb + 1)],
                            in_=OT[:, 512 * qb:512 * (qb + 1)]))
                    else:
                        sV.inc(vector.scalar_tensor_tensor(
                            out=D16[:, h * S + 512 * qb: h * S + 512 * (qb + 1)],
                            in0=OT[:, 512 * qb:512 * (qb + 1)],
                            scalar=1.0, in1=BCS[:], op0=ALU.mult, op1=ALU.mult))
    tok_attn = sV.tok()
    stAT.close()
    if stop_at == "attn":
        st2.pop_all()
        return ("dump16", D16[:, 0:16384], tok_attn)
    if stop_at == "rows":
        st2.pop_all()
        return ("dumpf", XF[0:1, 12288:13824], [tok_attn, sG.tok()])

    stYT = ExitStack()
    PYa = stYT.enter_context(nc.psum_tensor("af_pya", [128, 128], BF16))
    PYb = stYT.enter_context(nc.psum_tensor("af_pyb", [128, 128], BF16))
    tY0, vY0 = sT.n, sV.n
    with nc.Block() as blk:
        @blk.tensor
        def _(tensor):
            wait(tensor, tok_attn)
            for i in range(128):
                h, jb = divmod(i, 16)
                if i >= 2:
                    wait(tensor, (sV, vY0 + (i - 2) + 1))
                sT.inc(tensor.transpose(
                    out=(PYa if i % 2 == 0 else PYb)[:],
                    in_=D16[:, h * S + 128 * jb: h * S + 128 * (jb + 1)],
                    identity=p.IDB[:]))

        @blk.vector
        def _(vector):
            for i in range(128):
                h, jb = divmod(i, 16)
                wait(vector, (sT, tY0 + i + 1))
                sV.inc(vector.tensor_copy(
                    out=B16[:, jb * 1024 + 128 * h: jb * 1024 + 128 * (h + 1)],
                    in_=(PYa if i % 2 == 0 else PYb)[:]))
    tok_y = sV.tok()
    stYT.close()
    if stop_at == "y":
        st2.pop_all()
        return ("dump16", B16[:], tok_y)

    tok_wo = emit_big_load(p, WOB.rearrange("p (c f) -> p c f", c=NCH),
                           wot.rearrange("(c p) f -> p c f", p=128), cast=True,
                           deps=[tok_attn])
    tok_xr = emit_restore(p, X, xsp, [tok_y])
    stRO = ExitStack()
    RO = stRO.enter_context(nc.psum_tensor("af_ro", [128, 512], F32))
    RO2 = stRO.enter_context(nc.psum_tensor("af_ro2", [128, 512], F32))
    tR0, vR0 = sT.n, sV.n
    with nc.Block() as blk:
        @blk.tensor
        def _(tensor):
            wait(tensor, tok_y, tok_wo, tok_xr)
            i = 0
            for half in range(2):
                for cb in range(NCH):
                    for eb in range(2):
                        ROx = RO if i % 2 == 0 else RO2
                        if i >= 2:
                            wait(tensor, (sV, vR0 + (i - 2) + 1))
                        for ncd in range(NCH):
                            mm = tensor.matmul(
                                out=ROx[:, 0:512],
                                lhsT=B16[:, (8 * half + ncd) * 1024 + 128 * cb:
                                         (8 * half + ncd) * 1024 + 128 * (cb + 1)],
                                rhs=WOB[:, ncd * 1024 + 512 * eb: ncd * 1024 + 512 * (eb + 1)],
                                start=(ncd == 0), stop=(ncd == NCH - 1))
                        sT.inc(mm)
                        i += 1

        @blk.vector
        def _(vector):
            i = 0
            for half in range(2):
                for cb in range(NCH):
                    for eb in range(2):
                        wait(vector, (sT, tR0 + i + 1))
                        dst = X[:, cb * S + 1024 * half + 512 * eb:
                                cb * S + 1024 * half + 512 * eb + 512]
                        sV.inc(vector.scalar_tensor_tensor(
                            out=dst, in0=(RO if i % 2 == 0 else RO2)[:, 0:512],
                            scalar=1.0, in1=dst, op0=ALU.mult, op1=ALU.add))
                        i += 1
    tok_done = sV.tok()
    stRO.close()
    st2.close()
    return tok_done


def build_l0attn_dbg(stop_at, dbg_raw_ot=False):
    p = Prog()
    xin = p.din("xin", [1024, S])
    p.init_consts()
    _load_phase_consts(p, need=("idb",))
    X = p.sb("X", [128, NCH * S], F32)
    tok_x = emit_big_load(p, X[:].rearrange("p (c s) -> p c s", c=NCH),
                          xin.rearrange("(c p) s -> p c s", p=128))
    kind, ap, tok = emit_l0_attn(p, X, deps=[tok_x], stop_at=stop_at, dbg_raw_ot=dbg_raw_ot)
    if kind == "dump16":
        dbg = p.dout("dbg", list(ap.shape), BF16)
    else:
        dbg = p.dout("dbg", list(ap.shape), F32)
    ld = p.sem("uld%d" % p._ldctr); p._ldctr += 1
    toks = tok if isinstance(tok, list) else [tok]
    with p.nc.Block() as blk:
        @blk.gpsimd
        def _(gpsimd):
            wait(gpsimd, *toks)
            ld.inc(gpsimd.dma_start(out=dbg, in_=ap), 16)
    p.load_toks.append(ld.tok())
    emit_final_wait(p, [])
    return p.nc, p


def build_l0attn_prog():
    if "l0attn" in _prog_cache:
        return _prog_cache["l0attn"]
    p = Prog()
    xin = p.din("xin", [1024, S])
    xout = p.dout("xout", [1024, S])
    p.init_consts()
    _load_phase_consts(p, need=("idb",))
    X = p.sb("X", [128, NCH * S], F32)
    tok_x = emit_big_load(p, X[:].rearrange("p (c s) -> p c s", c=NCH),
                          xin.rearrange("(c p) s -> p c s", p=128))
    tok = emit_l0_attn(p, X, deps=[tok_x])
    tok_st = emit_store_xT(p, X, xout, [tok])
    emit_final_wait(p, [tok_st])
    nc = p.close()
    _prog_cache["l0attn"] = (nc, p)
    return nc, p


def _prep_l0attn(inputs):
    d = {}
    d["l0a_w1t"] = np.ascontiguousarray(inputs["L0_attn_W1"].T)
    d["l0a_w2t"] = np.ascontiguousarray(inputs["L0_attn_W2"].T)
    d["l0a_w3t"] = np.ascontiguousarray(inputs["L0_attn_W3"].T)
    d["l0a_wot"] = np.ascontiguousarray(inputs["L0_attn_Wo"].T)
    inv = 1.0 / 10000.0 ** (np.arange(0, 128, 2, dtype=np.float64) / 128.0)
    srow = np.arange(128)
    tvar = np.arange(2)
    pidx = np.arange(1024)
    jlo = pidx // 128
    dloc = pidx % 128
    ii = dloc // 2
    cos = np.zeros((128, 2, 1024), np.float64)
    sina = np.zeros((128, 2, 1024), np.float64)
    for tv in range(2):
        j = 8 * (128 * tv + srow)[:, None] + jlo[None, :]
        th = j * inv[ii][None, :]
        cos[:, tv, :] = np.cos(th)
        sg = np.where(dloc % 2 == 0, -1.0, 1.0)
        sina[:, tv, :] = np.sin(th) * sg[None, :]
    d["l0a_cos"] = cos.reshape(128, 2048).astype(np.float32)
    d["l0a_sina"] = sina.reshape(128, 2048).astype(np.float32)
    return d



def emit_l1_attn(p, X, deps):
    nc = p.nc
    sV, sT, sS, sG = p.sem("v"), p.sem("t"), p.sem("s"), p.sem("g")
    st2 = ExitStack()
    sbuf = lambda n, sh, dt: st2.enter_context(nc.sbuf_tensor(n, sh, dt))

    wqt = p.din("l1c_wqt", [1024, 3072])
    wot = p.din("l1c_wot", [1024, 1024])
    hid = p.din("c_hid16", [128, 128], BF16)
    expd = p.din("c_expd", [16, 1024], BF16)
    xsp = p.dtmp("l1c_xsp", [1024, S])

    LNB = sbuf("cf_lnb", [128, NCH * S], BF16)
    A16 = sbuf("cf_a16", [128, NCH * S], BF16)
    B16 = sbuf("cf_b16", [128, NT * 1024], BF16)
    C16 = sbuf("cf_c16", [128, NCH * S], BF16)
    BKS = sbuf("cf_bks", [128, 256], F32)
    BKC = sbuf("cf_bkc", [128, 264], F32)
    SCNA = sbuf("cf_scna", [128, 32], F32)
    SCNB = sbuf("cf_scnb", [128, 32], F32)
    HID = sbuf("cf_hid", [128, 128], BF16)
    EXPD = sbuf("cf_expd", [16, 1024], BF16)
    CTXR = sbuf("cf_ctxr", [128, 64], F32)
    CTXB = sbuf("cf_ctxb", [128, 64], BF16)
    XB = X[:].bitcast(BF16)
    VTOK = XB
    WTB = XB[:, 16384:24576]
    LNBF = LNB[:].bitcast(F32)
    QIS = LNBF[0:16, 0:2048]
    QISB16 = LNB[0:16, 4096:6144]
    DINVB = LNB[0:16, 6144:8192]
    DINV = LNBF[0:16, 4096:6144]
    DVX = LNB[:, 14336:16384]
    WOT = LNB[:, 0:8192]
    B16F = B16[:].bitcast(F32)
    C16F = C16[:].bitcast(F32)
    ln_scratch = (A16[:, 0:1024], C16F[0:1, 0:6144], B16F[:, 0:2048],
                  B16F[:, 2048:4096], A16[:, 1024:2048])

    tok_ln = emit_layernorm(p, X, LNB, "cf_ln", deps=deps, scratch=ln_scratch)
    tok_spill = emit_spill(p, X, xsp, [tok_ln])
    tok_hid = emit_big_load(p, HID[:], hid, deps=deps)
    tok_expd = emit_big_load(p, EXPD[:], expd, deps=deps)

    stPT = ExitStack()
    PT = stPT.enter_context(nc.psum_tensor("cf_pt", [128, 512], F32))
    PT2 = stPT.enter_context(nc.psum_tensor("cf_pt2", [128, 512], F32))

    def project(col_off, dst_sl, func, prior_tok, featmaj):
        tok_w = emit_big_load(p, WTB.rearrange("p (c f) -> p c f", c=NCH),
                              wqt[:, col_off:col_off + 1024].rearrange("(c p) f -> p c f", p=128),
                              cast=True, deps=[tok_spill] + ([prior_tok] if prior_tok else []))
        t0, s0 = sT.n, sS.n
        with nc.Block() as blk:
            @blk.tensor
            def _(tensor):
                wait(tensor, tok_ln, tok_w)
                for i in range(32):
                    PTx = PT if i % 2 == 0 else PT2
                    if i >= 2:
                        wait(tensor, (sS, s0 + (i - 2) + 1))
                    if featmaj:
                        oc, sb = divmod(i, 4)
                        for ec in range(NCH):
                            mm = tensor.matmul(
                                out=PTx[:, 0:512],
                                lhsT=WTB[:, ec * 1024 + 128 * oc: ec * 1024 + 128 * (oc + 1)],
                                rhs=LNB[:, ec * S + 512 * sb: ec * S + 512 * (sb + 1)],
                                start=(ec == 0), stop=(ec == NCH - 1))
                    else:
                        t, half = divmod(i, 2)
                        for ec in range(NCH):
                            mm = tensor.matmul(
                                out=PTx[:, 0:512],
                                lhsT=LNB[:, ec * S + 128 * t: ec * S + 128 * (t + 1)],
                                rhs=WTB[:, ec * 1024 + 512 * half: ec * 1024 + 512 * (half + 1)],
                                start=(ec == 0), stop=(ec == NCH - 1))
                    sT.inc(mm)

            @blk.scalar
            def _(scalar):
                for i in range(32):
                    wait(scalar, (sT, t0 + i + 1))
                    if featmaj:
                        oc, sb = divmod(i, 4)
                        dst = dst_sl[:, oc * S + 512 * sb: oc * S + 512 * (sb + 1)]
                    else:
                        t, half = divmod(i, 2)
                        dst = dst_sl[:, t * 1024 + 512 * half: t * 1024 + 512 * (half + 1)]
                    sS.inc(scalar.activation(out=dst, in_=(PT if i % 2 == 0 else PT2)[:, 0:512],
                                             func=func))
        return sS.tok()

    tok_q = project(0, A16[:], AF.Exp, None, True)
    tok_k = project(1024, C16[:], AF.Exp, tok_q, True)
    tok_v = project(2048, VTOK, AF.Identity, tok_k, False)

    stQS = ExitStack()
    QSUMP = stQS.enter_context(nc.psum_tensor("cf_qsum", [16, S], F32))
    t1, v1, s1 = sT.n, sV.n, sS.n
    with nc.Block() as blk:
        @blk.tensor
        def _(tensor):
            wait(tensor, tok_q, tok_hid)
            for c in range(NCH):
                for sb in range(4):
                    mm = tensor.matmul(out=QSUMP[0:16, 512 * sb:512 * (sb + 1)],
                                       lhsT=HID[:, 16 * c:16 * (c + 1)],
                                       rhs=A16[:, c * S + 512 * sb: c * S + 512 * (sb + 1)],
                                       start=(c == 0), stop=(c == NCH - 1))
            sT.inc(mm)

        @blk.scalar
        def _(scalar):
            wait(scalar, (sT, t1 + 1))
            tq = sS.inc(scalar.activation(out=QIS, in_=QSUMP[0:16, :], func=AF.Ln))
            wait(scalar, tq)
            sS.inc(scalar.activation(out=QIS, in_=QIS, func=AF.Exp, scale=-1.0,
                                     bias=p.cpool[0:16, 2:3]))

        @blk.vector
        def _(vector):
            wait(vector, (sS, s1 + 2))
            sV.inc(vector.tensor_copy(out=QISB16, in_=QIS))
    tok_qis = sV.tok()
    stQS.close()

    stQX = ExitStack()
    PXQ = stQX.enter_context(nc.psum_tensor("cf_pxq", [128, 512], F32))
    t2, v2 = sT.n, sV.n
    with nc.Block() as blk:
        @blk.tensor
        def _(tensor):
            wait(tensor, tok_qis, tok_expd)
            for i in range(32):
                c, sb = divmod(i, 4)
                if i >= 1:
                    wait(tensor, (sV, v2 + i))
                sT.inc(tensor.matmul(out=PXQ[:, 0:512],
                                     lhsT=EXPD[0:16, 128 * c:128 * (c + 1)],
                                     rhs=QISB16[0:16, 512 * sb:512 * (sb + 1)],
                                     start=True, stop=True))

        @blk.vector
        def _(vector):
            for i in range(32):
                c, sb = divmod(i, 4)
                wait(vector, (sT, t2 + i + 1))
                sl = A16[:, c * S + 512 * sb: c * S + 512 * (sb + 1)]
                sV.inc(vector.tensor_tensor(out=sl, in0=sl, in1=PXQ[:, 0:512],
                                            op=ALU.mult))
    tok_qsm = sV.tok()
    stQX.close()

    v3 = sV.n
    with nc.Block() as blk:
        @blk.vector
        def _(vector):
            wait(vector, tok_k)
            for c in range(NCH):
                tr = sV.inc(vector.tensor_reduce(
                    out=BKS[:, 32 * c:32 * (c + 1)],
                    in_=C16[:, c * S:(c + 1) * S].rearrange("p (n s) -> p n s", s=64),
                    axis=AX.X, op=ALU.add))
                wait(vector, tr)
                tz = sV.inc(vector.memset(BKC[:, 33 * c:33 * c + 1], 0.0))
                wait(vector, tz)
                sa_, sb_ = SCNA, SCNB
                tc = sV.inc(vector.tensor_copy(out=sa_[:], in_=BKS[:, 32 * c:32 * (c + 1)]))
                wait(vector, tc)
                for k in [1, 2, 4, 8, 16]:
                    ta = sV.inc(vector.tensor_copy(out=sb_[:, 0:k], in_=sa_[:, 0:k]))
                    wait(vector, ta)
                    tb = sV.inc(vector.tensor_add(out=sb_[:, k:32], in0=sa_[:, k:32],
                                                  in1=sa_[:, 0:32 - k]))
                    wait(vector, tb)
                    sa_, sb_ = sb_, sa_
                tf = sV.inc(vector.tensor_copy(out=BKC[:, 33 * c + 1:33 * c + 33], in_=sa_[:]))
                wait(vector, tf)
    tok_bkc = sV.tok()

    stKT = ExitStack()
    PTRa = stKT.enter_context(nc.psum_tensor("cf_ptra", [128, 128], BF16))
    PTRb = stKT.enter_context(nc.psum_tensor("cf_ptrb", [128, 128], BF16))
    t4, v4 = sT.n, sV.n
    with nc.Block() as blk:
        @blk.tensor
        def _(tensor):
            wait(tensor, tok_k, p.tok_idb)
            for i in range(128):
                c, t = divmod(i, 16)
                if i >= 2:
                    wait(tensor, (sV, v4 + (i - 2) + 1))
                sT.inc(tensor.transpose(
                    out=(PTRa if i % 2 == 0 else PTRb)[:],
                    in_=C16[:, c * S + 128 * t: c * S + 128 * (t + 1)],
                    identity=p.IDB[:]))

        @blk.vector
        def _(vector):
            for i in range(128):
                c, t = divmod(i, 16)
                wait(vector, (sT, t4 + i + 1))
                sV.inc(vector.tensor_copy(
                    out=B16[:, t * 1024 + 128 * c: t * 1024 + 128 * (c + 1)],
                    in_=(PTRa if i % 2 == 0 else PTRb)[:]))
    tok_ktok = sV.tok()
    stKT.close()

    DP16 = LNB[:, 12288:14336]
    stDN = ExitStack()
    PDEN = stDN.enter_context(nc.psum_tensor("cf_pden", [16, S], F32))
    t5, v5, s5, g5 = sT.n, sV.n, sS.n, sG.n
    with nc.Block() as blk:
        @blk.gpsimd
        def _(gpsimd):
            wait(gpsimd, tok_qsm, tok_bkc)
            for c in range(NCH):
                if c >= 1:
                    wait(gpsimd, (sT, t5 + c))
                tg = sG.inc(gpsimd.tensor_tensor(
                    out=DP16.rearrange("p (n s) -> p n s", s=64),
                    in0=A16[:, c * S:(c + 1) * S].rearrange("p (n s) -> p n s", s=64),
                    in1=bass.AP(BKC, 33 * c + 1, [[264, 128], [1, 32], [0, 64]]),
                    op=ALU.mult))

        @blk.tensor
        def _(tensor):
            wait(tensor, tok_hid)
            for c in range(NCH):
                wait(tensor, (sG, g5 + c + 1))
                for sb in range(4):
                    mm = tensor.matmul(out=PDEN[0:16, 512 * sb:512 * (sb + 1)],
                                       lhsT=HID[:, 16 * c:16 * (c + 1)],
                                       rhs=DP16[:, 512 * sb:512 * (sb + 1)],
                                       start=(c == 0), stop=(c == NCH - 1))
                sT.inc(mm)

        @blk.vector
        def _(vector):
            wait(vector, (sT, t5 + NCH))
            sV.inc(vector.tensor_scalar(out=DINV, in0=PDEN[0:16, :],
                                        scalar1=1e-6, scalar2=float(math.exp(-10.0)),
                                        op0=ALU.max, op1=ALU.mult))

        @blk.scalar
        def _(scalar):
            wait(scalar, (sV, v5 + 1))
            tl = sS.inc(scalar.activation(out=DINV, in_=DINV, func=AF.Ln))
            wait(scalar, tl)
            sS.inc(scalar.activation(out=DINV, in_=DINV, func=AF.Exp, scale=-1.0,
                                     bias=p.cpool[0:16, 3:4]))
    v6 = sV.n
    with nc.Block() as blk:
        @blk.vector
        def _(vector):
            wait(vector, (sS, s5 + 2))
            sV.inc(vector.tensor_copy(out=DINVB, in_=DINV))
    tok_dinv = sV.tok()
    stDN.close()

    stCT = ExitStack()
    PDV = stCT.enter_context(nc.psum_tensor("cf_pdv", [128, 512], F32))
    CTX = stCT.enter_context(nc.psum_tensor("cf_ctx", [64, 64], F32))
    ATa = stCT.enter_context(nc.psum_tensor("cf_ata", [64, 64], F32))
    ATb = stCT.enter_context(nc.psum_tensor("cf_atb", [64, 64], F32))
    XO = C16
    for c in range(NCH):
        t7, v7 = sT.n, sV.n
        with nc.Block() as blk:
            @blk.tensor
            def _(tensor):
                wait(tensor, tok_dinv, tok_expd)
                if c == 0:
                    wait(tensor, tok_ktok, tok_v)
                for sb in range(4):
                    if sb >= 1:
                        wait(tensor, (sV, v7 + sb))
                    sT.inc(tensor.matmul(out=PDV[:, 0:512],
                                         lhsT=EXPD[0:16, 128 * c:128 * (c + 1)],
                                         rhs=DINVB[0:16, 512 * sb:512 * (sb + 1)],
                                         start=True, stop=True))

            @blk.vector
            def _(vector):
                for sb in range(4):
                    wait(vector, (sT, t7 + sb + 1))
                    sV.inc(vector.tensor_copy(out=DVX[:, 512 * sb:512 * (sb + 1)],
                                              in_=PDV[:, 0:512]))
        tok_dvx = sV.tok()
        for hh in range(2):
            h = 2 * c + hh
            rb = 64 * hh
            tH, vH = sT.n, sV.n
            with nc.Block() as blk:
                @blk.tensor
                def _(tensor):
                    wait(tensor, tok_dvx, (sV, vH + 2))

                    def dctx(n):
                        tile, par = divmod(n, 2)
                        return sT.inc(tensor.matmul(
                            out=CTX[:],
                            lhsT=B16[64 * par:64 * par + 64,
                                     tile * 1024 + 64 * h: tile * 1024 + 64 * (h + 1)],
                            rhs=VTOK[64 * par:64 * par + 64,
                                     tile * 1024 + 64 * h: tile * 1024 + 64 * (h + 1)],
                            start=True, stop=True))
                    dctx(0)
                    for n in range(1, 32):
                        wait(tensor, (sV, vH + 3 * n + 1))
                        sT.inc(tensor.matmul(
                            out=(ATa if n % 2 == 0 else ATb)[:],
                            lhsT=CTXB[rb:rb + 64, :],
                            rhs=A16[rb:rb + 64, c * S + 64 * n: c * S + 64 * (n + 1)],
                            start=True, stop=True))
                        if n <= 30:
                            wait(tensor, (sV, vH + 3 * n))
                            dctx(n)

                @blk.vector
                def _(vector):
                    sV.inc(vector.memset(XO[rb:rb + 64, c * S: c * S + 64], 0.0))
                    sV.inc(vector.memset(CTXR[rb:rb + 64, :], 0.0))
                    for n in range(1, 32):
                        wait(vector, (sT, tH + (1 if n == 1 else 2 * (n - 1) + 1)))
                        ta = sV.inc(vector.tensor_add(out=CTXR[rb:rb + 64, :],
                                                      in0=CTX[:],
                                                      in1=CTXR[rb:rb + 64, :]))
                        wait(vector, ta)
                        sV.inc(vector.tensor_copy(out=CTXB[rb:rb + 64, :],
                                                  in_=CTXR[rb:rb + 64, :]))
                        wait(vector, (sT, tH + 2 * n))
                        sV.inc(vector.scalar_tensor_tensor(
                            out=XO[rb:rb + 64, c * S + 64 * n: c * S + 64 * (n + 1)],
                            in0=(ATa if n % 2 == 0 else ATb)[:],
                            scalar=1.0,
                            in1=DVX[rb:rb + 64, 64 * n:64 * (n + 1)],
                            op0=ALU.mult, op1=ALU.mult))
    tok_ctx = sV.tok()
    stCT.close()

    tok_wo = emit_big_load(p, WOT.rearrange("p (c f) -> p c f", c=NCH),
                           wot.rearrange("(c p) f -> p c f", p=128), cast=True,
                           deps=[tok_ctx])
    tok_xr = emit_restore(p, X, xsp, [tok_ctx])
    stRO = ExitStack()
    RO = stRO.enter_context(nc.psum_tensor("cf_ro", [128, 512], F32))
    RO2 = stRO.enter_context(nc.psum_tensor("cf_ro2", [128, 512], F32))
    tR0, vR0 = sT.n, sV.n
    with nc.Block() as blk:
        @blk.tensor
        def _(tensor):
            wait(tensor, tok_wo, tok_xr)
            for i in range(32):
                oc, sb = divmod(i, 4)
                ROx = RO if i % 2 == 0 else RO2
                if i >= 2:
                    wait(tensor, (sV, vR0 + (i - 2) + 1))
                for pc in range(NCH):
                    mm = tensor.matmul(
                        out=ROx[:, 0:512],
                        lhsT=WOT[:, pc * 1024 + 128 * oc: pc * 1024 + 128 * (oc + 1)],
                        rhs=XO[:, pc * S + 512 * sb: pc * S + 512 * (sb + 1)],
                        start=(pc == 0), stop=(pc == NCH - 1))
                sT.inc(mm)

        @blk.vector
        def _(vector):
            for i in range(32):
                oc, sb = divmod(i, 4)
                wait(vector, (sT, tR0 + i + 1))
                dst = X[:, oc * S + 512 * sb: oc * S + 512 * (sb + 1)]
                sV.inc(vector.scalar_tensor_tensor(
                    out=dst, in0=(RO if i % 2 == 0 else RO2)[:, 0:512],
                    scalar=1.0, in1=dst, op0=ALU.mult, op1=ALU.add))
    tok_done = sV.tok()
    stRO.close()
    stPT.close()
    st2.close()
    return tok_done


def build_l1attn_prog():
    if "l1attn" in _prog_cache:
        return _prog_cache["l1attn"]
    p = Prog()
    xin = p.din("xin", [1024, S])
    xout = p.dout("xout", [1024, S])
    p.init_consts()
    _load_phase_consts(p, need=("idb",))
    X = p.sb("X", [128, NCH * S], F32)
    tok_x = emit_big_load(p, X[:].rearrange("p (c s) -> p c s", c=NCH),
                          xin.rearrange("(c p) s -> p c s", p=128))
    tok = emit_l1_attn(p, X, deps=[tok_x])
    tok_st = emit_store_xT(p, X, xout, [tok])
    emit_final_wait(p, [tok_st])
    nc = p.close()
    _prog_cache["l1attn"] = (nc, p)
    return nc, p


def _prep_l1attn(inputs):
    d = {}
    d["l1c_wqt"] = np.ascontiguousarray(inputs["L1_attn_W_qkv"].T)
    d["l1c_wot"] = np.ascontiguousarray(inputs["L1_attn_W_out"].T)
    hid = np.zeros((128, 8, 16), np.float32)
    expd = np.zeros((16, 8, 128), np.float32)
    for c in range(8):
        for pp in range(128):
            hid[pp, c, 2 * c + pp // 64] = 1.0
            expd[2 * c + pp // 64, c, pp] = 1.0
    d["c_hid16"] = hid.reshape(128, 128).astype(BF)
    d["c_expd"] = expd.reshape(16, 1024).astype(BF)
    return d



def spill_flat(p, buf_ap, dram_ap, deps):
    ld = p.sem("uld%d" % p._ldctr); p._ldctr += 1
    with p.nc.Block() as blk:
        @blk.gpsimd
        def _(gpsimd):
            wait(gpsimd, *deps)
            ld.inc(gpsimd.dma_start(out=dram_ap, in_=buf_ap), 16)
    p.load_toks.append(ld.tok())
    return ld.tok()


def restore_flat(p, buf_ap, dram_ap, deps):
    ld = p.sem("uld%d" % p._ldctr); p._ldctr += 1
    with p.nc.Block() as blk:
        @blk.sync
        def _(sync):
            wait(sync, *deps)
            ld.inc(sync.dma_start(out=buf_ap, in_=dram_ap), 16)
    p.load_toks.append(ld.tok())
    return ld.tok()


def emit_moe_conv(p):
    w1 = p.din("l1f_w1", [8, 1024, 4096])
    w2 = p.din("l1f_w2", [8, 4096, 1024])
    w1b = p.dtmp("l1f_w1b", [8, 1024, 4096], BF16)
    w2b = p.dtmp("l1f_w2b", [8, 4096, 1024], BF16)
    cv = p.sem("moeconv")
    with p.nc.Block() as blk:
        @blk.gpsimd
        def _(gpsimd):
            for e in range(8):
                cv.inc(gpsimd.dma_start(out=w1b[e:e + 1], in_=w1[e:e + 1]), 16)
                cv.inc(gpsimd.dma_start(out=w2b[e:e + 1], in_=w2[e:e + 1]), 16)
    tok = cv.tok()
    p.load_toks.append(tok)
    return w1b, w2b, tok


def emit_l1_ffn(p, X, deps, y_out, moe_conv=None):
    nc = p.nc
    sV, sT, sS, sG = p.sem("v"), p.sem("t"), p.sem("s"), p.sem("g")
    st2 = ExitStack()
    sbuf = lambda n, sh, dt: st2.enter_context(nc.sbuf_tensor(n, sh, dt))

    wqkv = p.din("l1f_wqkv", [1024, 3072])
    wo = p.din("l1f_wo", [1024, 1024])
    ln1w = p.din("l1f_ln1w", [128, 8])
    ln1b = p.din("l1f_ln1b", [128, 8])
    ln2w = p.din("l1f_ln2w", [128, 8])
    ln2b = p.din("l1f_ln2b", [128, 8])
    bqkv = p.din("l1f_bqkv", [128, 24])
    bqv = p.din("l1f_bqv", [1, 1024])
    bo = p.din("l1f_bo", [128, 8])
    wg = p.din("l1f_wgate", [128, 64])
    b1c = p.din("l1f_b1c", [128, 256])
    b2r = p.din("l1f_b2r", [1, 8192])
    if moe_conv is None:
        moe_conv = emit_moe_conv(p)
    w1b, w2b, tok_conv = moe_conv
    xsp = p.dtmp("l1f_xsp", [128, NCH * S])
    tsp = p.dtmp("l1f_tsp", [128, NCH * S], BF16)
    hsp = p.dtmp("l1f_hsp", [128, NCH * S], BF16)

    LNB = sbuf("df_lnb", [128, NCH * S], BF16)
    LN1B = sbuf("df_ln1b", [128, NCH * S], BF16)
    A16 = sbuf("df_a16", [128, NCH * S], BF16)
    B16 = sbuf("df_b16", [128, NCH * S], BF16)
    LNP = sbuf("df_lnp", [128, 32], F32)
    BQ = sbuf("df_bq", [128, 32], F32)
    BO = sbuf("df_bo", [128, 8], F32)
    WG16 = sbuf("df_wg", [128, 64], BF16)
    WE = sbuf("df_we", [128, 128], F32)
    GT = sbuf("df_gt", [128, 128], F32)
    GM = sbuf("df_gm", [128, 48], F32)
    GMK = sbuf("df_gmk", [128, 128], F32)
    B1C = sbuf("df_b1c", [128, 256], F32)
    B2RE = sbuf("df_b2re", [1, 1024], BF16)
    W2Ga = sbuf("df_w2a", [128, 2048], BF16)
    W2Gb = sbuf("df_w2b", [128, 2048], BF16)
    TMX = sbuf("df_tmx", [128, 65], F32)

    XB = X[:].bitcast(BF16)
    XF = X[:]
    VTOK = XB
    WQ = XB[:, 16384:24576]
    BQVR = XB[0:1, 24576:25600]
    CTXF = LNB[:]
    LNBF32 = LNB[:].bitcast(F32)
    L1F = LN1B[:].bitcast(F32)
    L1B = LN1B[:]
    EXPT = L1B[:, 0:1024]
    BCS = L1F[:, 512:1024]
    SR = L1F[0:1, 1024:1536]
    RLN = L1F[0:1, 1536:2048]
    ISR = L1F[0:1, 2048:2560]
    NMB16 = L1F[:, 2592:2608]
    MROW = L1F[0:1, 2608:2624]
    WOTMP = L1B[:, 4096:5120]
    T1 = LN1B[:]
    HBS = LNB[:]
    ACC = XF
    HR = LNB[:]
    A16F = A16[:].bitcast(F32)
    B16F = B16[:].bitcast(F32)
    OUTSa = L1F
    OUTSb = LNBF32

    tok_l1w = emit_big_load(p, LNP[:, 0:8], ln1w, deps=deps)
    tok_l1b = emit_big_load(p, LNP[:, 8:16], ln1b, deps=deps)
    tok_l2w = emit_big_load(p, LNP[:, 16:24], ln2w, deps=deps)
    tok_l2b = emit_big_load(p, LNP[:, 24:32], ln2b, deps=deps)
    tok_bq = emit_big_load(p, BQ[:, 0:24], bqkv, deps=deps)
    tok_bqv = None
    tok_bo = emit_big_load(p, BO[:], bo, deps=deps)
    tok_wg = emit_big_load(p, WG16[:], wg, cast=True, deps=deps)
    tok_b1c = emit_big_load(p, B1C[:], b1c, deps=deps)

    ln_scr1 = (A16[:, 0:1024], B16F[0:1, 0:6144], A16F[:, 1024:3072],
               A16F[:, 3072:5120], A16[:, 10240:11264])
    tok_t0 = emit_layernorm(p, X, LNB, "df_ln3", deps=deps, scratch=ln_scr1)
    tok_xspill = spill_flat(p, XF, xsp, [tok_t0])
    tok_bqv = emit_big_load(p, BQVR, bqv, cast=True, deps=[tok_xspill])
    tok_ln1 = emit_layernorm(p, LNB, LN1B, "df_ln1",
                             deps=[tok_t0, tok_l1w, tok_l1b],
                             affine=(LNP[:, 0:8], LNP[:, 8:16]), scratch=ln_scr1)
    tok_tspill = spill_flat(p, LNB[:], tsp, [tok_ln1])

    with nc.Block() as blk:
        @blk.vector
        def _(vector):
            wait(vector, tok_bq)
            sV.inc(vector.tensor_scalar_mul(out=BQ[:, 24:32], in0=BQ[:, 0:8],
                                            scalar1=0.125))
    tok_bqs = sV.tok()

    stPT = ExitStack()
    PT = stPT.enter_context(nc.psum_tensor("df_pt", [128, 512], F32))
    PT2 = stPT.enter_context(nc.psum_tensor("df_pt2", [128, 512], F32))

    def project(col_off, kind, prior_tok):
        tok_w = emit_big_load(p, WQ.rearrange("p (c f) -> p c f", c=NCH),
                              wqkv[:, col_off:col_off + 1024].rearrange("(c p) f -> p c f", p=128),
                              cast=True, deps=[tok_xspill] + ([prior_tok] if prior_tok else []))
        t0_, s0_ = sT.n, sS.n
        with nc.Block() as blk:
            @blk.tensor
            def _(tensor):
                wait(tensor, tok_ln1, tok_w, tok_bqs, tok_bqv, p.consts_ready)
                for i in range(32):
                    PTx = PT if i % 2 == 0 else PT2
                    if i >= 2:
                        wait(tensor, (sS, s0_ + (i - 2) + 1))
                    if kind != "v":
                        oc, sb = divmod(i, 4)
                        for ec in range(NCH):
                            mm = tensor.matmul(
                                out=PTx[:, 0:512],
                                lhsT=WQ[:, ec * 1024 + 128 * oc: ec * 1024 + 128 * (oc + 1)],
                                rhs=LN1B[:, ec * S + 512 * sb: ec * S + 512 * (sb + 1)],
                                start=(ec == 0), stop=(ec == NCH - 1))
                        sT.inc(mm)
                    else:
                        t, half = divmod(i, 2)
                        for ec in range(NCH):
                            tensor.matmul(
                                out=PTx[:, 0:512],
                                lhsT=LN1B[:, ec * S + 128 * t: ec * S + 128 * (t + 1)],
                                rhs=WQ[:, ec * 1024 + 512 * half: ec * 1024 + 512 * (half + 1)],
                                start=(ec == 0), stop=False)
                        mm = tensor.matmul(
                            out=PTx[:, 0:512],
                            lhsT=p.ones_row_bf16[0:1, 0:128],
                            rhs=BQVR[0:1, 512 * half:512 * (half + 1)],
                            start=False, stop=True)
                        sT.inc(mm)

            @blk.scalar
            def _(scalar):
                for i in range(32):
                    wait(scalar, (sT, t0_ + i + 1))
                    src_ = (PT if i % 2 == 0 else PT2)[:, 0:512]
                    if kind == "q":
                        oc, sb = divmod(i, 4)
                        sS.inc(scalar.activation(
                            out=A16[:, oc * S + 512 * sb: oc * S + 512 * (sb + 1)],
                            in_=src_, func=AF.Identity, scale=0.125,
                            bias=BQ[:, 24 + oc:25 + oc]))
                    elif kind == "k":
                        oc, sb = divmod(i, 4)
                        sS.inc(scalar.activation(
                            out=B16[:, oc * S + 512 * sb: oc * S + 512 * (sb + 1)],
                            in_=src_, func=AF.Identity, bias=BQ[:, 8 + oc:9 + oc]))
                    else:
                        t, half = divmod(i, 2)
                        sS.inc(scalar.activation(
                            out=VTOK[:, t * 1024 + 512 * half: t * 1024 + 512 * (half + 1)],
                            in_=src_, func=AF.Identity))
        return sS.tok()

    with nc.named_scope("d_qkv"):
        tok_q = project(0, "q", None)
        tok_k = project(1024, "k", tok_q)
        tok_v = project(2048, "v", tok_k)
    stPT.close()

    stMX = ExitStack()
    _scope_mx = nc.enter_named_scope("d_max", False)
    STMa = stMX.enter_context(nc.psum_tensor("df_stma", [128, 512], F32))
    STMb = stMX.enter_context(nc.psum_tensor("df_stmb", [128, 512], F32))
    PM1 = stMX.enter_context(nc.psum_tensor("df_pm1", [1, 128], F32))
    PNB = stMX.enter_context(nc.psum_tensor("df_pnb", [128, 16], F32))
    for h in range(16):
        c, rb = h // 2, 64 * (h % 2)
        tH, vH = sT.n, sV.n
        with nc.Block() as blk:
            @blk.tensor
            def _(tensor):
                if h == 0:
                    wait(tensor, tok_q, tok_k, p.tok_idf)
                else:
                    wait(tensor, (sV, vH))
                for i in range(64):
                    qb, kk = divmod(i, 16)
                    if i >= 2:
                        wait(tensor, (sV, vH + (i - 2) + 1))
                    sT.inc(tensor.matmul(
                        out=(STMa if i % 2 == 0 else STMb)[:, 0:512],
                        lhsT=B16[rb:rb + 64, c * S + 128 * kk: c * S + 128 * (kk + 1)],
                        rhs=A16[rb:rb + 64, c * S + 512 * qb: c * S + 512 * (qb + 1)],
                        start=True, stop=True))

            @blk.vector
            def _(vector):
                wait(vector, (sT, tH))
                for i in range(64):
                    wait(vector, (sT, tH + i + 1))
                    sV.inc(vector.tensor_reduce(
                        out=TMX[:, i:i + 1],
                        in_=(STMa if i % 2 == 0 else STMb)[:, 0:512],
                        axis=AX.X, op=ALU.max))
        v2_ = sV.n
        with nc.Block() as blk:
            @blk.vector
            def _(vector):
                wait(vector, (sV, v2_))
                sV.inc(vector.tensor_reduce(out=TMX[:, 64:65], in_=TMX[:, 0:64],
                                            axis=AX.X, op=ALU.max))

            @blk.tensor
            def _(tensor):
                wait(tensor, (sV, v2_ + 1))
                sT.inc(tensor.transpose(out=PM1[0:1, :], in_=TMX[:, 64:65],
                                        identity=p.IDF[:]))
        v3_, t3_ = sV.n, sT.n
        with nc.Block() as blk:
            @blk.vector
            def _(vector):
                wait(vector, (sT, t3_))
                sV.inc(vector.tensor_reduce(out=MROW[0:1, h:h + 1], in_=PM1[0:1, :],
                                            axis=AX.X, op=ALU.max, negate=True))

            @blk.tensor
            def _(tensor):
                wait(tensor, (sV, v3_ + 1))
                sT.inc(tensor.matmul(out=PNB[:, h:h + 1], lhsT=p.ones_row_f[:],
                                     rhs=MROW[0:1, h:h + 1], start=True, stop=True))
    tN_ = sT.n
    with nc.Block() as blk:
        @blk.scalar
        def _(scalar):
            wait(scalar, (sT, tN_))
            sS.inc(scalar.activation(out=NMB16, in_=PNB[:], func=AF.Copy))
    tok_nmb = sS.tok()
    stMX.close()
    nc.leave_named_scope("d_max", _scope_mx[0], False)

    stAT = ExitStack()
    _sc = nc.enter_named_scope("d_attnB", False)
    OT = stAT.enter_context(nc.psum_tensor("df_ot", [64, S], F32))
    STa = stAT.enter_context(nc.psum_tensor("df_sta", [128, 512], F32))
    STb = stAT.enter_context(nc.psum_tensor("df_stb", [128, 512], F32))
    SIG = stAT.enter_context(nc.psum_tensor("df_sig", [1, 512], F32))
    BCp = stAT.enter_context(nc.psum_tensor("df_bcp", [128, 512], F32))
    t0a, s0a, v0a = sT.n, sS.n, sV.n

    def t_s(h, qb, k):
        return t0a + (h * 4 + qb) * 33 + (1 if k == 0 else 2 * k)

    def t_sig(h, qb, k):
        return t0a + (h * 4 + qb) * 33 + (2 * k + 3 if k < 15 else 32)

    def t_bc(h, qb):
        return t0a + (h * 4 + qb) * 33 + 33

    def s_exp(h, qb, k):
        return s0a + (h * 4 + qb) * 18 + k + 1

    def s_row(h, qb, j):
        return s0a + (h * 4 + qb) * 18 + 16 + j

    def v_sig(h, qb):
        return v0a + (h * 4 + qb) * 3 + 1

    def v_bcs(h, qb):
        return v0a + (h * 4 + qb) * 3 + 2

    def v_ot(h, qb):
        return v0a + (h * 4 + qb) * 3 + 3

    with nc.Block() as blk:
        @blk.tensor
        def _(tensor):
            wait(tensor, tok_nmb, tok_v, tok_tspill)
            for h in range(16):
                c, rb = h // 2, 64 * (h % 2)
                for qb in range(4):
                    idx = h * 4 + qb
                    if idx > 0:
                        pv = (h, qb - 1) if qb else (h - 1, 3)
                        wait(tensor, (sV, v_sig(*pv)))
                    if qb == 0 and h > 0:
                        wait(tensor, (sV, v_ot(h - 1, 3)))
                    for k in range(17):
                        if k < 16:
                            g_ = idx * 16 + k
                            if g_ >= 2:
                                i2, k2 = divmod(g_ - 2, 16)
                                wait(tensor, (sS, s_exp(i2 // 4, i2 % 4, k2)))
                            sT.inc(tensor.matmul(
                                out=(STa if k % 2 == 0 else STb)[:, 0:512],
                                lhsT=B16[rb:rb + 64, c * S + 128 * k: c * S + 128 * (k + 1)],
                                rhs=A16[rb:rb + 64, c * S + 512 * qb: c * S + 512 * (qb + 1)],
                                start=True, stop=True))
                        if k >= 1:
                            kk = k - 1
                            wait(tensor, (sS, s_exp(h, qb, kk)))
                            tensor.matmul(
                                out=OT[0:64, 512 * qb:512 * (qb + 1)],
                                lhsT=VTOK[:, kk * 1024 + 64 * h: kk * 1024 + 64 * (h + 1)],
                                rhs=EXPT[:, (kk % 2) * 512:(kk % 2) * 512 + 512],
                                start=(kk == 0), stop=(kk == 15))
                            sT.inc(tensor.matmul(
                                out=SIG[0:1, 0:512],
                                lhsT=p.ones_bf[:],
                                rhs=EXPT[:, (kk % 2) * 512:(kk % 2) * 512 + 512],
                                start=(kk == 0), stop=(kk == 15)))
                    wait(tensor, (sS, s_row(h, qb, 2)))
                    if idx > 0:
                        pv = (h, qb - 1) if qb else (h - 1, 3)
                        wait(tensor, (sV, v_bcs(*pv)))
                    sT.inc(tensor.matmul(out=BCp[:, 0:512], lhsT=p.ones_row_f[:],
                                         rhs=ISR[0:1, 0:512], start=True, stop=True))

        @blk.scalar
        def _(scalar):
            for h in range(16):
                for qb in range(4):
                    for k in range(16):
                        wait(scalar, (sT, t_s(h, qb, k)))
                        g_ = (h * 4 + qb) * 16 + k
                        if g_ >= 2:
                            i2, k2 = divmod(g_ - 2, 16)
                            wait(scalar, (sT, t_sig(i2 // 4, i2 % 4, k2)))
                        sS.inc(scalar.activation(
                            out=EXPT[:, (k % 2) * 512:(k % 2) * 512 + 512],
                            in_=(STa if k % 2 == 0 else STb)[:, 0:512],
                            func=AF.Exp, bias=NMB16[:, h:h + 1]))
                    wait(scalar, (sV, v_sig(h, qb)))
                    if h * 4 + qb > 0:
                        pv = (h, qb - 1) if qb else (h - 1, 3)
                        wait(scalar, (sT, t_bc(*pv)))
                    t1 = sS.inc(scalar.activation(out=RLN[0:1, 0:512], in_=SR[0:1, 0:512],
                                                  func=AF.Ln))
                    wait(scalar, t1)
                    sS.inc(scalar.activation(out=ISR[0:1, 0:512], in_=RLN[0:1, 0:512],
                                             func=AF.Exp, scale=-1.0,
                                             bias=p.cpool[0:1, 1:2]))

        @blk.vector
        def _(vector):
            for h in range(16):
                c, rb = h // 2, 64 * (h % 2)
                for qb in range(4):
                    wait(vector, (sT, t_sig(h, qb, 15)))
                    if h * 4 + qb > 0:
                        pv = (h, qb - 1) if qb else (h - 1, 3)
                        wait(vector, (sS, s_row(*pv, 1)))
                    sV.inc(vector.tensor_scalar_mul(out=SR[0:1, 0:512],
                                                    in0=SIG[0:1, 0:512],
                                                    scalar1=float(math.exp(24.0))))
                    wait(vector, (sT, t_bc(h, qb)))
                    tb = sV.inc(vector.tensor_copy(out=BCS[:, 0:512], in_=BCp[:, 0:512]))
                    wait(vector, tb)
                    sV.inc(vector.scalar_tensor_tensor(
                        out=CTXF[rb:rb + 64, c * S + 512 * qb: c * S + 512 * (qb + 1)],
                        in0=OT[0:64, 512 * qb:512 * (qb + 1)],
                        scalar=1.0, in1=BCS[rb:rb + 64, 0:512],
                        op0=ALU.mult, op1=ALU.mult))
    tok_attn = sV.tok()
    stAT.close()
    nc.leave_named_scope("d_attnB", _sc[0], False)

    tok_wo = emit_big_load(p, WQ.rearrange("p (c f) -> p c f", c=NCH),
                           wo.rearrange("(c p) f -> p c f", p=128), cast=True,
                           deps=[tok_attn])
    tok_t0r = restore_flat(p, A16[:], tsp, [tok_attn])
    stWO = ExitStack()
    RO = stWO.enter_context(nc.psum_tensor("df_ro", [128, 512], F32))
    RO2 = stWO.enter_context(nc.psum_tensor("df_ro2", [128, 512], F32))
    tW0, vW0, sW0 = sT.n, sV.n, sS.n
    with nc.Block() as blk:
        @blk.tensor
        def _(tensor):
            wait(tensor, tok_wo, tok_t0r, tok_bo)
            for i in range(32):
                oc, sb = divmod(i, 4)
                ROx = RO if i % 2 == 0 else RO2
                if i >= 2:
                    wait(tensor, (sV, vW0 + (i - 2) + 1))
                for pc in range(NCH):
                    mm = tensor.matmul(
                        out=ROx[:, 0:512],
                        lhsT=WQ[:, pc * 1024 + 128 * oc: pc * 1024 + 128 * (oc + 1)],
                        rhs=CTXF[:, pc * S + 512 * sb: pc * S + 512 * (sb + 1)],
                        start=(pc == 0), stop=(pc == NCH - 1))
                sT.inc(mm)

        @blk.scalar
        def _(scalar):
            for i in range(32):
                oc, sb = divmod(i, 4)
                wait(scalar, (sT, tW0 + i + 1))
                sS.inc(scalar.activation(
                    out=WOTMP[:, (i % 2) * 512:(i % 2) * 512 + 512],
                    in_=(RO if i % 2 == 0 else RO2)[:, 0:512],
                    func=AF.Identity, bias=BO[:, oc:oc + 1]))

        @blk.vector
        def _(vector):
            for i in range(32):
                oc, sb = divmod(i, 4)
                wait(vector, (sS, sW0 + i + 1))
                sV.inc(vector.tensor_add(
                    out=B16[:, oc * S + 512 * sb: oc * S + 512 * (sb + 1)],
                    in0=WOTMP[:, (i % 2) * 512:(i % 2) * 512 + 512],
                    in1=A16[:, oc * S + 512 * sb: oc * S + 512 * (sb + 1)]))
    tok_h = sV.tok()
    stWO.close()

    ln_scr2 = (A16[:, 0:1024], LNBF32[0:1, 0:6144], A16F[:, 1024:3072],
               A16F[:, 3072:5120], A16[:, 10240:11264])
    tok_ln2 = emit_layernorm(p, B16, LN1B, "df_ln2",
                             deps=[tok_h, tok_l2w, tok_l2b],
                             affine=(LNP[:, 16:24], LNP[:, 24:32]), scratch=ln_scr2)
    tok_hspill = spill_flat(p, B16[:], hsp, [tok_ln2])

    stPG = ExitStack()
    PG = stPG.enter_context(nc.psum_tensor("df_pg", [128, 128], F32))
    tG0 = sT.n
    with nc.Block() as blk:
        @blk.tensor
        def _(tensor):
            wait(tensor, tok_ln2, tok_wg)
            for t in range(16):
                for dc in range(NCH):
                    mm = tensor.matmul(out=PG[:, 8 * t:8 * (t + 1)],
                                       lhsT=LN1B[:, dc * S + 128 * t: dc * S + 128 * (t + 1)],
                                       rhs=WG16[:, 8 * dc:8 * (dc + 1)],
                                       start=(dc == 0), stop=(dc == NCH - 1))
            sT.inc(mm)

        @blk.vector
        def _(vector):
            wait(vector, (sT, tG0 + 1))
            r3 = lambda buf: buf.rearrange("p (t e) -> p t e", t=16)
            gm3 = lambda off: bass.AP(GM, off, [[48, 128], [1, 16], [0, 8]])
            t1 = sV.inc(vector.tensor_copy(out=GT[:], in_=PG[:]))
            wait(vector, t1)
            t2 = sV.inc(vector.tensor_reduce(out=GM[:, 0:16], in_=r3(GT[:]),
                                             axis=AX.X, op=ALU.max))
            wait(vector, t2)
            t3 = sV.inc(vector.tensor_tensor(out=r3(GMK[:]), in0=r3(GT[:]),
                                             in1=gm3(0), op=ALU.is_ge))
            wait(vector, t3)
            t4 = sV.inc(vector.tensor_scalar_mul(out=WE[:], in0=GMK[:], scalar1=1e30))
            wait(vector, t4)
            t5 = sV.inc(vector.tensor_sub(out=WE[:], in0=GT[:], in1=WE[:]))
            wait(vector, t5)
            t6 = sV.inc(vector.tensor_reduce(out=GM[:, 16:32], in_=r3(WE[:]),
                                             axis=AX.X, op=ALU.max))
            wait(vector, t6)
            t7 = sV.inc(vector.tensor_tensor(out=r3(WE[:]), in0=r3(WE[:]),
                                             in1=gm3(16), op=ALU.is_ge))
            wait(vector, t7)
            t9 = sV.inc(vector.tensor_add(out=GMK[:], in0=GMK[:], in1=WE[:]))
            wait(vector, t9)
            t10 = sV.inc(vector.tensor_scalar_mul(out=GM[:, 32:48], in0=GM[:, 0:16],
                                                  scalar1=-1.0))
            wait(vector, t10)
            sV.inc(vector.tensor_tensor(out=r3(GT[:]), in0=r3(GT[:]),
                                        in1=gm3(32), op=ALU.add))
        vG1 = sV.n

        @blk.scalar
        def _(scalar):
            wait(scalar, (sV, vG1))
            sS.inc(scalar.activation(out=GT[:], in_=GT[:], func=AF.Exp))
        sG1 = sS.n
    with nc.Block() as blk:
        @blk.vector
        def _(vector):
            wait(vector, (sS, sG1))
            r3 = lambda buf: buf.rearrange("p (t e) -> p t e", t=16)
            gm3 = lambda off: bass.AP(GM, off, [[48, 128], [1, 16], [0, 8]])
            t1 = sV.inc(vector.tensor_mul(out=WE[:], in0=GT[:], in1=GMK[:]))
            wait(vector, t1)
            t2 = sV.inc(vector.tensor_reduce(out=GM[:, 0:16], in_=r3(WE[:]),
                                             axis=AX.X, op=ALU.add))
            wait(vector, t2)
            t3 = sV.inc(vector.reciprocal(out=GM[:, 0:16], in_=GM[:, 0:16]))
            wait(vector, t3)
            sV.inc(vector.tensor_tensor(out=r3(WE[:]), in0=r3(WE[:]), in1=gm3(0),
                                        op=ALU.mult))
    tok_we = sV.tok()
    stPG.close()

    def w1slice(dc, hc):
        buf = A16 if dc < 4 else B16
        return buf[:, (dc % 4) * 4096 + 128 * hc: (dc % 4) * 4096 + 128 * (hc + 1)]

    _sc2 = nc.enter_named_scope("d_moe", False)
    W1Rtoks = [None] * 8
    W1Rtoks[0] = [
        emit_big_load(p, A16[:].rearrange("p (c f) -> p c f", c=4),
                      w1b[0:1, 0:512, :].rearrange("o (c p) f -> p (o c) f", p=128),
                      deps=[tok_hspill, tok_we, tok_conv]),
        emit_big_load(p, B16[:].rearrange("p (c f) -> p c f", c=4),
                      w1b[0:1, 512:1024, :].rearrange("o (c p) f -> p (o c) f", p=128),
                      deps=[tok_hspill, tok_we, tok_conv])]
    acc_tok = {}
    prev_out_tok = None
    prev_g_toks = None
    prev_flush_tok = None
    tok_b2e = None
    for e in range(8):
        b2dep = [tok_hspill] if e == 0 else [prev_out_tok]
        tok_b2e = emit_big_load(p, B2RE[:], b2r[0:1, 1024 * e:1024 * (e + 1)],
                                cast=True, deps=b2dep)
        for sg in range(4):
            stPH = ExitStack()
            PHa = stPH.enter_context(nc.psum_tensor("df_pha_%d_%d" % (e, sg), [128, 512], F32))
            PHb = stPH.enter_context(nc.psum_tensor("df_phb_%d_%d" % (e, sg), [128, 512], F32))
            tP0, sP0 = sT.n, sS.n
            with nc.Block() as blk:
                @blk.tensor
                def _(tensor):
                    wait(tensor, *W1Rtoks[e])
                    for hc in range(32):
                        PHx = PHa if hc % 2 == 0 else PHb
                        if hc >= 2:
                            wait(tensor, (sS, sP0 + (hc - 2) + 1))
                        for dc in range(NCH):
                            mm = tensor.matmul(
                                out=PHx[:, 0:512],
                                lhsT=w1slice(dc, hc),
                                rhs=T1[:, dc * S + 512 * sg: dc * S + 512 * (sg + 1)],
                                start=(dc == 0), stop=(dc == NCH - 1))
                        sT.inc(mm)

                @blk.scalar
                def _(scalar):
                    wait(scalar, tok_b1c)
                    for hc in range(32):
                        wait(scalar, (sT, tP0 + hc + 1))
                        if prev_g_toks is not None:
                            wait(scalar, prev_g_toks[hc // 2])
                        sS.inc(scalar.activation(
                            out=HBS[:, 512 * hc:512 * (hc + 1)],
                            in_=(PHa if hc % 2 == 0 else PHb)[:, 0:512],
                            func=AF.Gelu, bias=B1C[:, 32 * e + hc:32 * e + hc + 1]))
            tok_gelu = sS.tok()
            stPH.close()

            if sg == 3 and e < 7:
                W1Rtoks[e + 1] = [
                    emit_big_load(p, A16[:].rearrange("p (c f) -> p c f", c=4),
                                  w1b[e + 1:e + 2, 0:512, :].rearrange("o (c p) f -> p (o c) f", p=128),
                                  deps=[tok_gelu]),
                    emit_big_load(p, B16[:].rearrange("p (c f) -> p c f", c=4),
                                  w1b[e + 1:e + 2, 512:1024, :].rearrange("o (c p) f -> p (o c) f", p=128),
                                  deps=[tok_gelu])]

            stPO = ExitStack()
            PO = [stPO.enter_context(nc.psum_tensor("df_po_%d_%d_%d" % (e, sg, j),
                                                    [128, 512], F32))
                  for j in range(8)]
            tO0 = sT.n
            sGel0 = tok_gelu[1] - 32
            g_toks = []
            for g in range(16):
                dstw = W2Ga if g % 2 == 0 else W2Gb
                if g >= 2:
                    deps_w2 = [(sT, tO0 + 16 * (g - 1))]
                elif prev_out_tok is not None:
                    deps_w2 = [prev_out_tok, tok_conv]
                else:
                    deps_w2 = [tok_conv]
                tok_w2 = emit_big_load(p, dstw[:].rearrange("p (c f) -> p c f", c=2),
                                       w2b[e:e + 1, 256 * g:256 * (g + 1), :]
                                       .rearrange("o (c p) f -> p (o c) f", p=128),
                                       deps=deps_w2,
                                       sem=p.sem("w2s_a" if g % 2 == 0 else "w2s_b"))
                with nc.Block() as blk:
                    @blk.tensor
                    def _(tensor):
                        wait(tensor, tok_w2)
                        wait(tensor, (tok_gelu[0], sGel0 + 2 * g + 2))
                        if g == 0:
                            wait(tensor, tok_b2e)
                            if prev_flush_tok is not None:
                                wait(tensor, prev_flush_tok)
                        for hc in range(2 * g, 2 * g + 2):
                            for j in range(8):
                                sblk, oh = divmod(j, 2)
                                sT.inc(tensor.matmul(
                                    out=PO[j][:, 0:512],
                                    lhsT=HBS[:, 512 * hc + 128 * sblk:
                                             512 * hc + 128 * (sblk + 1)],
                                    rhs=dstw[:, (hc - 2 * g) * 1024 + 512 * oh:
                                             (hc - 2 * g) * 1024 + 512 * (oh + 1)],
                                    start=(hc == 0), stop=False))
                g_toks.append((p.sems["t"], tO0 + 16 * (g + 1)))
            tB0, vB0 = sT.n, sV.n
            with nc.Block() as blk:
                @blk.tensor
                def _(tensor):
                    for j in range(8):
                        sblk, oh = divmod(j, 2)
                        sT.inc(tensor.matmul(
                            out=PO[j][:, 0:512],
                            lhsT=p.ones_row_bf16[0:1, 0:128],
                            rhs=B2RE[0:1, 512 * oh: 512 * (oh + 1)],
                            start=False, stop=True))

                @blk.vector
                def _(vector):
                    if e == 0 and sg == 0:
                        wait(vector, tok_we)
                    for j in range(8):
                        sblk, oh = divmod(j, 2)
                        t = 4 * sg + sblk
                        wait(vector, (sT, tB0 + j + 1))
                        dst = ACC[:, 1024 * t + 512 * oh: 1024 * t + 512 * (oh + 1)]
                        wecol = WE[:, 8 * t + e:8 * t + e + 1]
                        if e == 0:
                            sV.inc(vector.tensor_scalar_mul(out=dst, in0=PO[j][:, 0:512],
                                                            scalar1=wecol))
                        else:
                            wait(vector, acc_tok[(t, oh)])
                            sV.inc(vector.scalar_tensor_tensor(
                                out=dst, in0=PO[j][:, 0:512], scalar=wecol,
                                in1=dst, op0=ALU.mult, op1=ALU.add))
                        acc_tok[(t, oh)] = sV.tok()
            prev_out_tok = sT.tok()
            prev_g_toks = g_toks
            prev_flush_tok = sV.tok()
            stPO.close()
    tok_moe = prev_flush_tok
    nc.leave_named_scope("d_moe", _sc2[0], False)

    tok_xr1 = restore_flat(p, A16F, xsp[:, 0:8192], [tok_moe])
    tok_xr2 = restore_flat(p, B16F, xsp[:, 8192:16384], [tok_moe])
    tok_hr = restore_flat(p, HR, hsp, [tok_moe])
    stFN = ExitStack()
    PFa = stFN.enter_context(nc.psum_tensor("df_pfa", [128, 128], F32))
    PFb = stFN.enter_context(nc.psum_tensor("df_pfb", [128, 128], F32))
    vF0 = sV.n
    with nc.Block() as blk:
        @blk.vector
        def _(vector):
            wait(vector, tok_xr1, tok_xr2, tok_hr)
            for c in range(NCH):
                xr = A16F if c < 4 else B16F
                sl = xr[:, (c % 4) * 2048:(c % 4) * 2048 + 2048]
                sV.inc(vector.tensor_add(out=sl, in0=sl,
                                         in1=HR[:, c * S:(c + 1) * S]))
    vF1 = sV.n
    tF0 = sT.n
    with nc.Block() as blk:
        @blk.tensor
        def _(tensor):
            wait(tensor, (sV, vF1), p.tok_idf)
            for i in range(128):
                t, c = divmod(i, 8)
                xr = A16F if c < 4 else B16F
                if i >= 2:
                    wait(tensor, (sV, vF1 + (i - 2) + 1))
                sT.inc(tensor.transpose(
                    out=(PFa if i % 2 == 0 else PFb)[:],
                    in_=xr[:, (c % 4) * 2048 + 128 * t: (c % 4) * 2048 + 128 * (t + 1)],
                    identity=p.IDF[:]))

        @blk.vector
        def _(vector):
            for i in range(128):
                t, c = divmod(i, 8)
                OUTS = OUTSa if t < 8 else OUTSb
                wait(vector, (sT, tF0 + i + 1))
                sV.inc(vector.scalar_tensor_tensor(
                    out=OUTS[:, 1024 * (t % 8) + 128 * c: 1024 * (t % 8) + 128 * (c + 1)],
                    in0=(PFa if i % 2 == 0 else PFb)[:],
                    scalar=1.0,
                    in1=ACC[:, 1024 * t + 128 * c: 1024 * t + 128 * (c + 1)],
                    op0=ALU.mult, op1=ALU.add))
    vF2 = sV.n
    stFN.close()
    ldy1 = p.sem("uld%d" % p._ldctr); p._ldctr += 1
    ldy2 = p.sem("uld%d" % p._ldctr); p._ldctr += 1
    with nc.Block() as blk:
        @blk.gpsimd
        def _(gpsimd):
            wait(gpsimd, (sV, vF1 + 64))
            ldy1.inc(gpsimd.dma_start(
                out=y_out[0:1024, :].rearrange("(t p) f -> p t f", p=128),
                in_=OUTSa[:, 0:8192].rearrange("p (t f) -> p t f", t=8)), 16)
            wait(gpsimd, (sV, vF2))
            ldy2.inc(gpsimd.dma_start(
                out=y_out[1024:2048, :].rearrange("(t p) f -> p t f", p=128),
                in_=OUTSb[:].rearrange("p (t f) -> p t f", t=8)), 16)
    p.load_toks.append(ldy1.tok())
    p.load_toks.append(ldy2.tok())
    st2.close()
    return [ldy1.tok(), ldy2.tok()]


def build_l1ffn_prog():
    if "l1ffn" in _prog_cache:
        return _prog_cache["l1ffn"]
    p = Prog()
    xin = p.din("xin", [1024, S])
    y = p.dout("y", [S, 1024])
    p.init_consts()
    _load_phase_consts(p, need=("idf",))
    X = p.sb("X", [128, NCH * S], F32)
    tok_x = emit_big_load(p, X[:].rearrange("p (c s) -> p c s", c=NCH),
                          xin.rearrange("(c p) s -> p c s", p=128))
    toks = emit_l1_ffn(p, X, deps=[tok_x], y_out=y)
    emit_final_wait(p, toks)
    nc = p.close()
    _prog_cache["l1ffn"] = (nc, p)
    return nc, p


def _prep_l1ffn(inputs):
    d = {}
    cols = lambda v, k: np.ascontiguousarray(v.reshape(k, 128).T)
    d["l1f_wqkv"] = inputs["L1_ffn_w_qkv"]
    d["l1f_wo"] = inputs["L1_ffn_w_o"]
    d["l1f_ln1w"] = cols(inputs["L1_ffn_ln1_w"], 8)
    d["l1f_ln1b"] = cols(inputs["L1_ffn_ln1_b"], 8)
    d["l1f_ln2w"] = cols(inputs["L1_ffn_ln2_w"], 8)
    d["l1f_ln2b"] = cols(inputs["L1_ffn_ln2_b"], 8)
    d["l1f_bqkv"] = cols(inputs["L1_ffn_b_qkv"], 24)
    d["l1f_bqv"] = np.ascontiguousarray(inputs["L1_ffn_b_qkv"][2048:3072].reshape(1, 1024))
    d["l1f_bo"] = cols(inputs["L1_ffn_b_o"], 8)
    d["l1f_wgate"] = np.ascontiguousarray(
        inputs["L1_ffn_w_gate"].reshape(8, 128, 8).transpose(1, 0, 2).reshape(128, 64))
    d["l1f_w1"] = inputs["L1_ffn_w1"]
    d["l1f_b1c"] = np.ascontiguousarray(
        inputs["L1_ffn_b1"].reshape(8, 32, 128).transpose(2, 0, 1).reshape(128, 256))
    d["l1f_w2"] = inputs["L1_ffn_w2"]
    d["l1f_b2r"] = np.ascontiguousarray(inputs["L1_ffn_b2"].reshape(1, 8192))
    return d


def _prep_common(inputs):
    c = {}
    c["c_idf"] = np.eye(128, dtype=np.float32)
    c["c_idb"] = np.eye(128).astype(BF)
    e415 = np.zeros((4, 15), dtype=np.float32)
    off = 0
    for e, r in enumerate([1, 2, 4, 8]):
        e415[e, off:off + r] = 1.0
        off += r
    c["c_e415"] = e415.astype(BF)
    return c


def _prep_l0ffn(inputs):
    d = {}
    d["l0f_wt"] = np.ascontiguousarray(inputs["L0_ffn_W"].T)
    d["l0f_wpt"] = np.ascontiguousarray(inputs["L0_ffn_Wp"].T)
    d["l0f_bp"] = np.ascontiguousarray(inputs["L0_ffn_bp"].reshape(2, 128).T)
    d["l0f_sim"] = np.ascontiguousarray(
        inputs["L0_ffn_sim"].reshape(2, 128, 4).transpose(1, 0, 2).reshape(128, 8))
    d["l0f_temp"] = inputs["L0_ffn_temp"].reshape(1, 1)
    wd = np.concatenate([inputs["L0_ffn_Wd0"], inputs["L0_ffn_Wd1"],
                         inputs["L0_ffn_Wd2"], inputs["L0_ffn_Wd3"]], axis=0)
    d["l0f_wdt"] = np.ascontiguousarray(
        wd.T.reshape(8, 128, 15).transpose(1, 0, 2).reshape(128, 120))
    wu = np.concatenate([inputs["L0_ffn_Wu0"], inputs["L0_ffn_Wu1"],
                         inputs["L0_ffn_Wu2"], inputs["L0_ffn_Wu3"]], axis=1)
    d["l0f_wut"] = np.ascontiguousarray(wu.T)
    return d


def _load_phase_consts(p, need=("idf", "idb", "e415")):
    if "idf" in need:
        p.IDF = p.sb("c_idf_sb", [128, 128], F32)
        p.tok_idf = emit_big_load(p, p.IDF[:], p.din("c_idf", [128, 128]))
    if "idb" in need:
        p.IDB = p.sb("c_idb_sb", [128, 128], BF16)
        p.tok_idb = emit_big_load(p, p.IDB[:], p.din("c_idb", [128, 128], BF16))
    if "e415" in need:
        p.E415 = p.sb("c_e415_sb", [4, 15], BF16)
        p.tok_e415 = emit_big_load(p, p.E415[:], p.din("c_e415", [4, 15], BF16))


_prog_cache = {}


def build_l0ffn_prog():
    if "l0ffn" in _prog_cache:
        return _prog_cache["l0ffn"]
    p = Prog()
    xin = p.din("xin", [1024, S])
    xout = p.dout("xout", [1024, S])
    p.init_consts()
    _load_phase_consts(p)
    X = p.sb("X", [128, NCH * S], F32)
    tok_x = emit_big_load(p, X[:].rearrange("p (c s) -> p c s", c=NCH),
                          xin.rearrange("(c p) s -> p c s", p=128))
    tok = emit_l0_ffn(p, X, deps=[tok_x])
    tok_st = emit_store_xT(p, X, xout, [tok])
    emit_final_wait(p, [tok_st])
    nc = p.close()
    _prog_cache["l0ffn"] = (nc, p)
    return nc, p


def build_fused_prog():
    if "fused" in _prog_cache:
        return _prog_cache["fused"]
    p = Prog()
    xin = p.din("xin", [1024, S])
    y = p.dout("y", [S, 1024])
    p.init_consts()
    _load_phase_consts(p, need=("idf", "idb", "e415"))
    X = p.sb("X", [128, NCH * S], F32)
    tok_x = emit_big_load(p, X[:].rearrange("p (c s) -> p c s", c=NCH),
                          xin.rearrange("(c p) s -> p c s", p=128))
    with p.nc.named_scope("l0_attn"):
        tok = emit_l0_attn(p, X, deps=[tok_x])
    moe_conv = emit_moe_conv(p)
    with p.nc.named_scope("l0_ffn"):
        tok = emit_l0_ffn(p, X, deps=[tok])
    with p.nc.named_scope("l1_attn"):
        tok = emit_l1_attn(p, X, deps=[tok])
    with p.nc.named_scope("l1_ffn"):
        toks = emit_l1_ffn(p, X, deps=[tok], y_out=y, moe_conv=moe_conv)
    emit_final_wait(p, toks)
    nc = p.close()
    _prog_cache["fused"] = (nc, p)
    return nc, p


def kernel(**inputs):
    emb = inputs["L0_attn_emb"]
    ncore = 8
    prep = _prep_common(inputs)
    prep.update(_prep_l0attn(inputs))
    prep.update(_prep_l0ffn(inputs))
    prep.update(_prep_l1attn(inputs))
    prep.update(_prep_l1ffn(inputs))

    nc, p = build_fused_prog()
    in_maps = []
    for i in range(ncore):
        m = {"xin": np.ascontiguousarray(emb[i].T)}
        m.update(prep)
        in_maps.append(m)
    res = run_bass_kernel_spmd(nc, in_maps, core_ids=list(range(ncore)))
    return np.stack([r["y"] for r in res.results], axis=0)


def kernel_unfused(**inputs):
    emb = inputs["L0_attn_emb"]
    ncore = 8
    prep = _prep_common(inputs)
    prep.update(_prep_l0attn(inputs))
    prep.update(_prep_l0ffn(inputs))
    prep.update(_prep_l1attn(inputs))
    prep.update(_prep_l1ffn(inputs))
    x = [np.ascontiguousarray(emb[i].T) for i in range(ncore)]
    for name, builder in [("l0attn", build_l0attn_prog), ("l0ffn", build_l0ffn_prog),
                          ("l1attn", build_l1attn_prog), ("l1ffn", build_l1ffn_prog)]:
        nc, p = builder()
        in_maps = []
        for i in range(ncore):
            m = {"xin": x[i]}
            m.update(prep)
            in_maps.append(m)
        res = run_bass_kernel_spmd(nc, in_maps, core_ids=list(range(ncore)))
        if name == "l1ffn":
            return np.stack([r["y"] for r in res.results], axis=0)
        x = [np.ascontiguousarray(r["xout"]) for r in res.results]


# version 07
# speedup vs baseline: 1.0413x; 1.0413x over previous
import math
import numpy as np
import ml_dtypes
from contextlib import ExitStack

import concourse.bass as bass
import concourse.mybir as mybir
from concourse.bass_utils import run_bass_kernel_spmd

F32 = mybir.dt.float32
BF16 = mybir.dt.bfloat16
AF = mybir.ActivationFunctionType
ALU = mybir.AluOpType
AX = mybir.AxisListType
BF = ml_dtypes.bfloat16

S = 2048
DM = 1024
NT = 16
NCH = 8
EPS = 1e-5

class Sem:
    def __init__(self, handle, name):
        self.h = handle
        self.name = name
        self.n = 0

    def inc(self, ins, k=1):
        ins.then_inc(self.h, k)
        self.n += k
        return (self, self.n)

    def tok(self):
        return (self, self.n)


def wait(eng, *toks):
    for t in toks:
        if t is None:
            continue
        sem, val = t
        if val > 0:
            eng.wait_ge(sem.h, val)


class Prog:
    def __init__(self):
        self.nc = bass.Bass("TRN2", target_bir_lowering=False)
        self.st = ExitStack()
        self.sems = {}
        self.consts_ready = None

    def sem(self, name):
        if name not in self.sems:
            h = self.st.enter_context(self.nc.semaphore(name))
            self.sems[name] = Sem(h, name)
        return self.sems[name]

    def sb(self, name, shape, dt):
        return self.st.enter_context(self.nc.sbuf_tensor(name, shape, dt))

    def psum(self, name, shape, dt=F32):
        return self.st.enter_context(self.nc.psum_tensor(name, shape, dt))

    def din(self, name, shape, dt=F32):
        return self.nc.dram_tensor(name, shape, dt, kind="ExternalInput").ap()

    def dout(self, name, shape, dt=F32):
        return self.nc.dram_tensor(name, shape, dt, kind="ExternalOutput").ap()

    def dtmp(self, name, shape, dt=F32):
        return self.nc.dram_tensor(name, shape, dt).ap()

    def close(self):
        self.st.close()
        return self.nc

    def init_consts(self):
        nc = self.nc
        self.ones_bf = self.sb("ones_bf", [128, 1], BF16)
        self.ones_f32col = self.sb("ones_f32col", [128, 1], F32)
        self.ones_row_f = self.sb("ones_row_f", [1, 128], F32)
        self.ones_row_bf16 = self.sb("ones_row_bf16", [1, 128], BF16)
        self.cpool = self.sb("cpool", [128, 4], F32)
        g = self.sem("g")
        with nc.Block() as blk:
            @blk.gpsimd
            def _(gpsimd):
                gpsimd.memset(self.ones_bf[:], 1.0)
                gpsimd.memset(self.ones_f32col[:], 1.0)
                gpsimd.memset(self.ones_row_f[:], 1.0)
                gpsimd.memset(self.ones_row_bf16[:], 1.0)
                gpsimd.memset(self.cpool[:, 0:1], 1e-5)
                gpsimd.memset(self.cpool[:, 1:2], 24.0)
                gpsimd.memset(self.cpool[:, 2:3], float(math.log(0.125)))
                t = g.inc(gpsimd.memset(self.cpool[:, 3:4], -10.0))
        self.consts_ready = g.tok()


def chunk(buf, c, width=S):
    return buf[:, c * width:(c + 1) * width]


def emit_big_load(p, dst_sb, src_ap, cast=False, deps=(), sem=None):
    if not hasattr(p, "_ldctr"):
        p._ldctr = 0
        p.load_toks = []
    if sem is not None:
        ld = sem
    else:
        ld = p.sem("uld%d" % p._ldctr)
        p._ldctr += 1
    with p.nc.Block() as blk:
        if cast:
            @blk.gpsimd
            def _(gpsimd):
                wait(gpsimd, *deps)
                ld.inc(gpsimd.dma_start(out=dst_sb, in_=src_ap), 16)
        else:
            @blk.sync
            def _(sync):
                wait(sync, *deps)
                ld.inc(sync.dma_start(out=dst_sb, in_=src_ap), 16)
    p.load_toks.append(ld.tok())
    return ld.tok()


def emit_layernorm(p, SRC, LNB, tag, deps=(), affine=None, scratch=None):
    nc = p.nc
    sV, sT, sS, sG = p.sem("v"), p.sem("t"), p.sem("s"), p.sem("g")
    st2 = ExitStack()
    if scratch is None:
        SQ = st2.enter_context(nc.sbuf_tensor(tag + "_sq", [128, 2 * 512], BF16))[:]
        ROWS = st2.enter_context(nc.sbuf_tensor(tag + "_rows", [1, 3 * S], F32))[:]
        BCA = st2.enter_context(nc.sbuf_tensor(tag + "_bca", [128, S], F32))[:]
        BCB = st2.enter_context(nc.sbuf_tensor(tag + "_bcb", [128, S], F32))[:]
        TMP = st2.enter_context(nc.sbuf_tensor(tag + "_tmp", [128, 2 * 512], BF16))[:]
    else:
        SQ, ROWS, BCA, BCB, TMP = scratch
    stStat = ExitStack()
    PS = stStat.enter_context(nc.psum_tensor(tag + "_ps", [1, S], F32))
    PQ = stStat.enter_context(nc.psum_tensor(tag + "_pq", [1, S], F32))

    v0, t0, s0 = sV.n, sT.n, sS.n
    with nc.Block() as blk:
        @blk.vector
        def _(vector):
            wait(vector, *deps)
            for c in range(NCH):
                sV.inc(vector.tensor_copy(out=chunk(LNB, c), in_=chunk(SRC, c)))

        @blk.scalar
        def _(scalar):
            for i in range(32):
                c, sb = divmod(i, 4)
                wait(scalar, (sV, v0 + c + 1))
                if i >= 2:
                    wait(scalar, (sT, t0 + (i - 2) + 1))
                sS.inc(scalar.activation(
                    out=SQ[:, (i % 2) * 512:(i % 2) * 512 + 512],
                    in_=LNB[:, c * S + 512 * sb: c * S + 512 * (sb + 1)],
                    func=AF.Square))

        @blk.tensor
        def _(tensor):
            wait(tensor, p.consts_ready)
            for i in range(32):
                c, sb = divmod(i, 4)
                wait(tensor, (sV, v0 + c + 1), (sS, s0 + i + 1))
                tensor.matmul(out=PS[0:1, 512 * sb:512 * (sb + 1)],
                              lhsT=p.ones_bf[:],
                              rhs=LNB[:, c * S + 512 * sb: c * S + 512 * (sb + 1)],
                              start=(c == 0), stop=(c == NCH - 1))
                mm = tensor.matmul(out=PQ[0:1, 512 * sb:512 * (sb + 1)],
                                   lhsT=p.ones_bf[:],
                                   rhs=SQ[:, (i % 2) * 512:(i % 2) * 512 + 512],
                                   start=(c == 0), stop=(c == NCH - 1))
                sT.inc(mm)

    with nc.Block() as blk:
        @blk.vector
        def _(vector):
            wait(vector, (sT, t0 + 32))
            vector.tensor_scalar_mul(out=ROWS[0:1, 0:S], in0=PS[0:1, :], scalar1=1.0 / DM)
            sV.inc(vector.tensor_scalar_mul(out=ROWS[0:1, 2 * S:3 * S], in0=PQ[0:1, :],
                                            scalar1=1.0 / DM))
        vA = sV.n

        @blk.scalar
        def _(scalar):
            wait(scalar, (sV, vA))
            sS.inc(scalar.activation(out=ROWS[0:1, S:2 * S], in_=ROWS[0:1, 0:S],
                                     func=AF.Square))
        sA = sS.n

        @blk.gpsimd
        def _(gpsimd):
            wait(gpsimd, (sS, sA))
            sG.inc(gpsimd.tensor_sub(out=ROWS[0:1, 2 * S:3 * S],
                                     in0=ROWS[0:1, 2 * S:3 * S],
                                     in1=ROWS[0:1, S:2 * S]))
        gA = sG.n

    with nc.Block() as blk:
        @blk.scalar
        def _(scalar):
            wait(scalar, (sG, gA))
            sS.inc(scalar.activation(out=ROWS[0:1, S:2 * S], in_=ROWS[0:1, 2 * S:3 * S],
                                     func=AF.Sqrt, bias=p.cpool[0:1, 0:1]))
        sB = sS.n

        @blk.vector
        def _(vector):
            wait(vector, (sS, sB))
            trec = sV.inc(vector.reciprocal(out=ROWS[0:1, S:2 * S], in_=ROWS[0:1, S:2 * S]))
            wait(vector, trec)
            sV.inc(vector.tensor_mul(out=ROWS[0:1, 2 * S:3 * S], in0=ROWS[0:1, 0:S],
                                     in1=ROWS[0:1, S:2 * S]))
        vB = sV.n

    stStat.close()
    PB = st2.enter_context(nc.psum_tensor(tag + "_pb", [128, 512], F32))
    PC = st2.enter_context(nc.psum_tensor(tag + "_pc", [128, 512], F32))
    tB = sT.n
    vC = sV.n
    with nc.Block() as blk:
        @blk.tensor
        def _(tensor):
            wait(tensor, (sV, vB))
            for sb in range(4):
                if sb >= 1:
                    wait(tensor, (sV, vC + 2 * (sb - 1) + 2))
                sT.inc(tensor.matmul(out=PB[:, 0:512], lhsT=p.ones_row_f[:],
                                     rhs=ROWS[0:1, S + 512 * sb:S + 512 * (sb + 1)],
                                     start=True, stop=True))
                sT.inc(tensor.matmul(out=PC[:, 0:512], lhsT=p.ones_row_f[:],
                                     rhs=ROWS[0:1, 2 * S + 512 * sb:2 * S + 512 * (sb + 1)],
                                     start=True, stop=True))

        @blk.vector
        def _(vector):
            for sb in range(4):
                wait(vector, (sT, tB + 2 * sb + 1))
                sV.inc(vector.tensor_copy(out=BCA[:, 512 * sb:512 * (sb + 1)], in_=PB[:, 0:512]))
                wait(vector, (sT, tB + 2 * sb + 2))
                sV.inc(vector.tensor_copy(out=BCB[:, 512 * sb:512 * (sb + 1)], in_=PC[:, 0:512]))

    vD = sV.n
    gD = sG.n
    with nc.Block() as blk:
        @blk.vector
        def _(vector):
            wait(vector, (sV, vD))
            for i in range(32):
                c, sb = divmod(i, 4)
                if i >= 2:
                    wait(vector, (sG, gD + (i - 2) + 1))
                sV.inc(vector.tensor_mul(
                    out=TMP[:, (i % 2) * 512:(i % 2) * 512 + 512],
                    in0=LNB[:, c * S + 512 * sb: c * S + 512 * (sb + 1)],
                    in1=BCA[:, 512 * sb:512 * (sb + 1)]))

        @blk.gpsimd
        def _(gpsimd):
            for i in range(32):
                c, sb = divmod(i, 4)
                wait(gpsimd, (sV, vD + i + 1))
                sG.inc(gpsimd.tensor_sub(
                    out=LNB[:, c * S + 512 * sb: c * S + 512 * (sb + 1)],
                    in0=TMP[:, (i % 2) * 512:(i % 2) * 512 + 512],
                    in1=BCB[:, 512 * sb:512 * (sb + 1)]))

    done = sG.tok()
    if affine is not None:
        Wc, Bc = affine
        gE = sG.n - 32
        with nc.Block() as blk:
            @blk.vector
            def _(vector):
                for c in range(NCH):
                    wait(vector, (sG, gE + 4 * (c + 1)))
                    sV.inc(vector.tensor_scalar(out=chunk(LNB, c), in0=chunk(LNB, c),
                                                scalar1=Wc[:, c:c + 1], scalar2=Bc[:, c:c + 1],
                                                op0=ALU.mult, op1=ALU.add))
        done = sV.tok()
    st2.close()
    return done


def emit_store_xT(p, X, dst_ap, deps):
    if not hasattr(p, "_ldctr"):
        p._ldctr = 0
    gd = p.sem("uld%d" % p._ldctr)
    p._ldctr += 1
    with p.nc.Block() as blk:
        @blk.gpsimd
        def _(gpsimd):
            wait(gpsimd, *deps)
            t = gd.inc(gpsimd.dma_start(
                out=dst_ap.rearrange("(c p) s -> p c s", p=128),
                in_=X[:].rearrange("p (c s) -> p c s", c=NCH)), 16)
    return gd.tok()


def emit_final_wait(p, toks):
    all_toks = list(toks) + getattr(p, "load_toks", [])
    with p.nc.Block() as blk:
        @blk.sync
        def _(sync):
            wait(sync, *all_toks)


def emit_l0_ffn(p, X, deps):
    nc = p.nc
    sV, sT, sS, sG = p.sem("v"), p.sem("t"), p.sem("s"), p.sem("g")
    st2 = ExitStack()
    sbuf = lambda n, sh, dt: st2.enter_context(nc.sbuf_tensor(n, sh, dt))
    psum = lambda n, sh: st2.enter_context(nc.psum_tensor(n, sh, F32))

    wt = p.din("l0f_wt", [1024, 1024])
    wpt = p.din("l0f_wpt", [1024, 256])
    bp = p.din("l0f_bp", [128, 2])
    simc = p.din("l0f_sim", [128, 8])
    temp = p.din("l0f_temp", [1, 1])
    wdt = p.din("l0f_wdt", [128, 120])
    wut = p.din("l0f_wut", [15, 1024])

    LNB = sbuf("bf_lnb", [128, NCH * S], BF16)
    WT16 = sbuf("bf_wt", [128, NCH * 1024], BF16)
    WPT16 = sbuf("bf_wpt", [128, NCH * 256], BF16)
    BPC = sbuf("bf_bp", [128, 2], F32)
    SIMC = sbuf("bf_simc", [128, 8], F32)
    SIMSQ = sbuf("bf_simsq", [128, 8], F32)
    SIMN = sbuf("bf_simn", [128, 8], BF16)
    TEMP = sbuf("bf_temp", [1, 1], F32)
    WDT16 = sbuf("bf_wdt", [128, 120], BF16)
    WUT16 = sbuf("bf_wut", [15, 1024], BF16)
    tok_wt = emit_big_load(p, WT16[:].rearrange("p (c f) -> p c f", c=NCH),
                           wt.rearrange("(c p) f -> p c f", p=128), cast=True,
                           deps=deps)
    tok_wpt = emit_big_load(p, WPT16[:].rearrange("p (c f) -> p c f", c=NCH),
                            wpt.rearrange("(c p) f -> p c f", p=128), cast=True,
                            deps=deps)
    tok_wdt = emit_big_load(p, WDT16[:], wdt, cast=True, deps=deps)
    tok_wut = emit_big_load(p, WUT16[:], wut, cast=True, deps=deps)
    tok_bp = emit_big_load(p, BPC[:], bp, deps=deps)
    tok_sim = emit_big_load(p, SIMC[:], simc, deps=deps)
    tok_temp = emit_big_load(p, TEMP[:], temp, deps=deps)

    tok_ln = emit_layernorm(p, X, LNB, "bf_ln", deps=deps)

    PRJ = sbuf("bf_prj", [128, 2 * S], BF16)
    PSQ = sbuf("bf_psq", [128, 2 * 512], BF16)
    R1 = sbuf("bf_r1", [1, S], F32)
    R4 = sbuf("bf_r4", [1, 16], F32)
    IV4 = sbuf("bf_iv4", [128, 4], F32)
    INVC = sbuf("bf_invc", [128, 16], F32)
    GT = sbuf("bf_gt", [128, 64], F32)
    GM = sbuf("bf_gm", [128, 16], F32)
    GMK = sbuf("bf_gmk", [128, 64], F32)
    RETT = sbuf("bf_rett", [4, S], BF16)
    RETX = sbuf("bf_retx", [15, S], BF16)
    DT = sbuf("bf_dt", [15, S], BF16)

    stPN = ExitStack()
    PN = stPN.enter_context(nc.psum_tensor("bf_pn", [1, S], F32))
    stPP = ExitStack()
    PP = stPP.enter_context(nc.psum_tensor("bf_pp", [128, 512], F32))
    PP2 = stPP.enter_context(nc.psum_tensor("bf_pp2", [128, 512], F32))
    t0 = sT.n
    s0 = sS.n
    sched = []
    for i in range(8):
        sched.append(("P", i))
        if i >= 2:
            sched.append(("N", i - 2))
    sched += [("N", 6), ("N", 7)]
    t_at = {}
    for k, ev in enumerate(sched):
        t_at[ev] = t0 + k + 1
    with nc.Block() as blk:
        @blk.tensor
        def _(tensor):
            wait(tensor, tok_ln, tok_wpt, tok_bp)
            for ev in sched:
                kind, i = ev
                if kind == "P":
                    pc, sb = divmod(i, 4)
                    PPx = PP if i % 2 == 0 else PP2
                    if i >= 2:
                        wait(tensor, (sS, s0 + 2 * (i - 2) + 2))
                    for dc in range(NCH):
                        mm = tensor.matmul(
                            out=PPx[:, 0:512],
                            lhsT=WPT16[:, dc * 256 + 128 * pc: dc * 256 + 128 * (pc + 1)],
                            rhs=LNB[:, dc * S + 512 * sb: dc * S + 512 * (sb + 1)],
                            start=(dc == 0), stop=(dc == NCH - 1))
                    sT.inc(mm)
                else:
                    pc, sb = divmod(i, 4)
                    wait(tensor, (sS, s0 + 2 * i + 2))
                    mm = tensor.matmul(out=PN[0:1, 512 * sb:512 * (sb + 1)],
                                       lhsT=p.ones_bf[:],
                                       rhs=PSQ[:, (i % 2) * 512:(i % 2) * 512 + 512],
                                       start=(pc == 0), stop=(pc == 1))
                    sT.inc(mm)

        @blk.scalar
        def _(scalar):
            for i in range(8):
                pc, sb = divmod(i, 4)
                wait(scalar, (sT, t_at[("P", i)]))
                if i >= 2:
                    wait(scalar, (sT, t_at[("N", i - 2)]))
                tevac = sS.inc(scalar.activation(
                    out=PRJ[:, pc * S + 512 * sb: pc * S + 512 * (sb + 1)],
                    in_=(PP if i % 2 == 0 else PP2)[:, 0:512],
                    func=AF.Identity, bias=BPC[:, pc:pc + 1]))
                wait(scalar, tevac)
                sS.inc(scalar.activation(
                    out=PSQ[:, (i % 2) * 512:(i % 2) * 512 + 512],
                    in_=PRJ[:, pc * S + 512 * sb: pc * S + 512 * (sb + 1)],
                    func=AF.Square))
    tok_projmm = sT.tok()
    stPP.close()

    sB0 = sS.n
    with nc.Block() as blk:
        @blk.scalar
        def _(scalar):
            wait(scalar, tok_projmm)
            sS.inc(scalar.activation(out=R1[0:1, :], in_=PN[0:1, :], func=AF.Sqrt))

        @blk.vector
        def _(vector):
            wait(vector, (sS, sB0 + 1))
            tmx = sV.inc(vector.tensor_scalar_max(out=R1[0:1, :], in0=R1[0:1, :], scalar1=1e-12))
            wait(vector, tmx)
            sV.inc(vector.reciprocal(out=R1[0:1, :], in_=R1[0:1, :]))
    tok_invn = sV.tok()
    stPN.close()

    stPIC = ExitStack()
    PIC = stPIC.enter_context(nc.psum_tensor("bf_pic", [128, 16], F32))
    tI = sT.n
    with nc.Block() as blk:
        @blk.tensor
        def _(tensor):
            wait(tensor, tok_invn)
            for t in range(16):
                tr = tensor.transpose(out=PIC[:, t:t + 1], in_=R1[0:1, 128 * t:128 * (t + 1)],
                                      identity=p.ones_row_f[0:1, 0:1])
            sT.inc(tr)

        @blk.vector
        def _(vector):
            wait(vector, (sT, tI + 1))
            sV.inc(vector.tensor_copy(out=INVC[:], in_=PIC[:]))
    tok_invc = sV.tok()
    stPIC.close()

    sC0, vC0, tC0, gC0 = sS.n, sV.n, sT.n, sG.n
    stS8 = ExitStack()
    PS8 = stS8.enter_context(nc.psum_tensor("bf_ps8", [1, 16], F32))
    PIV = stS8.enter_context(nc.psum_tensor("bf_piv", [128, 4], F32))
    with nc.Block() as blk:
        @blk.scalar
        def _(scalar):
            wait(scalar, tok_sim)
            sS.inc(scalar.activation(out=SIMSQ[:], in_=SIMC[:], func=AF.Square))
            wait(scalar, tok_temp, (sG, gC0 + 1))
            sS.inc(scalar.activation(out=TEMP[:], in_=TEMP[:], func=AF.Exp))

        @blk.gpsimd
        def _(gpsimd):
            wait(gpsimd, tok_temp)
            sG.inc(gpsimd.tensor_scalar_min(out=TEMP[:], in0=TEMP[:],
                                            scalar1=float(np.log(1.0 / 0.01))))

        @blk.tensor
        def _(tensor):
            wait(tensor, (sS, sC0 + 1))
            sT.inc(tensor.matmul(out=PS8[0:1, 0:8], lhsT=p.ones_f32col[:],
                                 rhs=SIMSQ[:], start=True, stop=True))

        @blk.vector
        def _(vector):
            wait(vector, (sT, tC0 + 1))
            tcp = sV.inc(vector.tensor_copy(out=R4[0:1, 8:16], in_=PS8[0:1, 0:8]))
            wait(vector, tcp)
            sV.inc(vector.tensor_add(out=R4[0:1, 0:4], in0=R4[0:1, 8:12], in1=R4[0:1, 12:16]))
    sD0, vD0, tD0 = sS.n, sV.n, sT.n
    with nc.Block() as blk:
        @blk.scalar
        def _(scalar):
            wait(scalar, (sV, vD0))
            sS.inc(scalar.activation(out=R4[0:1, 4:8], in_=R4[0:1, 0:4], func=AF.Sqrt))

        @blk.vector
        def _(vector):
            wait(vector, (sS, sD0 + 1))
            t1 = sV.inc(vector.tensor_scalar_max(out=R4[0:1, 4:8], in0=R4[0:1, 4:8], scalar1=1e-12))
            wait(vector, t1)
            t2 = sV.inc(vector.reciprocal(out=R4[0:1, 4:8], in_=R4[0:1, 4:8]))
            wait(vector, t2)
            sV.inc(vector.tensor_scalar_mul(out=R4[0:1, 4:8], in0=R4[0:1, 4:8],
                                            scalar1=TEMP[0:1, 0:1]))

        @blk.tensor
        def _(tensor):
            wait(tensor, (sV, vD0 + 3))
            sT.inc(tensor.matmul(out=PIV[:], lhsT=p.ones_row_f[:], rhs=R4[0:1, 4:8],
                                 start=True, stop=True))

        @blk.scalar
        def _(scalar):
            wait(scalar, (sT, tD0 + 1), (sS, sD0 + 1))
            sS.inc(scalar.activation(out=IV4[:], in_=PIV[:], func=AF.Copy))
    sE0 = sS.n
    with nc.Block() as blk:
        @blk.vector
        def _(vector):
            wait(vector, (sS, sE0))
            sV.inc(vector.tensor_tensor(
                out=SIMN[:].rearrange("p (c e) -> p c e", c=2),
                in0=SIMC[:].rearrange("p (c e) -> p c e", c=2),
                in1=bass.AP(IV4, 0, [[4, 128], [0, 2], [1, 4]]),
                op=ALU.mult))
    tok_simn = sV.tok()
    stS8.close()

    stPG = ExitStack()
    PG = stPG.enter_context(nc.psum_tensor("bf_pg", [128, 64], F32))
    tG0 = sT.n
    with nc.Block() as blk:
        @blk.tensor
        def _(tensor):
            wait(tensor, tok_simn, tok_projmm)
            for t in range(16):
                for pc in range(2):
                    mm = tensor.matmul(out=PG[:, 4 * t:4 * (t + 1)],
                                       lhsT=PRJ[:, pc * S + 128 * t: pc * S + 128 * (t + 1)],
                                       rhs=SIMN[:, 4 * pc:4 * (pc + 1)],
                                       start=(pc == 0), stop=(pc == 1))
            sT.inc(mm)

        @blk.vector
        def _(vector):
            wait(vector, (sT, tG0 + 1), tok_invc)
            t1 = sV.inc(vector.tensor_tensor(
                out=GT[:].rearrange("p (t e) -> p t e", t=16),
                in0=PG[:].rearrange("p (t e) -> p t e", t=16),
                in1=bass.AP(INVC, 0, [[16, 128], [1, 16], [0, 4]]),
                op=ALU.mult))
            wait(vector, t1)
            t2 = sV.inc(vector.tensor_reduce(out=GM[:], in_=GT[:].rearrange("p (t e) -> p t e", t=16),
                                             axis=AX.X, op=ALU.max, negate=True))
            wait(vector, t2)
            sV.inc(vector.tensor_tensor(
                out=GT[:].rearrange("p (t e) -> p t e", t=16),
                in0=GT[:].rearrange("p (t e) -> p t e", t=16),
                in1=bass.AP(GM, 0, [[16, 128], [1, 16], [0, 4]]),
                op=ALU.add))
        vG0 = sV.n

        @blk.scalar
        def _(scalar):
            wait(scalar, (sV, vG0))
            sS.inc(scalar.activation(out=GT[:], in_=GT[:], func=AF.Exp))
        sG0 = sS.n

    vH0 = sV.n
    sH0 = sS.n
    with nc.Block() as blk:
        @blk.vector
        def _(vector):
            wait(vector, (sS, sG0))
            t1 = sV.inc(vector.tensor_reduce(out=GM[:], in_=GT[:].rearrange("p (t e) -> p t e", t=16),
                                             axis=AX.X, op=ALU.add))
            wait(vector, t1)
            t2 = sV.inc(vector.reciprocal(out=GM[:], in_=GM[:]))
            wait(vector, t2)
            t3 = sV.inc(vector.tensor_tensor(
                out=GT[:].rearrange("p (t e) -> p t e", t=16),
                in0=GT[:].rearrange("p (t e) -> p t e", t=16),
                in1=bass.AP(GM, 0, [[16, 128], [1, 16], [0, 4]]),
                op=ALU.mult))
            wait(vector, t3)
            t4 = sV.inc(vector.tensor_reduce(out=GM[:], in_=GT[:].rearrange("p (t e) -> p t e", t=16),
                                             axis=AX.X, op=ALU.max))
            wait(vector, t4)
            t5 = sV.inc(vector.tensor_tensor(
                out=GMK[:].rearrange("p (t e) -> p t e", t=16),
                in0=GT[:].rearrange("p (t e) -> p t e", t=16),
                in1=bass.AP(GM, 0, [[16, 128], [1, 16], [0, 4]]),
                op=ALU.is_ge))
            wait(vector, t5)
            sV.inc(vector.tensor_mul(out=GT[:], in0=GT[:], in1=GMK[:]))
    tok_ret = sV.tok()
    stPG.close()

    stPRT = ExitStack()
    PRT = stPRT.enter_context(nc.psum_tensor("bf_prt", [4, 128], F32))
    PRT2 = stPRT.enter_context(nc.psum_tensor("bf_prt2", [4, 128], F32))
    tR0 = sT.n
    vR0 = sV.n
    with nc.Block() as blk:
        @blk.tensor
        def _(tensor):
            wait(tensor, tok_ret, p.tok_idf)
            for t in range(16):
                if t >= 2:
                    wait(tensor, (sV, vR0 + (t - 2) + 1))
                sT.inc(tensor.transpose(out=(PRT if t % 2 == 0 else PRT2)[:],
                                        in_=GT[:, 4 * t:4 * (t + 1)],
                                        identity=p.IDF[:]))

        @blk.vector
        def _(vector):
            for t in range(16):
                wait(vector, (sT, tR0 + t + 1))
                sV.inc(vector.tensor_copy(out=RETT[0:4, 128 * t:128 * (t + 1)],
                                          in_=(PRT if t % 2 == 0 else PRT2)[:]))
    tok_rett = sV.tok()

    stPRT.close()
    stPRX = ExitStack()
    PRX = stPRX.enter_context(nc.psum_tensor("bf_prx", [15, 512], F32))
    tX0 = sT.n
    vX0 = sV.n
    with nc.Block() as blk:
        @blk.tensor
        def _(tensor):
            wait(tensor, tok_rett, p.tok_e415)
            for sb in range(4):
                if sb >= 1:
                    wait(tensor, (sV, vX0 + sb))
                sT.inc(tensor.matmul(out=PRX[:], lhsT=p.E415[:],
                                     rhs=RETT[0:4, 512 * sb:512 * (sb + 1)],
                                     start=True, stop=True))

        @blk.vector
        def _(vector):
            for sb in range(4):
                wait(vector, (sT, tX0 + sb + 1))
                sV.inc(vector.tensor_copy(out=RETX[0:15, 512 * sb:512 * (sb + 1)], in_=PRX[:]))
    tok_retx = sV.tok()

    stPRX.close()
    stPD = ExitStack()
    PD = stPD.enter_context(nc.psum_tensor("bf_pd", [15, 512], F32))
    tD1 = sT.n
    vD1 = sV.n
    with nc.Block() as blk:
        @blk.tensor
        def _(tensor):
            wait(tensor, tok_wdt, tok_retx)
            for sb in range(4):
                if sb >= 1:
                    wait(tensor, (sV, vD1 + sb))
                for dc in range(NCH):
                    mm = tensor.matmul(out=PD[:],
                                       lhsT=WDT16[:, 15 * dc:15 * (dc + 1)],
                                       rhs=LNB[:, dc * S + 512 * sb: dc * S + 512 * (sb + 1)],
                                       start=(dc == 0), stop=(dc == NCH - 1))
                sT.inc(mm)

        @blk.vector
        def _(vector):
            for sb in range(4):
                wait(vector, (sT, tD1 + sb + 1))
                sV.inc(vector.tensor_tensor(out=DT[0:15, 512 * sb:512 * (sb + 1)], in0=PD[:],
                                            in1=RETX[0:15, 512 * sb:512 * (sb + 1)],
                                            op=ALU.mult))
    tok_dt = sV.tok()

    stPD.close()
    stPO = ExitStack()
    PO = stPO.enter_context(nc.psum_tensor("bf_po", [128, 512], F32))
    PO2 = stPO.enter_context(nc.psum_tensor("bf_po2", [128, 512], F32))
    tF0 = sT.n
    vF0 = sV.n
    with nc.Block() as blk:
        @blk.tensor
        def _(tensor):
            wait(tensor, tok_wt, tok_wut, tok_dt)
            i = 0
            for oc in range(NCH):
                for sb in range(4):
                    POx = PO if i % 2 == 0 else PO2
                    if i >= 2:
                        wait(tensor, (sV, vF0 + (i - 2) + 1))
                    for dc in range(NCH):
                        tensor.matmul(
                            out=POx[:, 0:512],
                            lhsT=WT16[:, dc * 1024 + 128 * oc: dc * 1024 + 128 * (oc + 1)],
                            rhs=LNB[:, dc * S + 512 * sb: dc * S + 512 * (sb + 1)],
                            start=(dc == 0), stop=False)
                    mm = tensor.matmul(out=POx[:, 0:512],
                                       lhsT=WUT16[0:15, 128 * oc:128 * (oc + 1)],
                                       rhs=DT[0:15, 512 * sb:512 * (sb + 1)],
                                       start=False, stop=True)
                    sT.inc(mm)
                    i += 1

        @blk.vector
        def _(vector):
            i = 0
            for oc in range(NCH):
                for sb in range(4):
                    wait(vector, (sT, tF0 + i + 1))
                    dst = X[:, oc * S + 512 * sb: oc * S + 512 * (sb + 1)]
                    sV.inc(vector.scalar_tensor_tensor(
                        out=dst, in0=(PO if i % 2 == 0 else PO2)[:, 0:512],
                        scalar=1.0, in1=dst, op0=ALU.mult, op1=ALU.add))
                    i += 1
    tok_done = sV.tok()
    stPO.close()
    st2.close()
    return tok_done



def emit_spill(p, X, dram_ap, deps):
    ld = p.sem("uld%d" % p._ldctr); p._ldctr += 1
    with p.nc.Block() as blk:
        @blk.gpsimd
        def _(gpsimd):
            wait(gpsimd, *deps)
            ld.inc(gpsimd.dma_start(out=dram_ap.rearrange("(c p) s -> p c s", p=128),
                                    in_=X[:].rearrange("p (c s) -> p c s", c=NCH)), 16)
    p.load_toks.append(ld.tok())
    return ld.tok()


def emit_restore(p, X, dram_ap, deps):
    ld = p.sem("uld%d" % p._ldctr); p._ldctr += 1
    with p.nc.Block() as blk:
        @blk.sync
        def _(sync):
            wait(sync, *deps)
            ld.inc(sync.dma_start(out=X[:].rearrange("p (c s) -> p c s", c=NCH),
                                  in_=dram_ap.rearrange("(c p) s -> p c s", p=128)), 16)
    p.load_toks.append(ld.tok())
    return ld.tok()


def emit_l0_attn(p, X, deps, stop_at=None, dbg_raw_ot=False):
    nc = p.nc
    sV, sT, sS, sG = p.sem("v"), p.sem("t"), p.sem("s"), p.sem("g")
    st2 = ExitStack()
    sbuf = lambda n, sh, dt: st2.enter_context(nc.sbuf_tensor(n, sh, dt))

    w1t = p.din("l0a_w1t", [1024, 1024])
    w2t = p.din("l0a_w2t", [1024, 1024])
    w3t = p.din("l0a_w3t", [1024, 1024])
    wot = p.din("l0a_wot", [1024, 1024])
    cost = p.din("l0a_cos", [128, 2048])
    sint = p.din("l0a_sina", [128, 2048])
    xsp = p.dtmp("l0a_xsp", [1024, S])

    LNB = sbuf("af_lnb", [128, NCH * S], BF16)
    A16 = sbuf("af_a16", [128, NT * 1024], BF16)
    B16 = sbuf("af_b16", [128, NCH * S], BF16)
    C16 = sbuf("af_c16", [128, NCH * S], BF16)
    TMPA = sbuf("af_tmpa", [128, 2 * 1024], BF16)
    TMPB = sbuf("af_tmpb", [128, 2 * 1024], BF16)
    SQH = sbuf("af_sqh", [128, 2048], BF16)
    XB = X[:].bitcast(BF16)
    XF = X[:]
    D16 = XB
    WTB = XB[:, 16384:24576]
    COS = XF[:, 12288:14336]
    SIN = XF[:, 14336:16384]
    LNBF = LNB[:].bitcast(F32)
    LNBB = LNB[:]
    EXPT = LNBB[:, 0:1024]
    BCS = LNBF[:, 512:1024]
    SR = LNBF[0:1, 1024:1536]
    RLN = LNBF[0:1, 1536:2048]
    ISR = LNBF[0:1, 2048:2560]
    MQ = LNBF[0:1, 2560:2592]
    NMB = LNBF[:, 2592:2600]
    WOB = LNBB[:, 6144:14336]
    B16F = B16[:].bitcast(F32)
    C16F = C16[:].bitcast(F32)
    ln_scratch = (A16[:, 0:1024], C16F[0:1, 0:6144], B16F[:, 0:2048],
                  B16F[:, 2048:4096], A16[:, 1024:2048])

    tok_ln = emit_layernorm(p, X, LNB, "af_ln", deps=deps, scratch=ln_scratch)
    if stop_at == "ln":
        st2.pop_all()
        return ("dump16", LNB[:], tok_ln)
    tok_spill = emit_spill(p, X, xsp, [tok_ln])
    tok_cos = emit_big_load(p, COS, cost, deps=[tok_spill])
    tok_sin = emit_big_load(p, SIN, sint, deps=[tok_spill])

    stPT = ExitStack()
    PT = stPT.enter_context(nc.psum_tensor("af_pt", [128, 512], F32))
    PT2 = stPT.enter_context(nc.psum_tensor("af_pt2", [128, 512], F32))
    PTRa = stPT.enter_context(nc.psum_tensor("af_ptra", [128, 128], BF16))
    PTRb = stPT.enter_context(nc.psum_tensor("af_ptrb", [128, 128], BF16))

    def project_tok(wsrc, tok_w, qscale, prior_tok):
        t0, s0 = sT.n, sS.n
        evac_toks = []
        with nc.Block() as blk:
            @blk.tensor
            def _(tensor):
                wait(tensor, tok_ln, tok_w, prior_tok)
                for i in range(32):
                    t, half = divmod(i, 2)
                    PTx = PT if i % 2 == 0 else PT2
                    if i >= 2:
                        wait(tensor, (sS, s0 + (i - 2) + 1))
                    for ec in range(NCH):
                        mm = tensor.matmul(
                            out=PTx[:, 0:512],
                            lhsT=LNB[:, ec * S + 128 * t: ec * S + 128 * (t + 1)],
                            rhs=WTB[:, ec * 1024 + 512 * half: ec * 1024 + 512 * (half + 1)],
                            start=(ec == 0), stop=(ec == NCH - 1))
                    sT.inc(mm)

            @blk.scalar
            def _(scalar):
                for i in range(32):
                    t, half = divmod(i, 2)
                    wait(scalar, (sT, t0 + i + 1))
                    tk = sS.inc(scalar.activation(
                        out=A16[:, t * 1024 + 512 * half: t * 1024 + 512 * (half + 1)],
                        in_=(PT if i % 2 == 0 else PT2)[:, 0:512],
                        func=AF.Identity, scale=qscale))
                    evac_toks.append(tk)
        return evac_toks, sS.tok()

    def rope_inplace(tok_proj_evacs):
        v0, g0 = sV.n, sG.n
        add_toks = []
        with nc.Block() as blk:
            @blk.vector
            def _(vector):
                wait(vector, tok_cos, tok_sin)
                for t in range(NT):
                    tv = t % 2
                    wait(vector, tok_proj_evacs[2 * t + 1])
                    if t >= 2:
                        wait(vector, (sV, v0 + 3 * (t - 2) + 3))
                    tm = sV.inc(vector.tensor_mul(
                        out=TMPA[:, (t % 2) * 1024:(t % 2) * 1024 + 1024],
                        in0=A16[:, t * 1024:(t + 1) * 1024],
                        in1=COS[:, tv * 1024:(tv + 1) * 1024]))
                    swap = bass.AP(A16, t * 1024 + 1,
                                   [[NT * 1024, 128], [2, 512], [-1, 2]])
                    tw = sV.inc(vector.tensor_mul(
                        out=TMPB[:, (t % 2) * 1024:(t % 2) * 1024 + 1024],
                        in0=swap.rearrange("p a b -> p (a b)") if False else swap,
                        in1=SIN[:, tv * 1024:(tv + 1) * 1024].rearrange("p (a b) -> p a b", b=2)))
                    wait(vector, tw)
                    tk = sV.inc(vector.tensor_add(
                        out=A16[:, t * 1024:(t + 1) * 1024],
                        in0=TMPA[:, (t % 2) * 1024:(t % 2) * 1024 + 1024],
                        in1=TMPB[:, (t % 2) * 1024:(t % 2) * 1024 + 1024]))
                    add_toks.append(tk)
        return add_toks

    def transpose_to(dst, truej, tile_toks):
        t0, v0 = sT.n, sV.n
        with nc.Block() as blk:
            @blk.tensor
            def _(tensor):
                wait(tensor, p.tok_idb)
                for i in range(128):
                    t, c = divmod(i, 8)
                    wait(tensor, tile_toks[t])
                    if i >= 2:
                        wait(tensor, (sV, v0 + (i - 2) + 1))
                    sT.inc(tensor.transpose(
                        out=(PTRa if i % 2 == 0 else PTRb)[:],
                        in_=A16[:, t * 1024 + 128 * c: t * 1024 + 128 * (c + 1)],
                        identity=p.IDB[:]))

            @blk.vector
            def _(vector):
                for i in range(128):
                    t, c = divmod(i, 8)
                    h, trel = divmod(t, 2)
                    wait(vector, (sT, t0 + i + 1))
                    if truej:
                        dstap = bass.AP(dst, h * 2048 + 1024 * trel + c,
                                        [[NCH * S, 128], [8, 128]])
                    else:
                        dstap = dst[:, h * 2048 + 256 * c + 128 * trel:
                                    h * 2048 + 256 * c + 128 * trel + 128]
                    sV.inc(vector.tensor_copy(
                        out=dstap, in_=(PTRa if i % 2 == 0 else PTRb)[:]))
        return sV.tok()

    tok_w = emit_big_load(p, WTB.rearrange("p (c f) -> p c f", c=NCH),
                          w1t.rearrange("(c p) f -> p c f", p=128), cast=True,
                          deps=[tok_spill])
    ev, tok_pq = project_tok(w1t, tok_w, 1.0 / math.sqrt(8.0), None)
    if stop_at == "qproj":
        st2.pop_all()
        return ("dump16", A16[:], tok_pq)
    rtoks = rope_inplace(ev)
    if stop_at == "qrope":
        st2.pop_all()
        return ("dump16", A16[:], rtoks[-1])
    tok_qf = transpose_to(B16, True, rtoks)
    if stop_at == "qf":
        st2.pop_all()
        return ("dump16", B16[:], tok_qf)
    tok_w = emit_big_load(p, WTB.rearrange("p (c f) -> p c f", c=NCH),
                          w2t.rearrange("(c p) f -> p c f", p=128), cast=True,
                          deps=[tok_pq])
    ev, tok_pk = project_tok(w2t, tok_w, 1.0, tok_qf)
    rtoks = rope_inplace(ev)
    tok_kf = transpose_to(C16, False, rtoks)
    if stop_at == "kf":
        st2.pop_all()
        return ("dump16", C16[:], tok_kf)
    tok_w = emit_big_load(p, WTB.rearrange("p (c f) -> p c f", c=NCH),
                          w3t.rearrange("(c p) f -> p c f", p=128), cast=True,
                          deps=[tok_pk])
    ev, tok_pv = project_tok(w3t, tok_w, 1.0, tok_kf)
    tok_v = tok_pv
    if stop_at == "v":
        st2.pop_all()
        return ("dump16", A16[:], tok_v)

    stPT.close()
    stNR = ExitStack()
    PNR = stNR.enter_context(nc.psum_tensor("af_pnr", [1, S], F32))
    tN0, sN0, vN0 = sT.n, sS.n, sV.n
    with nc.Block() as blk:
        @blk.scalar
        def _(scalar):
            wait(scalar, tok_qf, tok_kf)
            for it in range(16):
                src = B16 if it < 8 else C16
                h = it % 8
                if it > 0:
                    wait(scalar, (sT, tN0 + 4 * it))
                sS.inc(scalar.activation(out=SQH[:], in_=src[:, h * S:(h + 1) * S],
                                         func=AF.Square))

        @blk.tensor
        def _(tensor):
            wait(tensor, p.consts_ready)
            for it in range(16):
                wait(tensor, (sS, sN0 + it + 1))
                if it > 0:
                    wait(tensor, (sV, vN0 + it))
                for sb in range(4):
                    mm = tensor.matmul(out=PNR[0:1, 512 * sb:512 * (sb + 1)],
                                       lhsT=p.ones_bf[:],
                                       rhs=SQH[:, 512 * sb:512 * (sb + 1)],
                                       start=True, stop=True)
                    sT.inc(mm)

        @blk.vector
        def _(vector):
            for it in range(16):
                wait(vector, (sT, tN0 + 4 * it + 4))
                sV.inc(vector.tensor_reduce(out=MQ[0:1, it:it + 1], in_=PNR[0:1, :],
                                            axis=AX.X, op=ALU.max))
    vN1, sN1, tN1 = sV.n, sS.n, sT.n
    stBQ = ExitStack()
    PBQ = stBQ.enter_context(nc.psum_tensor("af_pbq", [128, 8], F32))
    with nc.Block() as blk:
        @blk.vector
        def _(vector):
            wait(vector, (sV, vN1))
            tk = sV.inc(vector.tensor_mul(out=MQ[0:1, 16:24], in0=MQ[0:1, 0:8],
                                          in1=MQ[0:1, 8:16]))
        @blk.scalar
        def _(scalar):
            wait(scalar, (sV, vN1 + 1))
            sS.inc(scalar.activation(out=MQ[0:1, 24:32], in_=MQ[0:1, 16:24],
                                     func=AF.Sqrt))
        @blk.gpsimd
        def _(gpsimd):
            wait(gpsimd, (sS, sN1 + 1))
            sG.inc(gpsimd.tensor_scalar_mul(out=MQ[0:1, 24:32], in0=MQ[0:1, 24:32],
                                            scalar1=-1.0))
    gN1 = sG.n
    with nc.Block() as blk:
        @blk.tensor
        def _(tensor):
            wait(tensor, (sG, gN1))
            sT.inc(tensor.matmul(out=PBQ[:], lhsT=p.ones_row_f[:], rhs=MQ[0:1, 24:32],
                                 start=True, stop=True))
        @blk.scalar
        def _(scalar):
            wait(scalar, (sT, tN1 + 1))
            sS.inc(scalar.activation(out=NMB[:], in_=PBQ[:], func=AF.Copy))
    tok_nmb = sS.tok()
    stBQ.close()
    stNR.close()
    if stop_at == "nmb":
        st2.pop_all()
        return ("dumpf", NMB, tok_nmb)

    stAT = ExitStack()
    OT = stAT.enter_context(nc.psum_tensor("af_ot", [128, S], F32))
    STa = stAT.enter_context(nc.psum_tensor("af_sta", [128, 512], F32))
    STb = stAT.enter_context(nc.psum_tensor("af_stb", [128, 512], F32))
    SIG = stAT.enter_context(nc.psum_tensor("af_sig", [1, 512], F32))
    BCp = stAT.enter_context(nc.psum_tensor("af_bcp", [128, 512], F32))

    t0, s0, v0 = sT.n, sS.n, sV.n
    def t_s(h, qb, k):
        base = t0 + (h * 4 + qb) * 33
        return base + (1 if k == 0 else 2 * k)

    def t_sig(h, qb, k):
        base = t0 + (h * 4 + qb) * 33
        return base + (2 * k + 3 if k < 15 else 32)

    def t_bc(h, qb):
        return t0 + (h * 4 + qb) * 33 + 33

    def s_exp(h, qb, k):
        return s0 + (h * 4 + qb) * 18 + k + 1

    def s_row(h, qb, j):
        return s0 + (h * 4 + qb) * 18 + 16 + j

    def v_sig(h, qb):
        return v0 + (h * 4 + qb) * 3 + 1

    def v_bcs(h, qb):
        return v0 + (h * 4 + qb) * 3 + 2

    def v_ot(h, qb):
        return v0 + (h * 4 + qb) * 3 + 3

    with nc.Block() as blk:
        @blk.tensor
        def _(tensor):
            wait(tensor, tok_nmb, tok_v)
            for h in range(8):
                for qb in range(4):
                    idx = h * 4 + qb
                    if idx > 0:
                        wait(tensor, (sV, v_sig(h if qb else h - 1, qb - 1 if qb else 3)))
                    if h > 0:
                        wait(tensor, (sV, v_ot(h - 1, qb)))
                    for k in range(17):
                        if k < 16:
                            g_ = idx * 16 + k
                            if g_ >= 2:
                                i2, k2 = divmod(g_ - 2, 16)
                                wait(tensor, (sS, s_exp(i2 // 4, i2 % 4, k2)))
                            STx = STa if k % 2 == 0 else STb
                            sT.inc(tensor.matmul(
                                out=STx[:, 0:512],
                                lhsT=C16[:, h * S + 128 * k: h * S + 128 * (k + 1)],
                                rhs=B16[:, h * S + 512 * qb: h * S + 512 * (qb + 1)],
                                start=True, stop=True))
                        if k >= 1:
                            kk = k - 1
                            wait(tensor, (sS, s_exp(h, qb, kk)))
                            tile = 2 * h + (kk % 2)
                            jlo = kk // 2
                            tensor.matmul(
                                out=OT[:, 512 * qb:512 * (qb + 1)],
                                lhsT=A16[:, tile * 1024 + 128 * jlo:
                                         tile * 1024 + 128 * (jlo + 1)],
                                rhs=EXPT[:, (kk % 2) * 512:(kk % 2) * 512 + 512],
                                start=(kk == 0), stop=(kk == 15))
                            sT.inc(tensor.matmul(
                                out=SIG[0:1, 0:512],
                                lhsT=p.ones_bf[:],
                                rhs=EXPT[:, (kk % 2) * 512:(kk % 2) * 512 + 512],
                                start=(kk == 0), stop=(kk == 15)))
                    wait(tensor, (sS, s_row(h, qb, 2)))
                    if idx > 0:
                        wait(tensor, (sV, v_bcs(h if qb else h - 1, qb - 1 if qb else 3)))
                    sT.inc(tensor.matmul(out=BCp[:, 0:512], lhsT=p.ones_row_f[:],
                                         rhs=ISR[0:1, 0:512], start=True, stop=True))

        @blk.scalar
        def _(scalar):
            for h in range(8):
                for qb in range(4):
                    for k in range(16):
                        wait(scalar, (sT, t_s(h, qb, k)))
                        g_ = (h * 4 + qb) * 16 + k
                        if g_ >= 2:
                            i2, k2 = divmod(g_ - 2, 16)
                            wait(scalar, (sT, t_sig(i2 // 4, i2 % 4, k2)))
                        sS.inc(scalar.activation(
                            out=EXPT[:, (k % 2) * 512:(k % 2) * 512 + 512],
                            in_=(STa if k % 2 == 0 else STb)[:, 0:512],
                            func=AF.Exp, bias=NMB[:, h:h + 1]))
                    wait(scalar, (sV, v_sig(h, qb)))
                    if h * 4 + qb > 0:
                        pv = (h, qb - 1) if qb else (h - 1, 3)
                        wait(scalar, (sT, t_bc(*pv)))
                    t1 = sS.inc(scalar.activation(out=RLN[0:1, 0:512], in_=SR[0:1, 0:512],
                                                  func=AF.Ln))
                    wait(scalar, t1)
                    sS.inc(scalar.activation(out=ISR[0:1, 0:512], in_=RLN[0:1, 0:512],
                                             func=AF.Exp, scale=-1.0,
                                             bias=p.cpool[0:1, 1:2]))

        @blk.gpsimd
        def _(gpsimd):
            if dbg_raw_ot == "rows":
                wait(gpsimd, (sV, v0 + 1))
                gpsimd.tensor_copy(out=XF[0:1, 12288:12800], in_=SR[0:1, 0:512])
                wait(gpsimd, (sS, s0 + 18))
                gpsimd.tensor_copy(out=XF[0:1, 12800:13312], in_=RLN[0:1, 0:512])
                sG.inc(gpsimd.tensor_copy(out=XF[0:1, 13312:13824], in_=ISR[0:1, 0:512]))

        @blk.vector
        def _(vector):
            wait(vector, tok_spill)
            for h in range(8):
                for qb in range(4):
                    wait(vector, (sT, t_sig(h, qb, 15)))
                    if h * 4 + qb > 0:
                        pv = (h, qb - 1) if qb else (h - 1, 3)
                        wait(vector, (sS, s_row(*pv, 1)))
                    sV.inc(vector.tensor_scalar_mul(out=SR[0:1, 0:512],
                                                    in0=SIG[0:1, 0:512],
                                                    scalar1=26489122129.84347))
                    wait(vector, (sT, t_bc(h, qb)))
                    tb = sV.inc(vector.tensor_copy(out=BCS[:], in_=BCp[:, 0:512]))
                    wait(vector, tb)
                    if dbg_raw_ot == "bcs":
                        sV.inc(vector.tensor_copy(
                            out=D16[:, h * S + 512 * qb: h * S + 512 * (qb + 1)],
                            in_=BCS[:]))
                    elif dbg_raw_ot:
                        sV.inc(vector.tensor_copy(
                            out=D16[:, h * S + 512 * qb: h * S + 512 * (qb + 1)],
                            in_=OT[:, 512 * qb:512 * (qb + 1)]))
                    else:
                        sV.inc(vector.scalar_tensor_tensor(
                            out=D16[:, h * S + 512 * qb: h * S + 512 * (qb + 1)],
                            in0=OT[:, 512 * qb:512 * (qb + 1)],
                            scalar=1.0, in1=BCS[:], op0=ALU.mult, op1=ALU.mult))
    tok_attn = sV.tok()
    stAT.close()
    if stop_at == "attn":
        st2.pop_all()
        return ("dump16", D16[:, 0:16384], tok_attn)
    if stop_at == "rows":
        st2.pop_all()
        return ("dumpf", XF[0:1, 12288:13824], [tok_attn, sG.tok()])

    stYT = ExitStack()
    PYa = stYT.enter_context(nc.psum_tensor("af_pya", [128, 128], BF16))
    PYb = stYT.enter_context(nc.psum_tensor("af_pyb", [128, 128], BF16))
    tY0, vY0 = sT.n, sV.n
    with nc.Block() as blk:
        @blk.tensor
        def _(tensor):
            wait(tensor, tok_attn)
            for i in range(128):
                h, jb = divmod(i, 16)
                if i >= 2:
                    wait(tensor, (sV, vY0 + (i - 2) + 1))
                sT.inc(tensor.transpose(
                    out=(PYa if i % 2 == 0 else PYb)[:],
                    in_=D16[:, h * S + 128 * jb: h * S + 128 * (jb + 1)],
                    identity=p.IDB[:]))

        @blk.vector
        def _(vector):
            for i in range(128):
                h, jb = divmod(i, 16)
                wait(vector, (sT, tY0 + i + 1))
                sV.inc(vector.tensor_copy(
                    out=B16[:, jb * 1024 + 128 * h: jb * 1024 + 128 * (h + 1)],
                    in_=(PYa if i % 2 == 0 else PYb)[:]))
    tok_y = sV.tok()
    stYT.close()
    if stop_at == "y":
        st2.pop_all()
        return ("dump16", B16[:], tok_y)

    tok_wo = emit_big_load(p, WOB.rearrange("p (c f) -> p c f", c=NCH),
                           wot.rearrange("(c p) f -> p c f", p=128), cast=True,
                           deps=[tok_attn])
    tok_xr = emit_restore(p, X, xsp, [tok_y])
    stRO = ExitStack()
    RO = stRO.enter_context(nc.psum_tensor("af_ro", [128, 512], F32))
    RO2 = stRO.enter_context(nc.psum_tensor("af_ro2", [128, 512], F32))
    tR0, vR0 = sT.n, sV.n
    with nc.Block() as blk:
        @blk.tensor
        def _(tensor):
            wait(tensor, tok_y, tok_wo, tok_xr)
            i = 0
            for half in range(2):
                for cb in range(NCH):
                    for eb in range(2):
                        ROx = RO if i % 2 == 0 else RO2
                        if i >= 2:
                            wait(tensor, (sV, vR0 + (i - 2) + 1))
                        for ncd in range(NCH):
                            mm = tensor.matmul(
                                out=ROx[:, 0:512],
                                lhsT=B16[:, (8 * half + ncd) * 1024 + 128 * cb:
                                         (8 * half + ncd) * 1024 + 128 * (cb + 1)],
                                rhs=WOB[:, ncd * 1024 + 512 * eb: ncd * 1024 + 512 * (eb + 1)],
                                start=(ncd == 0), stop=(ncd == NCH - 1))
                        sT.inc(mm)
                        i += 1

        @blk.vector
        def _(vector):
            i = 0
            for half in range(2):
                for cb in range(NCH):
                    for eb in range(2):
                        wait(vector, (sT, tR0 + i + 1))
                        dst = X[:, cb * S + 1024 * half + 512 * eb:
                                cb * S + 1024 * half + 512 * eb + 512]
                        sV.inc(vector.scalar_tensor_tensor(
                            out=dst, in0=(RO if i % 2 == 0 else RO2)[:, 0:512],
                            scalar=1.0, in1=dst, op0=ALU.mult, op1=ALU.add))
                        i += 1
    tok_done = sV.tok()
    stRO.close()
    st2.close()
    return tok_done


def build_l0attn_dbg(stop_at, dbg_raw_ot=False):
    p = Prog()
    xin = p.din("xin", [1024, S])
    p.init_consts()
    _load_phase_consts(p, need=("idb",))
    X = p.sb("X", [128, NCH * S], F32)
    tok_x = emit_big_load(p, X[:].rearrange("p (c s) -> p c s", c=NCH),
                          xin.rearrange("(c p) s -> p c s", p=128))
    kind, ap, tok = emit_l0_attn(p, X, deps=[tok_x], stop_at=stop_at, dbg_raw_ot=dbg_raw_ot)
    if kind == "dump16":
        dbg = p.dout("dbg", list(ap.shape), BF16)
    else:
        dbg = p.dout("dbg", list(ap.shape), F32)
    ld = p.sem("uld%d" % p._ldctr); p._ldctr += 1
    toks = tok if isinstance(tok, list) else [tok]
    with p.nc.Block() as blk:
        @blk.gpsimd
        def _(gpsimd):
            wait(gpsimd, *toks)
            ld.inc(gpsimd.dma_start(out=dbg, in_=ap), 16)
    p.load_toks.append(ld.tok())
    emit_final_wait(p, [])
    return p.nc, p


def build_l0attn_prog():
    if "l0attn" in _prog_cache:
        return _prog_cache["l0attn"]
    p = Prog()
    xin = p.din("xin", [1024, S])
    xout = p.dout("xout", [1024, S])
    p.init_consts()
    _load_phase_consts(p, need=("idb",))
    X = p.sb("X", [128, NCH * S], F32)
    tok_x = emit_big_load(p, X[:].rearrange("p (c s) -> p c s", c=NCH),
                          xin.rearrange("(c p) s -> p c s", p=128))
    tok = emit_l0_attn(p, X, deps=[tok_x])
    tok_st = emit_store_xT(p, X, xout, [tok])
    emit_final_wait(p, [tok_st])
    nc = p.close()
    _prog_cache["l0attn"] = (nc, p)
    return nc, p


def _prep_l0attn(inputs):
    d = {}
    d["l0a_w1t"] = np.ascontiguousarray(inputs["L0_attn_W1"].T)
    d["l0a_w2t"] = np.ascontiguousarray(inputs["L0_attn_W2"].T)
    d["l0a_w3t"] = np.ascontiguousarray(inputs["L0_attn_W3"].T)
    d["l0a_wot"] = np.ascontiguousarray(inputs["L0_attn_Wo"].T)
    inv = 1.0 / 10000.0 ** (np.arange(0, 128, 2, dtype=np.float64) / 128.0)
    srow = np.arange(128)
    tvar = np.arange(2)
    pidx = np.arange(1024)
    jlo = pidx // 128
    dloc = pidx % 128
    ii = dloc // 2
    cos = np.zeros((128, 2, 1024), np.float64)
    sina = np.zeros((128, 2, 1024), np.float64)
    for tv in range(2):
        j = 8 * (128 * tv + srow)[:, None] + jlo[None, :]
        th = j * inv[ii][None, :]
        cos[:, tv, :] = np.cos(th)
        sg = np.where(dloc % 2 == 0, -1.0, 1.0)
        sina[:, tv, :] = np.sin(th) * sg[None, :]
    d["l0a_cos"] = cos.reshape(128, 2048).astype(np.float32)
    d["l0a_sina"] = sina.reshape(128, 2048).astype(np.float32)
    return d



def emit_l1_attn(p, X, deps):
    nc = p.nc
    sV, sT, sS, sG = p.sem("v"), p.sem("t"), p.sem("s"), p.sem("g")
    st2 = ExitStack()
    sbuf = lambda n, sh, dt: st2.enter_context(nc.sbuf_tensor(n, sh, dt))

    wqt = p.din("l1c_wqt", [1024, 3072])
    wot = p.din("l1c_wot", [1024, 1024])
    hid = p.din("c_hid16", [128, 128], BF16)
    expd = p.din("c_expd", [16, 1024], BF16)
    xsp = p.dtmp("l1c_xsp", [1024, S])

    LNB = sbuf("cf_lnb", [128, NCH * S], BF16)
    A16 = sbuf("cf_a16", [128, NCH * S], BF16)
    B16 = sbuf("cf_b16", [128, NT * 1024], BF16)
    C16 = sbuf("cf_c16", [128, NCH * S], BF16)
    BKS = sbuf("cf_bks", [128, 256], F32)
    BKC = sbuf("cf_bkc", [128, 264], F32)
    SCNA = sbuf("cf_scna", [128, 32], F32)
    SCNB = sbuf("cf_scnb", [128, 32], F32)
    HID = sbuf("cf_hid", [128, 128], BF16)
    EXPD = sbuf("cf_expd", [16, 1024], BF16)
    CTXR = sbuf("cf_ctxr", [128, 64], F32)
    CTXB = sbuf("cf_ctxb", [128, 64], BF16)
    XB = X[:].bitcast(BF16)
    VTOK = XB
    WTB = XB[:, 16384:24576]
    LNBF = LNB[:].bitcast(F32)
    QIS = LNBF[0:16, 0:2048]
    QISB16 = LNB[0:16, 4096:6144]
    DINVB = LNB[0:16, 6144:8192]
    DINV = LNBF[0:16, 4096:6144]
    DVX = LNB[:, 14336:16384]
    WOT = LNB[:, 0:8192]
    B16F = B16[:].bitcast(F32)
    C16F = C16[:].bitcast(F32)
    ln_scratch = (A16[:, 0:1024], C16F[0:1, 0:6144], B16F[:, 0:2048],
                  B16F[:, 2048:4096], A16[:, 1024:2048])

    tok_ln = emit_layernorm(p, X, LNB, "cf_ln", deps=deps, scratch=ln_scratch)
    tok_spill = emit_spill(p, X, xsp, [tok_ln])
    tok_hid = emit_big_load(p, HID[:], hid, deps=deps)
    tok_expd = emit_big_load(p, EXPD[:], expd, deps=deps)

    stPT = ExitStack()
    PT = stPT.enter_context(nc.psum_tensor("cf_pt", [128, 512], F32))
    PT2 = stPT.enter_context(nc.psum_tensor("cf_pt2", [128, 512], F32))

    def project(col_off, dst_sl, func, prior_tok, featmaj):
        tok_w = emit_big_load(p, WTB.rearrange("p (c f) -> p c f", c=NCH),
                              wqt[:, col_off:col_off + 1024].rearrange("(c p) f -> p c f", p=128),
                              cast=True, deps=[tok_spill] + ([prior_tok] if prior_tok else []))
        t0, s0 = sT.n, sS.n
        with nc.Block() as blk:
            @blk.tensor
            def _(tensor):
                wait(tensor, tok_ln, tok_w)
                for i in range(32):
                    PTx = PT if i % 2 == 0 else PT2
                    if i >= 2:
                        wait(tensor, (sS, s0 + (i - 2) + 1))
                    if featmaj:
                        oc, sb = divmod(i, 4)
                        for ec in range(NCH):
                            mm = tensor.matmul(
                                out=PTx[:, 0:512],
                                lhsT=WTB[:, ec * 1024 + 128 * oc: ec * 1024 + 128 * (oc + 1)],
                                rhs=LNB[:, ec * S + 512 * sb: ec * S + 512 * (sb + 1)],
                                start=(ec == 0), stop=(ec == NCH - 1))
                    else:
                        t, half = divmod(i, 2)
                        for ec in range(NCH):
                            mm = tensor.matmul(
                                out=PTx[:, 0:512],
                                lhsT=LNB[:, ec * S + 128 * t: ec * S + 128 * (t + 1)],
                                rhs=WTB[:, ec * 1024 + 512 * half: ec * 1024 + 512 * (half + 1)],
                                start=(ec == 0), stop=(ec == NCH - 1))
                    sT.inc(mm)

            @blk.scalar
            def _(scalar):
                for i in range(32):
                    wait(scalar, (sT, t0 + i + 1))
                    if featmaj:
                        oc, sb = divmod(i, 4)
                        dst = dst_sl[:, oc * S + 512 * sb: oc * S + 512 * (sb + 1)]
                    else:
                        t, half = divmod(i, 2)
                        dst = dst_sl[:, t * 1024 + 512 * half: t * 1024 + 512 * (half + 1)]
                    sS.inc(scalar.activation(out=dst, in_=(PT if i % 2 == 0 else PT2)[:, 0:512],
                                             func=func))
        return sS.tok()

    tok_q = project(0, A16[:], AF.Exp, None, True)
    tok_k = project(1024, C16[:], AF.Exp, tok_q, True)
    tok_v = project(2048, VTOK, AF.Identity, tok_k, False)

    stQS = ExitStack()
    QSUMP = stQS.enter_context(nc.psum_tensor("cf_qsum", [16, S], F32))
    t1, v1, s1 = sT.n, sV.n, sS.n
    with nc.Block() as blk:
        @blk.tensor
        def _(tensor):
            wait(tensor, tok_q, tok_hid)
            for c in range(NCH):
                for sb in range(4):
                    mm = tensor.matmul(out=QSUMP[0:16, 512 * sb:512 * (sb + 1)],
                                       lhsT=HID[:, 16 * c:16 * (c + 1)],
                                       rhs=A16[:, c * S + 512 * sb: c * S + 512 * (sb + 1)],
                                       start=(c == 0), stop=(c == NCH - 1))
            sT.inc(mm)

        @blk.scalar
        def _(scalar):
            wait(scalar, (sT, t1 + 1))
            tq = sS.inc(scalar.activation(out=QIS, in_=QSUMP[0:16, :], func=AF.Ln))
            wait(scalar, tq)
            sS.inc(scalar.activation(out=QIS, in_=QIS, func=AF.Exp, scale=-1.0,
                                     bias=p.cpool[0:16, 2:3]))

        @blk.vector
        def _(vector):
            wait(vector, (sS, s1 + 2))
            sV.inc(vector.tensor_copy(out=QISB16, in_=QIS))
    tok_qis = sV.tok()
    stQS.close()

    stQX = ExitStack()
    PXQ = stQX.enter_context(nc.psum_tensor("cf_pxq", [128, 512], F32))
    t2, v2 = sT.n, sV.n
    with nc.Block() as blk:
        @blk.tensor
        def _(tensor):
            wait(tensor, tok_qis, tok_expd)
            for i in range(32):
                c, sb = divmod(i, 4)
                if i >= 1:
                    wait(tensor, (sV, v2 + i))
                sT.inc(tensor.matmul(out=PXQ[:, 0:512],
                                     lhsT=EXPD[0:16, 128 * c:128 * (c + 1)],
                                     rhs=QISB16[0:16, 512 * sb:512 * (sb + 1)],
                                     start=True, stop=True))

        @blk.vector
        def _(vector):
            for i in range(32):
                c, sb = divmod(i, 4)
                wait(vector, (sT, t2 + i + 1))
                sl = A16[:, c * S + 512 * sb: c * S + 512 * (sb + 1)]
                sV.inc(vector.tensor_tensor(out=sl, in0=sl, in1=PXQ[:, 0:512],
                                            op=ALU.mult))
    tok_qsm = sV.tok()
    stQX.close()

    v3 = sV.n
    with nc.Block() as blk:
        @blk.vector
        def _(vector):
            wait(vector, tok_k)
            for c in range(NCH):
                tr = sV.inc(vector.tensor_reduce(
                    out=BKS[:, 32 * c:32 * (c + 1)],
                    in_=C16[:, c * S:(c + 1) * S].rearrange("p (n s) -> p n s", s=64),
                    axis=AX.X, op=ALU.add))
                wait(vector, tr)
                tz = sV.inc(vector.memset(BKC[:, 33 * c:33 * c + 1], 0.0))
                wait(vector, tz)
                sa_, sb_ = SCNA, SCNB
                tc = sV.inc(vector.tensor_copy(out=sa_[:], in_=BKS[:, 32 * c:32 * (c + 1)]))
                wait(vector, tc)
                for k in [1, 2, 4, 8, 16]:
                    ta = sV.inc(vector.tensor_copy(out=sb_[:, 0:k], in_=sa_[:, 0:k]))
                    wait(vector, ta)
                    tb = sV.inc(vector.tensor_add(out=sb_[:, k:32], in0=sa_[:, k:32],
                                                  in1=sa_[:, 0:32 - k]))
                    wait(vector, tb)
                    sa_, sb_ = sb_, sa_
                tf = sV.inc(vector.tensor_copy(out=BKC[:, 33 * c + 1:33 * c + 33], in_=sa_[:]))
                wait(vector, tf)
    tok_bkc = sV.tok()

    stKT = ExitStack()
    PTRa = stKT.enter_context(nc.psum_tensor("cf_ptra", [128, 128], BF16))
    PTRb = stKT.enter_context(nc.psum_tensor("cf_ptrb", [128, 128], BF16))
    t4, v4 = sT.n, sV.n
    with nc.Block() as blk:
        @blk.tensor
        def _(tensor):
            wait(tensor, tok_k, p.tok_idb)
            for i in range(128):
                c, t = divmod(i, 16)
                if i >= 2:
                    wait(tensor, (sV, v4 + (i - 2) + 1))
                sT.inc(tensor.transpose(
                    out=(PTRa if i % 2 == 0 else PTRb)[:],
                    in_=C16[:, c * S + 128 * t: c * S + 128 * (t + 1)],
                    identity=p.IDB[:]))

        @blk.vector
        def _(vector):
            for i in range(128):
                c, t = divmod(i, 16)
                wait(vector, (sT, t4 + i + 1))
                sV.inc(vector.tensor_copy(
                    out=B16[:, t * 1024 + 128 * c: t * 1024 + 128 * (c + 1)],
                    in_=(PTRa if i % 2 == 0 else PTRb)[:]))
    tok_ktok = sV.tok()
    stKT.close()

    DP16 = LNB[:, 12288:14336]
    stDN = ExitStack()
    PDEN = stDN.enter_context(nc.psum_tensor("cf_pden", [16, S], F32))
    t5, v5, s5, g5 = sT.n, sV.n, sS.n, sG.n
    with nc.Block() as blk:
        @blk.gpsimd
        def _(gpsimd):
            wait(gpsimd, tok_qsm, tok_bkc)
            for c in range(NCH):
                if c >= 1:
                    wait(gpsimd, (sT, t5 + c))
                tg = sG.inc(gpsimd.tensor_tensor(
                    out=DP16.rearrange("p (n s) -> p n s", s=64),
                    in0=A16[:, c * S:(c + 1) * S].rearrange("p (n s) -> p n s", s=64),
                    in1=bass.AP(BKC, 33 * c + 1, [[264, 128], [1, 32], [0, 64]]),
                    op=ALU.mult))

        @blk.tensor
        def _(tensor):
            wait(tensor, tok_hid)
            for c in range(NCH):
                wait(tensor, (sG, g5 + c + 1))
                for sb in range(4):
                    mm = tensor.matmul(out=PDEN[0:16, 512 * sb:512 * (sb + 1)],
                                       lhsT=HID[:, 16 * c:16 * (c + 1)],
                                       rhs=DP16[:, 512 * sb:512 * (sb + 1)],
                                       start=(c == 0), stop=(c == NCH - 1))
                sT.inc(mm)

        @blk.vector
        def _(vector):
            wait(vector, (sT, t5 + NCH))
            sV.inc(vector.tensor_scalar(out=DINV, in0=PDEN[0:16, :],
                                        scalar1=1e-6, scalar2=float(math.exp(-10.0)),
                                        op0=ALU.max, op1=ALU.mult))

        @blk.scalar
        def _(scalar):
            wait(scalar, (sV, v5 + 1))
            tl = sS.inc(scalar.activation(out=DINV, in_=DINV, func=AF.Ln))
            wait(scalar, tl)
            sS.inc(scalar.activation(out=DINV, in_=DINV, func=AF.Exp, scale=-1.0,
                                     bias=p.cpool[0:16, 3:4]))
    v6 = sV.n
    with nc.Block() as blk:
        @blk.vector
        def _(vector):
            wait(vector, (sS, s5 + 2))
            sV.inc(vector.tensor_copy(out=DINVB, in_=DINV))
    tok_dinv = sV.tok()
    stDN.close()

    stCT = ExitStack()
    PDV = stCT.enter_context(nc.psum_tensor("cf_pdv", [128, 512], F32))
    CTX = stCT.enter_context(nc.psum_tensor("cf_ctx", [64, 64], F32))
    ATa = stCT.enter_context(nc.psum_tensor("cf_ata", [64, 64], F32))
    ATb = stCT.enter_context(nc.psum_tensor("cf_atb", [64, 64], F32))
    XO = C16
    for c in range(NCH):
        t7, v7 = sT.n, sV.n
        with nc.Block() as blk:
            @blk.tensor
            def _(tensor):
                wait(tensor, tok_dinv, tok_expd)
                if c == 0:
                    wait(tensor, tok_ktok, tok_v)
                for sb in range(4):
                    if sb >= 1:
                        wait(tensor, (sV, v7 + sb))
                    sT.inc(tensor.matmul(out=PDV[:, 0:512],
                                         lhsT=EXPD[0:16, 128 * c:128 * (c + 1)],
                                         rhs=DINVB[0:16, 512 * sb:512 * (sb + 1)],
                                         start=True, stop=True))

            @blk.vector
            def _(vector):
                for sb in range(4):
                    wait(vector, (sT, t7 + sb + 1))
                    sV.inc(vector.tensor_copy(out=DVX[:, 512 * sb:512 * (sb + 1)],
                                              in_=PDV[:, 0:512]))
        tok_dvx = sV.tok()
        for hh in range(2):
            h = 2 * c + hh
            rb = 64 * hh
            tH, vH = sT.n, sV.n
            with nc.Block() as blk:
                @blk.tensor
                def _(tensor):
                    wait(tensor, tok_dvx, (sV, vH + 2))

                    def dctx(n):
                        tile, par = divmod(n, 2)
                        return sT.inc(tensor.matmul(
                            out=CTX[:],
                            lhsT=B16[64 * par:64 * par + 64,
                                     tile * 1024 + 64 * h: tile * 1024 + 64 * (h + 1)],
                            rhs=VTOK[64 * par:64 * par + 64,
                                     tile * 1024 + 64 * h: tile * 1024 + 64 * (h + 1)],
                            start=True, stop=True))
                    dctx(0)
                    for n in range(1, 32):
                        wait(tensor, (sV, vH + 3 * n + 1))
                        sT.inc(tensor.matmul(
                            out=(ATa if n % 2 == 0 else ATb)[:],
                            lhsT=CTXB[rb:rb + 64, :],
                            rhs=A16[rb:rb + 64, c * S + 64 * n: c * S + 64 * (n + 1)],
                            start=True, stop=True))
                        if n <= 30:
                            wait(tensor, (sV, vH + 3 * n))
                            dctx(n)

                @blk.vector
                def _(vector):
                    sV.inc(vector.memset(XO[rb:rb + 64, c * S: c * S + 64], 0.0))
                    sV.inc(vector.memset(CTXR[rb:rb + 64, :], 0.0))
                    for n in range(1, 32):
                        wait(vector, (sT, tH + (1 if n == 1 else 2 * (n - 1) + 1)))
                        ta = sV.inc(vector.tensor_add(out=CTXR[rb:rb + 64, :],
                                                      in0=CTX[:],
                                                      in1=CTXR[rb:rb + 64, :]))
                        wait(vector, ta)
                        sV.inc(vector.tensor_copy(out=CTXB[rb:rb + 64, :],
                                                  in_=CTXR[rb:rb + 64, :]))
                        wait(vector, (sT, tH + 2 * n))
                        sV.inc(vector.scalar_tensor_tensor(
                            out=XO[rb:rb + 64, c * S + 64 * n: c * S + 64 * (n + 1)],
                            in0=(ATa if n % 2 == 0 else ATb)[:],
                            scalar=1.0,
                            in1=DVX[rb:rb + 64, 64 * n:64 * (n + 1)],
                            op0=ALU.mult, op1=ALU.mult))
    tok_ctx = sV.tok()
    stCT.close()

    tok_wo = emit_big_load(p, WOT.rearrange("p (c f) -> p c f", c=NCH),
                           wot.rearrange("(c p) f -> p c f", p=128), cast=True,
                           deps=[tok_ctx])
    tok_xr = emit_restore(p, X, xsp, [tok_ctx])
    stRO = ExitStack()
    RO = stRO.enter_context(nc.psum_tensor("cf_ro", [128, 512], F32))
    RO2 = stRO.enter_context(nc.psum_tensor("cf_ro2", [128, 512], F32))
    tR0, vR0 = sT.n, sV.n
    with nc.Block() as blk:
        @blk.tensor
        def _(tensor):
            wait(tensor, tok_wo, tok_xr)
            for i in range(32):
                oc, sb = divmod(i, 4)
                ROx = RO if i % 2 == 0 else RO2
                if i >= 2:
                    wait(tensor, (sV, vR0 + (i - 2) + 1))
                for pc in range(NCH):
                    mm = tensor.matmul(
                        out=ROx[:, 0:512],
                        lhsT=WOT[:, pc * 1024 + 128 * oc: pc * 1024 + 128 * (oc + 1)],
                        rhs=XO[:, pc * S + 512 * sb: pc * S + 512 * (sb + 1)],
                        start=(pc == 0), stop=(pc == NCH - 1))
                sT.inc(mm)

        @blk.vector
        def _(vector):
            for i in range(32):
                oc, sb = divmod(i, 4)
                wait(vector, (sT, tR0 + i + 1))
                dst = X[:, oc * S + 512 * sb: oc * S + 512 * (sb + 1)]
                sV.inc(vector.scalar_tensor_tensor(
                    out=dst, in0=(RO if i % 2 == 0 else RO2)[:, 0:512],
                    scalar=1.0, in1=dst, op0=ALU.mult, op1=ALU.add))
    tok_done = sV.tok()
    stRO.close()
    stPT.close()
    st2.close()
    return tok_done


def build_l1attn_prog():
    if "l1attn" in _prog_cache:
        return _prog_cache["l1attn"]
    p = Prog()
    xin = p.din("xin", [1024, S])
    xout = p.dout("xout", [1024, S])
    p.init_consts()
    _load_phase_consts(p, need=("idb",))
    X = p.sb("X", [128, NCH * S], F32)
    tok_x = emit_big_load(p, X[:].rearrange("p (c s) -> p c s", c=NCH),
                          xin.rearrange("(c p) s -> p c s", p=128))
    tok = emit_l1_attn(p, X, deps=[tok_x])
    tok_st = emit_store_xT(p, X, xout, [tok])
    emit_final_wait(p, [tok_st])
    nc = p.close()
    _prog_cache["l1attn"] = (nc, p)
    return nc, p


def _prep_l1attn(inputs):
    d = {}
    d["l1c_wqt"] = np.ascontiguousarray(inputs["L1_attn_W_qkv"].T)
    d["l1c_wot"] = np.ascontiguousarray(inputs["L1_attn_W_out"].T)
    hid = np.zeros((128, 8, 16), np.float32)
    expd = np.zeros((16, 8, 128), np.float32)
    for c in range(8):
        for pp in range(128):
            hid[pp, c, 2 * c + pp // 64] = 1.0
            expd[2 * c + pp // 64, c, pp] = 1.0
    d["c_hid16"] = hid.reshape(128, 128).astype(BF)
    d["c_expd"] = expd.reshape(16, 1024).astype(BF)
    return d



def spill_flat(p, buf_ap, dram_ap, deps):
    ld = p.sem("uld%d" % p._ldctr); p._ldctr += 1
    with p.nc.Block() as blk:
        @blk.gpsimd
        def _(gpsimd):
            wait(gpsimd, *deps)
            ld.inc(gpsimd.dma_start(out=dram_ap, in_=buf_ap), 16)
    p.load_toks.append(ld.tok())
    return ld.tok()


def restore_flat(p, buf_ap, dram_ap, deps):
    ld = p.sem("uld%d" % p._ldctr); p._ldctr += 1
    with p.nc.Block() as blk:
        @blk.sync
        def _(sync):
            wait(sync, *deps)
            ld.inc(sync.dma_start(out=buf_ap, in_=dram_ap), 16)
    p.load_toks.append(ld.tok())
    return ld.tok()


def emit_moe_conv(p):
    w1 = p.din("l1f_w1", [8, 1024, 4096])
    w2 = p.din("l1f_w2", [8, 4096, 1024])
    w1b = p.dtmp("l1f_w1b", [8, 1024, 4096], BF16)
    w2b = p.dtmp("l1f_w2b", [8, 4096, 1024], BF16)
    cv = p.sem("moeconv")
    with p.nc.Block() as blk:
        @blk.gpsimd
        def _(gpsimd):
            for e in range(8):
                cv.inc(gpsimd.dma_start(out=w1b[e:e + 1], in_=w1[e:e + 1]), 16)
                cv.inc(gpsimd.dma_start(out=w2b[e:e + 1], in_=w2[e:e + 1]), 16)
    tok = cv.tok()
    p.load_toks.append(tok)
    return w1b, w2b, tok


def emit_l1_ffn(p, X, deps, y_out, moe_conv=None):
    nc = p.nc
    sV, sT, sS, sG = p.sem("v"), p.sem("t"), p.sem("s"), p.sem("g")
    st2 = ExitStack()
    sbuf = lambda n, sh, dt: st2.enter_context(nc.sbuf_tensor(n, sh, dt))

    wqkv = p.din("l1f_wqkv", [1024, 3072])
    wo = p.din("l1f_wo", [1024, 1024])
    ln1w = p.din("l1f_ln1w", [128, 8])
    ln1b = p.din("l1f_ln1b", [128, 8])
    ln2w = p.din("l1f_ln2w", [128, 8])
    ln2b = p.din("l1f_ln2b", [128, 8])
    bqkv = p.din("l1f_bqkv", [128, 24])
    bqv = p.din("l1f_bqv", [1, 1024])
    bo = p.din("l1f_bo", [128, 8])
    wg = p.din("l1f_wgate", [128, 64])
    b1c = p.din("l1f_b1c", [128, 256])
    b2r = p.din("l1f_b2r", [1, 8192])
    if moe_conv is None:
        moe_conv = emit_moe_conv(p)
    w1b, w2b, tok_conv = moe_conv
    xsp = p.dtmp("l1f_xsp", [128, NCH * S])
    tsp = p.dtmp("l1f_tsp", [128, NCH * S], BF16)
    hsp = p.dtmp("l1f_hsp", [128, NCH * S], BF16)

    LNB = sbuf("df_lnb", [128, NCH * S], BF16)
    LN1B = sbuf("df_ln1b", [128, NCH * S], BF16)
    A16 = sbuf("df_a16", [128, NCH * S], BF16)
    B16 = sbuf("df_b16", [128, NCH * S], BF16)
    LNP = sbuf("df_lnp", [128, 32], F32)
    BQ = sbuf("df_bq", [128, 32], F32)
    BO = sbuf("df_bo", [128, 8], F32)
    WG16 = sbuf("df_wg", [128, 64], BF16)
    WE = sbuf("df_we", [128, 128], F32)
    GT = sbuf("df_gt", [128, 128], F32)
    GM = sbuf("df_gm", [128, 48], F32)
    GMK = sbuf("df_gmk", [128, 128], F32)
    B1C = sbuf("df_b1c", [128, 256], F32)
    B2RE = sbuf("df_b2re", [1, 1024], BF16)
    TMX = sbuf("df_tmx", [128, 65], F32)

    XB = X[:].bitcast(BF16)
    XF = X[:]
    VTOK = XB
    WQ = XB[:, 16384:24576]
    BQVR = XB[0:1, 28672:29696]
    W2G = [XB[:, 16384 + 4096 * j:16384 + 4096 * (j + 1)] for j in range(3)]
    CTXF = LNB[:]
    LNBF32 = LNB[:].bitcast(F32)
    L1F = LN1B[:].bitcast(F32)
    L1B = LN1B[:]
    EXPT = L1B[:, 0:1024]
    BCS = L1F[:, 512:1024]
    SR = L1F[0:1, 1024:1536]
    RLN = L1F[0:1, 1536:2048]
    ISR = L1F[0:1, 2048:2560]
    NMB16 = L1F[:, 2592:2608]
    MROW = L1F[0:1, 2608:2624]
    WOTMP = L1B[:, 4096:5120]
    T1 = LN1B[:]
    HBS = LNB[:]
    ACC = XB[:, 0:16384]
    HR = LNB[:]
    A16F = A16[:].bitcast(F32)
    B16F = B16[:].bitcast(F32)
    OUTSa = L1F
    OUTSb = LNBF32

    tok_l1w = emit_big_load(p, LNP[:, 0:8], ln1w, deps=deps)
    tok_l1b = emit_big_load(p, LNP[:, 8:16], ln1b, deps=deps)
    tok_l2w = emit_big_load(p, LNP[:, 16:24], ln2w, deps=deps)
    tok_l2b = emit_big_load(p, LNP[:, 24:32], ln2b, deps=deps)
    tok_bq = emit_big_load(p, BQ[:, 0:24], bqkv, deps=deps)
    tok_bqv = None
    tok_bo = emit_big_load(p, BO[:], bo, deps=deps)
    tok_wg = emit_big_load(p, WG16[:], wg, cast=True, deps=deps)
    tok_b1c = emit_big_load(p, B1C[:], b1c, deps=deps)

    ln_scr1 = (A16[:, 0:1024], B16F[0:1, 0:6144], A16F[:, 1024:3072],
               A16F[:, 3072:5120], A16[:, 10240:11264])
    tok_t0 = emit_layernorm(p, X, LNB, "df_ln3", deps=deps, scratch=ln_scr1)
    tok_xspill = spill_flat(p, XF, xsp, [tok_t0])
    tok_bqv = emit_big_load(p, BQVR, bqv, cast=True, deps=[tok_xspill])
    tok_ln1 = emit_layernorm(p, LNB, LN1B, "df_ln1",
                             deps=[tok_t0, tok_l1w, tok_l1b],
                             affine=(LNP[:, 0:8], LNP[:, 8:16]), scratch=ln_scr1)
    tok_tspill = spill_flat(p, LNB[:], tsp, [tok_ln1])

    with nc.Block() as blk:
        @blk.vector
        def _(vector):
            wait(vector, tok_bq)
            sV.inc(vector.tensor_scalar_mul(out=BQ[:, 24:32], in0=BQ[:, 0:8],
                                            scalar1=0.125))
    tok_bqs = sV.tok()

    stPT = ExitStack()
    PT = stPT.enter_context(nc.psum_tensor("df_pt", [128, 512], F32))
    PT2 = stPT.enter_context(nc.psum_tensor("df_pt2", [128, 512], F32))

    def project(col_off, kind, prior_tok):
        tok_w = emit_big_load(p, WQ.rearrange("p (c f) -> p c f", c=NCH),
                              wqkv[:, col_off:col_off + 1024].rearrange("(c p) f -> p c f", p=128),
                              cast=True, deps=[tok_xspill] + ([prior_tok] if prior_tok else []))
        t0_, s0_ = sT.n, sS.n
        with nc.Block() as blk:
            @blk.tensor
            def _(tensor):
                wait(tensor, tok_ln1, tok_w, tok_bqs, tok_bqv, p.consts_ready)
                for i in range(32):
                    PTx = PT if i % 2 == 0 else PT2
                    if i >= 2:
                        wait(tensor, (sS, s0_ + (i - 2) + 1))
                    if kind != "v":
                        oc, sb = divmod(i, 4)
                        for ec in range(NCH):
                            mm = tensor.matmul(
                                out=PTx[:, 0:512],
                                lhsT=WQ[:, ec * 1024 + 128 * oc: ec * 1024 + 128 * (oc + 1)],
                                rhs=LN1B[:, ec * S + 512 * sb: ec * S + 512 * (sb + 1)],
                                start=(ec == 0), stop=(ec == NCH - 1))
                        sT.inc(mm)
                    else:
                        t, half = divmod(i, 2)
                        for ec in range(NCH):
                            tensor.matmul(
                                out=PTx[:, 0:512],
                                lhsT=LN1B[:, ec * S + 128 * t: ec * S + 128 * (t + 1)],
                                rhs=WQ[:, ec * 1024 + 512 * half: ec * 1024 + 512 * (half + 1)],
                                start=(ec == 0), stop=False)
                        mm = tensor.matmul(
                            out=PTx[:, 0:512],
                            lhsT=p.ones_row_bf16[0:1, 0:128],
                            rhs=BQVR[0:1, 512 * half:512 * (half + 1)],
                            start=False, stop=True)
                        sT.inc(mm)

            @blk.scalar
            def _(scalar):
                for i in range(32):
                    wait(scalar, (sT, t0_ + i + 1))
                    src_ = (PT if i % 2 == 0 else PT2)[:, 0:512]
                    if kind == "q":
                        oc, sb = divmod(i, 4)
                        sS.inc(scalar.activation(
                            out=A16[:, oc * S + 512 * sb: oc * S + 512 * (sb + 1)],
                            in_=src_, func=AF.Identity, scale=0.125,
                            bias=BQ[:, 24 + oc:25 + oc]))
                    elif kind == "k":
                        oc, sb = divmod(i, 4)
                        sS.inc(scalar.activation(
                            out=B16[:, oc * S + 512 * sb: oc * S + 512 * (sb + 1)],
                            in_=src_, func=AF.Identity, bias=BQ[:, 8 + oc:9 + oc]))
                    else:
                        t, half = divmod(i, 2)
                        sS.inc(scalar.activation(
                            out=VTOK[:, t * 1024 + 512 * half: t * 1024 + 512 * (half + 1)],
                            in_=src_, func=AF.Identity))
        return sS.tok()

    with nc.named_scope("d_qkv"):
        tok_q = project(0, "q", None)
        tok_k = project(1024, "k", tok_q)
        tok_v = project(2048, "v", tok_k)
    stPT.close()

    stMX = ExitStack()
    _scope_mx = nc.enter_named_scope("d_max", False)
    STMa = stMX.enter_context(nc.psum_tensor("df_stma", [128, 512], F32))
    STMb = stMX.enter_context(nc.psum_tensor("df_stmb", [128, 512], F32))
    PM1 = stMX.enter_context(nc.psum_tensor("df_pm1", [1, 128], F32))
    PNB = stMX.enter_context(nc.psum_tensor("df_pnb", [128, 16], F32))
    for h in range(16):
        c, rb = h // 2, 64 * (h % 2)
        tH, vH = sT.n, sV.n
        with nc.Block() as blk:
            @blk.tensor
            def _(tensor):
                if h == 0:
                    wait(tensor, tok_q, tok_k, p.tok_idf)
                else:
                    wait(tensor, (sV, vH))
                for i in range(64):
                    qb, kk = divmod(i, 16)
                    if i >= 2:
                        wait(tensor, (sV, vH + (i - 2) + 1))
                    sT.inc(tensor.matmul(
                        out=(STMa if i % 2 == 0 else STMb)[:, 0:512],
                        lhsT=B16[rb:rb + 64, c * S + 128 * kk: c * S + 128 * (kk + 1)],
                        rhs=A16[rb:rb + 64, c * S + 512 * qb: c * S + 512 * (qb + 1)],
                        start=True, stop=True))

            @blk.vector
            def _(vector):
                wait(vector, (sT, tH))
                for i in range(64):
                    wait(vector, (sT, tH + i + 1))
                    sV.inc(vector.tensor_reduce(
                        out=TMX[:, i:i + 1],
                        in_=(STMa if i % 2 == 0 else STMb)[:, 0:512],
                        axis=AX.X, op=ALU.max))
        v2_ = sV.n
        with nc.Block() as blk:
            @blk.vector
            def _(vector):
                wait(vector, (sV, v2_))
                sV.inc(vector.tensor_reduce(out=TMX[:, 64:65], in_=TMX[:, 0:64],
                                            axis=AX.X, op=ALU.max))

            @blk.tensor
            def _(tensor):
                wait(tensor, (sV, v2_ + 1))
                sT.inc(tensor.transpose(out=PM1[0:1, :], in_=TMX[:, 64:65],
                                        identity=p.IDF[:]))
        v3_, t3_ = sV.n, sT.n
        with nc.Block() as blk:
            @blk.vector
            def _(vector):
                wait(vector, (sT, t3_))
                sV.inc(vector.tensor_reduce(out=MROW[0:1, h:h + 1], in_=PM1[0:1, :],
                                            axis=AX.X, op=ALU.max, negate=True))

            @blk.tensor
            def _(tensor):
                wait(tensor, (sV, v3_ + 1))
                sT.inc(tensor.matmul(out=PNB[:, h:h + 1], lhsT=p.ones_row_f[:],
                                     rhs=MROW[0:1, h:h + 1], start=True, stop=True))
    tN_ = sT.n
    with nc.Block() as blk:
        @blk.scalar
        def _(scalar):
            wait(scalar, (sT, tN_))
            sS.inc(scalar.activation(out=NMB16, in_=PNB[:], func=AF.Copy))
    tok_nmb = sS.tok()
    stMX.close()
    nc.leave_named_scope("d_max", _scope_mx[0], False)

    stAT = ExitStack()
    _sc = nc.enter_named_scope("d_attnB", False)
    OT = stAT.enter_context(nc.psum_tensor("df_ot", [64, S], F32))
    STa = stAT.enter_context(nc.psum_tensor("df_sta", [128, 512], F32))
    STb = stAT.enter_context(nc.psum_tensor("df_stb", [128, 512], F32))
    SIG = stAT.enter_context(nc.psum_tensor("df_sig", [1, 512], F32))
    BCp = stAT.enter_context(nc.psum_tensor("df_bcp", [128, 512], F32))
    t0a, s0a, v0a = sT.n, sS.n, sV.n

    def t_s(h, qb, k):
        return t0a + (h * 4 + qb) * 33 + (1 if k == 0 else 2 * k)

    def t_sig(h, qb, k):
        return t0a + (h * 4 + qb) * 33 + (2 * k + 3 if k < 15 else 32)

    def t_bc(h, qb):
        return t0a + (h * 4 + qb) * 33 + 33

    def s_exp(h, qb, k):
        return s0a + (h * 4 + qb) * 18 + k + 1

    def s_row(h, qb, j):
        return s0a + (h * 4 + qb) * 18 + 16 + j

    def v_sig(h, qb):
        return v0a + (h * 4 + qb) * 3 + 1

    def v_bcs(h, qb):
        return v0a + (h * 4 + qb) * 3 + 2

    def v_ot(h, qb):
        return v0a + (h * 4 + qb) * 3 + 3

    with nc.Block() as blk:
        @blk.tensor
        def _(tensor):
            wait(tensor, tok_nmb, tok_v, tok_tspill)
            for h in range(16):
                c, rb = h // 2, 64 * (h % 2)
                for qb in range(4):
                    idx = h * 4 + qb
                    if idx > 0:
                        pv = (h, qb - 1) if qb else (h - 1, 3)
                        wait(tensor, (sV, v_sig(*pv)))
                    if qb == 0 and h > 0:
                        wait(tensor, (sV, v_ot(h - 1, 3)))
                    for k in range(17):
                        if k < 16:
                            g_ = idx * 16 + k
                            if g_ >= 2:
                                i2, k2 = divmod(g_ - 2, 16)
                                wait(tensor, (sS, s_exp(i2 // 4, i2 % 4, k2)))
                            sT.inc(tensor.matmul(
                                out=(STa if k % 2 == 0 else STb)[:, 0:512],
                                lhsT=B16[rb:rb + 64, c * S + 128 * k: c * S + 128 * (k + 1)],
                                rhs=A16[rb:rb + 64, c * S + 512 * qb: c * S + 512 * (qb + 1)],
                                start=True, stop=True))
                        if k >= 1:
                            kk = k - 1
                            wait(tensor, (sS, s_exp(h, qb, kk)))
                            tensor.matmul(
                                out=OT[0:64, 512 * qb:512 * (qb + 1)],
                                lhsT=VTOK[:, kk * 1024 + 64 * h: kk * 1024 + 64 * (h + 1)],
                                rhs=EXPT[:, (kk % 2) * 512:(kk % 2) * 512 + 512],
                                start=(kk == 0), stop=(kk == 15))
                            sT.inc(tensor.matmul(
                                out=SIG[0:1, 0:512],
                                lhsT=p.ones_bf[:],
                                rhs=EXPT[:, (kk % 2) * 512:(kk % 2) * 512 + 512],
                                start=(kk == 0), stop=(kk == 15)))
                    wait(tensor, (sS, s_row(h, qb, 2)))
                    if idx > 0:
                        pv = (h, qb - 1) if qb else (h - 1, 3)
                        wait(tensor, (sV, v_bcs(*pv)))
                    sT.inc(tensor.matmul(out=BCp[:, 0:512], lhsT=p.ones_row_f[:],
                                         rhs=ISR[0:1, 0:512], start=True, stop=True))

        @blk.scalar
        def _(scalar):
            for h in range(16):
                for qb in range(4):
                    for k in range(16):
                        wait(scalar, (sT, t_s(h, qb, k)))
                        g_ = (h * 4 + qb) * 16 + k
                        if g_ >= 2:
                            i2, k2 = divmod(g_ - 2, 16)
                            wait(scalar, (sT, t_sig(i2 // 4, i2 % 4, k2)))
                        sS.inc(scalar.activation(
                            out=EXPT[:, (k % 2) * 512:(k % 2) * 512 + 512],
                            in_=(STa if k % 2 == 0 else STb)[:, 0:512],
                            func=AF.Exp, bias=NMB16[:, h:h + 1]))
                    wait(scalar, (sV, v_sig(h, qb)))
                    if h * 4 + qb > 0:
                        pv = (h, qb - 1) if qb else (h - 1, 3)
                        wait(scalar, (sT, t_bc(*pv)))
                    t1 = sS.inc(scalar.activation(out=RLN[0:1, 0:512], in_=SR[0:1, 0:512],
                                                  func=AF.Ln))
                    wait(scalar, t1)
                    sS.inc(scalar.activation(out=ISR[0:1, 0:512], in_=RLN[0:1, 0:512],
                                             func=AF.Exp, scale=-1.0,
                                             bias=p.cpool[0:1, 1:2]))

        @blk.vector
        def _(vector):
            for h in range(16):
                c, rb = h // 2, 64 * (h % 2)
                for qb in range(4):
                    wait(vector, (sT, t_sig(h, qb, 15)))
                    if h * 4 + qb > 0:
                        pv = (h, qb - 1) if qb else (h - 1, 3)
                        wait(vector, (sS, s_row(*pv, 1)))
                    sV.inc(vector.tensor_scalar_mul(out=SR[0:1, 0:512],
                                                    in0=SIG[0:1, 0:512],
                                                    scalar1=float(math.exp(24.0))))
                    wait(vector, (sT, t_bc(h, qb)))
                    tb = sV.inc(vector.tensor_copy(out=BCS[:, 0:512], in_=BCp[:, 0:512]))
                    wait(vector, tb)
                    sV.inc(vector.scalar_tensor_tensor(
                        out=CTXF[rb:rb + 64, c * S + 512 * qb: c * S + 512 * (qb + 1)],
                        in0=OT[0:64, 512 * qb:512 * (qb + 1)],
                        scalar=1.0, in1=BCS[rb:rb + 64, 0:512],
                        op0=ALU.mult, op1=ALU.mult))
    tok_attn = sV.tok()
    stAT.close()
    nc.leave_named_scope("d_attnB", _sc[0], False)

    tok_wo = emit_big_load(p, WQ.rearrange("p (c f) -> p c f", c=NCH),
                           wo.rearrange("(c p) f -> p c f", p=128), cast=True,
                           deps=[tok_attn])
    tok_t0r = restore_flat(p, A16[:], tsp, [tok_attn])
    stWO = ExitStack()
    RO = stWO.enter_context(nc.psum_tensor("df_ro", [128, 512], F32))
    RO2 = stWO.enter_context(nc.psum_tensor("df_ro2", [128, 512], F32))
    tW0, vW0, sW0 = sT.n, sV.n, sS.n
    with nc.Block() as blk:
        @blk.tensor
        def _(tensor):
            wait(tensor, tok_wo, tok_t0r, tok_bo)
            for i in range(32):
                oc, sb = divmod(i, 4)
                ROx = RO if i % 2 == 0 else RO2
                if i >= 2:
                    wait(tensor, (sV, vW0 + (i - 2) + 1))
                for pc in range(NCH):
                    mm = tensor.matmul(
                        out=ROx[:, 0:512],
                        lhsT=WQ[:, pc * 1024 + 128 * oc: pc * 1024 + 128 * (oc + 1)],
                        rhs=CTXF[:, pc * S + 512 * sb: pc * S + 512 * (sb + 1)],
                        start=(pc == 0), stop=(pc == NCH - 1))
                sT.inc(mm)

        @blk.scalar
        def _(scalar):
            for i in range(32):
                oc, sb = divmod(i, 4)
                wait(scalar, (sT, tW0 + i + 1))
                sS.inc(scalar.activation(
                    out=WOTMP[:, (i % 2) * 512:(i % 2) * 512 + 512],
                    in_=(RO if i % 2 == 0 else RO2)[:, 0:512],
                    func=AF.Identity, bias=BO[:, oc:oc + 1]))

        @blk.vector
        def _(vector):
            for i in range(32):
                oc, sb = divmod(i, 4)
                wait(vector, (sS, sW0 + i + 1))
                sV.inc(vector.tensor_add(
                    out=B16[:, oc * S + 512 * sb: oc * S + 512 * (sb + 1)],
                    in0=WOTMP[:, (i % 2) * 512:(i % 2) * 512 + 512],
                    in1=A16[:, oc * S + 512 * sb: oc * S + 512 * (sb + 1)]))
    tok_h = sV.tok()
    stWO.close()

    ln_scr2 = (A16[:, 0:1024], LNBF32[0:1, 0:6144], A16F[:, 1024:3072],
               A16F[:, 3072:5120], A16[:, 10240:11264])
    tok_ln2 = emit_layernorm(p, B16, LN1B, "df_ln2",
                             deps=[tok_h, tok_l2w, tok_l2b],
                             affine=(LNP[:, 16:24], LNP[:, 24:32]), scratch=ln_scr2)
    tok_hspill = spill_flat(p, B16[:], hsp, [tok_ln2])

    stPG = ExitStack()
    PG = stPG.enter_context(nc.psum_tensor("df_pg", [128, 128], F32))
    tG0 = sT.n
    with nc.Block() as blk:
        @blk.tensor
        def _(tensor):
            wait(tensor, tok_ln2, tok_wg)
            for t in range(16):
                for dc in range(NCH):
                    mm = tensor.matmul(out=PG[:, 8 * t:8 * (t + 1)],
                                       lhsT=LN1B[:, dc * S + 128 * t: dc * S + 128 * (t + 1)],
                                       rhs=WG16[:, 8 * dc:8 * (dc + 1)],
                                       start=(dc == 0), stop=(dc == NCH - 1))
            sT.inc(mm)

        @blk.vector
        def _(vector):
            wait(vector, (sT, tG0 + 1))
            r3 = lambda buf: buf.rearrange("p (t e) -> p t e", t=16)
            gm3 = lambda off: bass.AP(GM, off, [[48, 128], [1, 16], [0, 8]])
            t1 = sV.inc(vector.tensor_copy(out=GT[:], in_=PG[:]))
            wait(vector, t1)
            t2 = sV.inc(vector.tensor_reduce(out=GM[:, 0:16], in_=r3(GT[:]),
                                             axis=AX.X, op=ALU.max))
            wait(vector, t2)
            t3 = sV.inc(vector.tensor_tensor(out=r3(GMK[:]), in0=r3(GT[:]),
                                             in1=gm3(0), op=ALU.is_ge))
            wait(vector, t3)
            t4 = sV.inc(vector.tensor_scalar_mul(out=WE[:], in0=GMK[:], scalar1=1e30))
            wait(vector, t4)
            t5 = sV.inc(vector.tensor_sub(out=WE[:], in0=GT[:], in1=WE[:]))
            wait(vector, t5)
            t6 = sV.inc(vector.tensor_reduce(out=GM[:, 16:32], in_=r3(WE[:]),
                                             axis=AX.X, op=ALU.max))
            wait(vector, t6)
            t7 = sV.inc(vector.tensor_tensor(out=r3(WE[:]), in0=r3(WE[:]),
                                             in1=gm3(16), op=ALU.is_ge))
            wait(vector, t7)
            t9 = sV.inc(vector.tensor_add(out=GMK[:], in0=GMK[:], in1=WE[:]))
            wait(vector, t9)
            t10 = sV.inc(vector.tensor_scalar_mul(out=GM[:, 32:48], in0=GM[:, 0:16],
                                                  scalar1=-1.0))
            wait(vector, t10)
            sV.inc(vector.tensor_tensor(out=r3(GT[:]), in0=r3(GT[:]),
                                        in1=gm3(32), op=ALU.add))
        vG1 = sV.n

        @blk.scalar
        def _(scalar):
            wait(scalar, (sV, vG1))
            sS.inc(scalar.activation(out=GT[:], in_=GT[:], func=AF.Exp))
        sG1 = sS.n
    with nc.Block() as blk:
        @blk.vector
        def _(vector):
            wait(vector, (sS, sG1))
            r3 = lambda buf: buf.rearrange("p (t e) -> p t e", t=16)
            gm3 = lambda off: bass.AP(GM, off, [[48, 128], [1, 16], [0, 8]])
            t1 = sV.inc(vector.tensor_mul(out=WE[:], in0=GT[:], in1=GMK[:]))
            wait(vector, t1)
            t2 = sV.inc(vector.tensor_reduce(out=GM[:, 0:16], in_=r3(WE[:]),
                                             axis=AX.X, op=ALU.add))
            wait(vector, t2)
            t3 = sV.inc(vector.reciprocal(out=GM[:, 0:16], in_=GM[:, 0:16]))
            wait(vector, t3)
            sV.inc(vector.tensor_tensor(out=r3(WE[:]), in0=r3(WE[:]), in1=gm3(0),
                                        op=ALU.mult))
    tok_we = sV.tok()
    stPG.close()

    def w1slice(dc, hc):
        buf = A16 if dc < 4 else B16
        return buf[:, (dc % 4) * 4096 + 128 * hc: (dc % 4) * 4096 + 128 * (hc + 1)]

    _sc2 = nc.enter_named_scope("d_moe", False)
    W1Rtoks = [None] * 8
    W1Rtoks[0] = [
        emit_big_load(p, A16[:].rearrange("p (c f) -> p c f", c=4),
                      w1b[0:1, 0:512, :].rearrange("o (c p) f -> p (o c) f", p=128),
                      deps=[tok_hspill, tok_we, tok_conv]),
        emit_big_load(p, B16[:].rearrange("p (c f) -> p c f", c=4),
                      w1b[0:1, 512:1024, :].rearrange("o (c p) f -> p (o c) f", p=128),
                      deps=[tok_hspill, tok_we, tok_conv])]
    acc_tok = {}
    prev_out_tok = None
    prev_g_toks = None
    prev_g_toks8 = None
    prev_flush_tok = None
    tok_b2e = None
    for e in range(8):
        b2dep = [tok_hspill] if e == 0 else [prev_out_tok]
        tok_b2e = emit_big_load(p, B2RE[:], b2r[0:1, 1024 * e:1024 * (e + 1)],
                                cast=True, deps=b2dep)
        for sg in range(4):
            stPH = ExitStack()
            PHa = stPH.enter_context(nc.psum_tensor("df_pha_%d_%d" % (e, sg), [128, 512], F32))
            PHb = stPH.enter_context(nc.psum_tensor("df_phb_%d_%d" % (e, sg), [128, 512], F32))
            tP0, sP0 = sT.n, sS.n
            with nc.Block() as blk:
                @blk.tensor
                def _(tensor):
                    wait(tensor, *W1Rtoks[e])
                    for hc in range(32):
                        PHx = PHa if hc % 2 == 0 else PHb
                        if hc >= 2:
                            wait(tensor, (sS, sP0 + (hc - 2) + 1))
                        for dc in range(NCH):
                            mm = tensor.matmul(
                                out=PHx[:, 0:512],
                                lhsT=w1slice(dc, hc),
                                rhs=T1[:, dc * S + 512 * sg: dc * S + 512 * (sg + 1)],
                                start=(dc == 0), stop=(dc == NCH - 1))
                        sT.inc(mm)

                @blk.scalar
                def _(scalar):
                    wait(scalar, tok_b1c)
                    for hc in range(32):
                        wait(scalar, (sT, tP0 + hc + 1))
                        if prev_g_toks is not None:
                            wait(scalar, prev_g_toks[hc // 4])
                        sS.inc(scalar.activation(
                            out=HBS[:, 512 * hc:512 * (hc + 1)],
                            in_=(PHa if hc % 2 == 0 else PHb)[:, 0:512],
                            func=AF.Gelu, bias=B1C[:, 32 * e + hc:32 * e + hc + 1]))
            tok_gelu = sS.tok()
            stPH.close()

            if sg == 3 and e < 7:
                W1Rtoks[e + 1] = [
                    emit_big_load(p, A16[:].rearrange("p (c f) -> p c f", c=4),
                                  w1b[e + 1:e + 2, 0:512, :].rearrange("o (c p) f -> p (o c) f", p=128),
                                  deps=[tok_gelu]),
                    emit_big_load(p, B16[:].rearrange("p (c f) -> p c f", c=4),
                                  w1b[e + 1:e + 2, 512:1024, :].rearrange("o (c p) f -> p (o c) f", p=128),
                                  deps=[tok_gelu])]

            stPO = ExitStack()
            PO = [stPO.enter_context(nc.psum_tensor("df_po_%d_%d_%d" % (e, sg, j),
                                                    [128, 512], F32))
                  for j in range(8)]
            tO0 = sT.n
            sGel0 = tok_gelu[1] - 32
            g_toks = []
            for g in range(8):
                dstw = W2G[g % 3]
                if g >= 3:
                    deps_w2 = [(sT, tO0 + 32 * (g - 2))]
                elif prev_g_toks8 is not None:
                    deps_w2 = [prev_g_toks8[g + 5], tok_conv]
                else:
                    deps_w2 = [tok_conv]
                tok_w2 = emit_big_load(p, dstw.rearrange("p (c f) -> p c f", c=4),
                                       w2b[e:e + 1, 512 * g:512 * (g + 1), :]
                                       .rearrange("o (c p) f -> p (o c) f", p=128),
                                       deps=deps_w2,
                                       sem=p.sem("w2s_%d" % (g % 3)))
                with nc.Block() as blk:
                    @blk.tensor
                    def _(tensor):
                        wait(tensor, tok_w2)
                        wait(tensor, (tok_gelu[0], sGel0 + 4 * g + 4))
                        if g == 0:
                            wait(tensor, tok_b2e)
                            if prev_flush_tok is not None:
                                wait(tensor, prev_flush_tok)
                        for hc in range(4 * g, 4 * g + 4):
                            for j in range(8):
                                sblk, oh = divmod(j, 2)
                                sT.inc(tensor.matmul(
                                    out=PO[j][:, 0:512],
                                    lhsT=HBS[:, 512 * hc + 128 * sblk:
                                             512 * hc + 128 * (sblk + 1)],
                                    rhs=dstw[:, (hc - 4 * g) * 1024 + 512 * oh:
                                             (hc - 4 * g) * 1024 + 512 * (oh + 1)],
                                    start=(hc == 0), stop=False))
                g_toks.append((p.sems["t"], tO0 + 32 * (g + 1)))
            tB0, vB0 = sT.n, sV.n
            with nc.Block() as blk:
                @blk.tensor
                def _(tensor):
                    for j in range(8):
                        sblk, oh = divmod(j, 2)
                        sT.inc(tensor.matmul(
                            out=PO[j][:, 0:512],
                            lhsT=p.ones_row_bf16[0:1, 0:128],
                            rhs=B2RE[0:1, 512 * oh: 512 * (oh + 1)],
                            start=False, stop=True))

                @blk.vector
                def _(vector):
                    if e == 0 and sg == 0:
                        wait(vector, tok_we)
                    for j in range(8):
                        sblk, oh = divmod(j, 2)
                        t = 4 * sg + sblk
                        wait(vector, (sT, tB0 + j + 1))
                        dst = ACC[:, 1024 * t + 512 * oh: 1024 * t + 512 * (oh + 1)]
                        wecol = WE[:, 8 * t + e:8 * t + e + 1]
                        if e == 0:
                            sV.inc(vector.tensor_scalar_mul(out=dst, in0=PO[j][:, 0:512],
                                                            scalar1=wecol))
                        else:
                            wait(vector, acc_tok[(t, oh)])
                            sV.inc(vector.scalar_tensor_tensor(
                                out=dst, in0=PO[j][:, 0:512], scalar=wecol,
                                in1=dst, op0=ALU.mult, op1=ALU.add))
                        acc_tok[(t, oh)] = sV.tok()
            prev_out_tok = sT.tok()
            prev_g_toks = g_toks
            prev_g_toks8 = g_toks
            prev_flush_tok = sV.tok()
            stPO.close()
    tok_moe = prev_flush_tok
    nc.leave_named_scope("d_moe", _sc2[0], False)

    tok_xr1 = restore_flat(p, A16F, xsp[:, 0:8192], [tok_moe])
    tok_xr2 = restore_flat(p, B16F, xsp[:, 8192:16384], [tok_moe])
    tok_hr = restore_flat(p, HR, hsp, [tok_moe])
    stFN = ExitStack()
    PFa = stFN.enter_context(nc.psum_tensor("df_pfa", [128, 128], F32))
    PFb = stFN.enter_context(nc.psum_tensor("df_pfb", [128, 128], F32))
    vF0 = sV.n
    with nc.Block() as blk:
        @blk.vector
        def _(vector):
            wait(vector, tok_xr1, tok_xr2, tok_hr)
            for c in range(NCH):
                xr = A16F if c < 4 else B16F
                sl = xr[:, (c % 4) * 2048:(c % 4) * 2048 + 2048]
                sV.inc(vector.tensor_add(out=sl, in0=sl,
                                         in1=HR[:, c * S:(c + 1) * S]))
    vF1 = sV.n
    tF0 = sT.n
    with nc.Block() as blk:
        @blk.tensor
        def _(tensor):
            wait(tensor, (sV, vF1), p.tok_idf)
            for i in range(128):
                t, c = divmod(i, 8)
                xr = A16F if c < 4 else B16F
                if i >= 2:
                    wait(tensor, (sV, vF1 + (i - 2) + 1))
                sT.inc(tensor.transpose(
                    out=(PFa if i % 2 == 0 else PFb)[:],
                    in_=xr[:, (c % 4) * 2048 + 128 * t: (c % 4) * 2048 + 128 * (t + 1)],
                    identity=p.IDF[:]))

        @blk.vector
        def _(vector):
            for i in range(128):
                t, c = divmod(i, 8)
                OUTS = OUTSa if t < 8 else OUTSb
                wait(vector, (sT, tF0 + i + 1))
                sV.inc(vector.scalar_tensor_tensor(
                    out=OUTS[:, 1024 * (t % 8) + 128 * c: 1024 * (t % 8) + 128 * (c + 1)],
                    in0=(PFa if i % 2 == 0 else PFb)[:],
                    scalar=1.0,
                    in1=ACC[:, 1024 * t + 128 * c: 1024 * t + 128 * (c + 1)],
                    op0=ALU.mult, op1=ALU.add))
    vF2 = sV.n
    stFN.close()
    ldy1 = p.sem("uld%d" % p._ldctr); p._ldctr += 1
    ldy2 = p.sem("uld%d" % p._ldctr); p._ldctr += 1
    with nc.Block() as blk:
        @blk.gpsimd
        def _(gpsimd):
            wait(gpsimd, (sV, vF1 + 64))
            ldy1.inc(gpsimd.dma_start(
                out=y_out[0:1024, :].rearrange("(t p) f -> p t f", p=128),
                in_=OUTSa[:, 0:8192].rearrange("p (t f) -> p t f", t=8)), 16)
            wait(gpsimd, (sV, vF2))
            ldy2.inc(gpsimd.dma_start(
                out=y_out[1024:2048, :].rearrange("(t p) f -> p t f", p=128),
                in_=OUTSb[:].rearrange("p (t f) -> p t f", t=8)), 16)
    p.load_toks.append(ldy1.tok())
    p.load_toks.append(ldy2.tok())
    st2.close()
    return [ldy1.tok(), ldy2.tok()]


def build_l1ffn_prog():
    if "l1ffn" in _prog_cache:
        return _prog_cache["l1ffn"]
    p = Prog()
    xin = p.din("xin", [1024, S])
    y = p.dout("y", [S, 1024])
    p.init_consts()
    _load_phase_consts(p, need=("idf",))
    X = p.sb("X", [128, NCH * S], F32)
    tok_x = emit_big_load(p, X[:].rearrange("p (c s) -> p c s", c=NCH),
                          xin.rearrange("(c p) s -> p c s", p=128))
    toks = emit_l1_ffn(p, X, deps=[tok_x], y_out=y)
    emit_final_wait(p, toks)
    nc = p.close()
    _prog_cache["l1ffn"] = (nc, p)
    return nc, p


def _prep_l1ffn(inputs):
    d = {}
    cols = lambda v, k: np.ascontiguousarray(v.reshape(k, 128).T)
    d["l1f_wqkv"] = inputs["L1_ffn_w_qkv"]
    d["l1f_wo"] = inputs["L1_ffn_w_o"]
    d["l1f_ln1w"] = cols(inputs["L1_ffn_ln1_w"], 8)
    d["l1f_ln1b"] = cols(inputs["L1_ffn_ln1_b"], 8)
    d["l1f_ln2w"] = cols(inputs["L1_ffn_ln2_w"], 8)
    d["l1f_ln2b"] = cols(inputs["L1_ffn_ln2_b"], 8)
    d["l1f_bqkv"] = cols(inputs["L1_ffn_b_qkv"], 24)
    d["l1f_bqv"] = np.ascontiguousarray(inputs["L1_ffn_b_qkv"][2048:3072].reshape(1, 1024))
    d["l1f_bo"] = cols(inputs["L1_ffn_b_o"], 8)
    d["l1f_wgate"] = np.ascontiguousarray(
        inputs["L1_ffn_w_gate"].reshape(8, 128, 8).transpose(1, 0, 2).reshape(128, 64))
    d["l1f_w1"] = inputs["L1_ffn_w1"]
    d["l1f_b1c"] = np.ascontiguousarray(
        inputs["L1_ffn_b1"].reshape(8, 32, 128).transpose(2, 0, 1).reshape(128, 256))
    d["l1f_w2"] = inputs["L1_ffn_w2"]
    d["l1f_b2r"] = np.ascontiguousarray(inputs["L1_ffn_b2"].reshape(1, 8192))
    return d


def _prep_common(inputs):
    c = {}
    c["c_idf"] = np.eye(128, dtype=np.float32)
    c["c_idb"] = np.eye(128).astype(BF)
    e415 = np.zeros((4, 15), dtype=np.float32)
    off = 0
    for e, r in enumerate([1, 2, 4, 8]):
        e415[e, off:off + r] = 1.0
        off += r
    c["c_e415"] = e415.astype(BF)
    return c


def _prep_l0ffn(inputs):
    d = {}
    d["l0f_wt"] = np.ascontiguousarray(inputs["L0_ffn_W"].T)
    d["l0f_wpt"] = np.ascontiguousarray(inputs["L0_ffn_Wp"].T)
    d["l0f_bp"] = np.ascontiguousarray(inputs["L0_ffn_bp"].reshape(2, 128).T)
    d["l0f_sim"] = np.ascontiguousarray(
        inputs["L0_ffn_sim"].reshape(2, 128, 4).transpose(1, 0, 2).reshape(128, 8))
    d["l0f_temp"] = inputs["L0_ffn_temp"].reshape(1, 1)
    wd = np.concatenate([inputs["L0_ffn_Wd0"], inputs["L0_ffn_Wd1"],
                         inputs["L0_ffn_Wd2"], inputs["L0_ffn_Wd3"]], axis=0)
    d["l0f_wdt"] = np.ascontiguousarray(
        wd.T.reshape(8, 128, 15).transpose(1, 0, 2).reshape(128, 120))
    wu = np.concatenate([inputs["L0_ffn_Wu0"], inputs["L0_ffn_Wu1"],
                         inputs["L0_ffn_Wu2"], inputs["L0_ffn_Wu3"]], axis=1)
    d["l0f_wut"] = np.ascontiguousarray(wu.T)
    return d


def _load_phase_consts(p, need=("idf", "idb", "e415")):
    if "idf" in need:
        p.IDF = p.sb("c_idf_sb", [128, 128], F32)
        p.tok_idf = emit_big_load(p, p.IDF[:], p.din("c_idf", [128, 128]))
    if "idb" in need:
        p.IDB = p.sb("c_idb_sb", [128, 128], BF16)
        p.tok_idb = emit_big_load(p, p.IDB[:], p.din("c_idb", [128, 128], BF16))
    if "e415" in need:
        p.E415 = p.sb("c_e415_sb", [4, 15], BF16)
        p.tok_e415 = emit_big_load(p, p.E415[:], p.din("c_e415", [4, 15], BF16))


_prog_cache = {}


def build_l0ffn_prog():
    if "l0ffn" in _prog_cache:
        return _prog_cache["l0ffn"]
    p = Prog()
    xin = p.din("xin", [1024, S])
    xout = p.dout("xout", [1024, S])
    p.init_consts()
    _load_phase_consts(p)
    X = p.sb("X", [128, NCH * S], F32)
    tok_x = emit_big_load(p, X[:].rearrange("p (c s) -> p c s", c=NCH),
                          xin.rearrange("(c p) s -> p c s", p=128))
    tok = emit_l0_ffn(p, X, deps=[tok_x])
    tok_st = emit_store_xT(p, X, xout, [tok])
    emit_final_wait(p, [tok_st])
    nc = p.close()
    _prog_cache["l0ffn"] = (nc, p)
    return nc, p


def build_fused_prog():
    if "fused" in _prog_cache:
        return _prog_cache["fused"]
    p = Prog()
    xin = p.din("xin", [1024, S])
    y = p.dout("y", [S, 1024])
    p.init_consts()
    _load_phase_consts(p, need=("idf", "idb", "e415"))
    X = p.sb("X", [128, NCH * S], F32)
    tok_x = emit_big_load(p, X[:].rearrange("p (c s) -> p c s", c=NCH),
                          xin.rearrange("(c p) s -> p c s", p=128))
    with p.nc.named_scope("l0_attn"):
        tok = emit_l0_attn(p, X, deps=[tok_x])
    moe_conv = emit_moe_conv(p)
    with p.nc.named_scope("l0_ffn"):
        tok = emit_l0_ffn(p, X, deps=[tok])
    with p.nc.named_scope("l1_attn"):
        tok = emit_l1_attn(p, X, deps=[tok])
    with p.nc.named_scope("l1_ffn"):
        toks = emit_l1_ffn(p, X, deps=[tok], y_out=y, moe_conv=moe_conv)
    emit_final_wait(p, toks)
    nc = p.close()
    _prog_cache["fused"] = (nc, p)
    return nc, p


def kernel(**inputs):
    emb = inputs["L0_attn_emb"]
    ncore = 8
    prep = _prep_common(inputs)
    prep.update(_prep_l0attn(inputs))
    prep.update(_prep_l0ffn(inputs))
    prep.update(_prep_l1attn(inputs))
    prep.update(_prep_l1ffn(inputs))

    nc, p = build_fused_prog()
    in_maps = []
    for i in range(ncore):
        m = {"xin": np.ascontiguousarray(emb[i].T)}
        m.update(prep)
        in_maps.append(m)
    res = run_bass_kernel_spmd(nc, in_maps, core_ids=list(range(ncore)))
    return np.stack([r["y"] for r in res.results], axis=0)


def kernel_unfused(**inputs):
    emb = inputs["L0_attn_emb"]
    ncore = 8
    prep = _prep_common(inputs)
    prep.update(_prep_l0attn(inputs))
    prep.update(_prep_l0ffn(inputs))
    prep.update(_prep_l1attn(inputs))
    prep.update(_prep_l1ffn(inputs))
    x = [np.ascontiguousarray(emb[i].T) for i in range(ncore)]
    for name, builder in [("l0attn", build_l0attn_prog), ("l0ffn", build_l0ffn_prog),
                          ("l1attn", build_l1attn_prog), ("l1ffn", build_l1ffn_prog)]:
        nc, p = builder()
        in_maps = []
        for i in range(ncore):
            m = {"xin": x[i]}
            m.update(prep)
            in_maps.append(m)
        res = run_bass_kernel_spmd(nc, in_maps, core_ids=list(range(ncore)))
        if name == "l1ffn":
            return np.stack([r["y"] for r in res.results], axis=0)
        x = [np.ascontiguousarray(r["xout"]) for r in res.results]


# version 09 (final)
# speedup vs baseline: 1.3691x; 1.3148x over previous
import math
import numpy as np
import ml_dtypes
from contextlib import ExitStack

import concourse.bass as bass
import concourse.mybir as mybir
from concourse.bass_utils import run_bass_kernel_spmd

F32 = mybir.dt.float32
BF16 = mybir.dt.bfloat16
AF = mybir.ActivationFunctionType
ALU = mybir.AluOpType
AX = mybir.AxisListType
BF = ml_dtypes.bfloat16

S = 2048
DM = 1024
NT = 16
NCH = 8
EPS = 1e-5

class Sem:
    def __init__(self, handle, name):
        self.h = handle
        self.name = name
        self.n = 0

    def inc(self, ins, k=1):
        ins.then_inc(self.h, k)
        self.n += k
        return (self, self.n)

    def tok(self):
        return (self, self.n)


def wait(eng, *toks):
    for t in toks:
        if t is None:
            continue
        sem, val = t
        if val > 0:
            eng.wait_ge(sem.h, val)


class _BlockShim:
    def __init__(self, b):
        self.b = b

    def __enter__(self):
        return self.b

    def __exit__(self, *a):
        return False


class Prog:
    def __init__(self):
        self.nc = bass.Bass("TRN2", target_bir_lowering=False)
        self.st = ExitStack()
        self.sems = {}
        self.consts_ready = None
        self._blk = None
        self._blkcm = None

    def blockx(self):
        if self._blk is None:
            self._blkcm = self.nc.Block()
            self._blk = self._blkcm.__enter__()
        return _BlockShim(self._blk)

    def sem(self, name):
        if name not in self.sems:
            h = self.st.enter_context(self.nc.semaphore(name))
            self.sems[name] = Sem(h, name)
        return self.sems[name]

    def sb(self, name, shape, dt):
        return self.st.enter_context(self.nc.sbuf_tensor(name, shape, dt))

    def psum(self, name, shape, dt=F32):
        return self.st.enter_context(self.nc.psum_tensor(name, shape, dt))

    def din(self, name, shape, dt=F32):
        return self.nc.dram_tensor(name, shape, dt, kind="ExternalInput").ap()

    def dout(self, name, shape, dt=F32):
        return self.nc.dram_tensor(name, shape, dt, kind="ExternalOutput").ap()

    def dtmp(self, name, shape, dt=F32):
        return self.nc.dram_tensor(name, shape, dt).ap()

    def close(self):
        if self._blkcm is not None:
            self._blkcm.__exit__(None, None, None)
            self._blkcm = None
        self.st.close()
        return self.nc

    def init_consts(self):
        nc = self.nc
        self.ones_bf = self.sb("ones_bf", [128, 1], BF16)
        self.ones_f32col = self.sb("ones_f32col", [128, 1], F32)
        self.ones_row_f = self.sb("ones_row_f", [1, 128], F32)
        self.ones_row_bf16 = self.sb("ones_row_bf16", [1, 128], BF16)
        self.cpool = self.sb("cpool", [128, 4], F32)
        g = self.sem("g")
        with self.blockx() as blk:
            @blk.gpsimd
            def _(gpsimd):
                gpsimd.memset(self.ones_bf[:], 1.0)
                gpsimd.memset(self.ones_f32col[:], 1.0)
                gpsimd.memset(self.ones_row_f[:], 1.0)
                gpsimd.memset(self.ones_row_bf16[:], 1.0)
                gpsimd.memset(self.cpool[:, 0:1], 1e-5)
                gpsimd.memset(self.cpool[:, 1:2], 24.0)
                gpsimd.memset(self.cpool[:, 2:3], float(math.log(0.125)))
                t = g.inc(gpsimd.memset(self.cpool[:, 3:4], -10.0))
        self.consts_ready = g.tok()


def chunk(buf, c, width=S):
    return buf[:, c * width:(c + 1) * width]


def emit_big_load(p, dst_sb, src_ap, cast=False, deps=(), sem=None):
    if not hasattr(p, "_ldctr"):
        p._ldctr = 0
        p.load_toks = []
    if sem is not None:
        ld = sem
    else:
        ld = p.sem("uld%d" % p._ldctr)
        p._ldctr += 1
    with p.blockx() as blk:
        if cast:
            @blk.gpsimd
            def _(gpsimd):
                wait(gpsimd, *deps)
                ld.inc(gpsimd.dma_start(out=dst_sb, in_=src_ap), 16)
        else:
            @blk.sync
            def _(sync):
                wait(sync, *deps)
                ld.inc(sync.dma_start(out=dst_sb, in_=src_ap), 16)
    p.load_toks.append(ld.tok())
    return ld.tok()


def emit_layernorm(p, SRC, LNB, tag, deps=(), affine=None, scratch=None):
    nc = p.nc
    sV, sT, sS, sG = p.sem("v"), p.sem("t"), p.sem("s"), p.sem("g")
    st2 = ExitStack()
    if scratch is None:
        SQ = st2.enter_context(nc.sbuf_tensor(tag + "_sq", [128, 2 * 512], BF16))[:]
        ROWS = st2.enter_context(nc.sbuf_tensor(tag + "_rows", [1, 3 * S], F32))[:]
        BCA = st2.enter_context(nc.sbuf_tensor(tag + "_bca", [128, S], F32))[:]
        BCB = st2.enter_context(nc.sbuf_tensor(tag + "_bcb", [128, S], F32))[:]
        TMP = st2.enter_context(nc.sbuf_tensor(tag + "_tmp", [128, 2 * 512], BF16))[:]
    else:
        SQ, ROWS, BCA, BCB, TMP = scratch
    stStat = ExitStack()
    PS = stStat.enter_context(nc.psum_tensor(tag + "_ps", [1, S], F32))
    PQ = stStat.enter_context(nc.psum_tensor(tag + "_pq", [1, S], F32))

    v0, t0, s0 = sV.n, sT.n, sS.n
    with p.blockx() as blk:
        @blk.vector
        def _(vector):
            wait(vector, *deps)
            for c in range(NCH):
                sV.inc(vector.tensor_copy(out=chunk(LNB, c), in_=chunk(SRC, c)))

        @blk.scalar
        def _(scalar):
            for i in range(32):
                c, sb = divmod(i, 4)
                wait(scalar, (sV, v0 + c + 1))
                if i >= 2:
                    wait(scalar, (sT, t0 + (i - 2) + 1))
                sS.inc(scalar.activation(
                    out=SQ[:, (i % 2) * 512:(i % 2) * 512 + 512],
                    in_=LNB[:, c * S + 512 * sb: c * S + 512 * (sb + 1)],
                    func=AF.Square))

        @blk.tensor
        def _(tensor):
            wait(tensor, p.consts_ready)
            for i in range(32):
                c, sb = divmod(i, 4)
                wait(tensor, (sV, v0 + c + 1), (sS, s0 + i + 1))
                tensor.matmul(out=PS[0:1, 512 * sb:512 * (sb + 1)],
                              lhsT=p.ones_bf[:],
                              rhs=LNB[:, c * S + 512 * sb: c * S + 512 * (sb + 1)],
                              start=(c == 0), stop=(c == NCH - 1))
                mm = tensor.matmul(out=PQ[0:1, 512 * sb:512 * (sb + 1)],
                                   lhsT=p.ones_bf[:],
                                   rhs=SQ[:, (i % 2) * 512:(i % 2) * 512 + 512],
                                   start=(c == 0), stop=(c == NCH - 1))
                sT.inc(mm)

    with p.blockx() as blk:
        @blk.vector
        def _(vector):
            wait(vector, (sT, t0 + 32))
            vector.tensor_scalar_mul(out=ROWS[0:1, 0:S], in0=PS[0:1, :], scalar1=1.0 / DM)
            sV.inc(vector.tensor_scalar_mul(out=ROWS[0:1, 2 * S:3 * S], in0=PQ[0:1, :],
                                            scalar1=1.0 / DM))
        vA = sV.n

        @blk.scalar
        def _(scalar):
            wait(scalar, (sV, vA))
            sS.inc(scalar.activation(out=ROWS[0:1, S:2 * S], in_=ROWS[0:1, 0:S],
                                     func=AF.Square))
        sA = sS.n

        @blk.gpsimd
        def _(gpsimd):
            wait(gpsimd, (sS, sA))
            sG.inc(gpsimd.tensor_sub(out=ROWS[0:1, 2 * S:3 * S],
                                     in0=ROWS[0:1, 2 * S:3 * S],
                                     in1=ROWS[0:1, S:2 * S]))
        gA = sG.n

    with p.blockx() as blk:
        @blk.scalar
        def _(scalar):
            wait(scalar, (sG, gA))
            sS.inc(scalar.activation(out=ROWS[0:1, S:2 * S], in_=ROWS[0:1, 2 * S:3 * S],
                                     func=AF.Sqrt, bias=p.cpool[0:1, 0:1]))
        sB = sS.n

        @blk.vector
        def _(vector):
            wait(vector, (sS, sB))
            trec = sV.inc(vector.reciprocal(out=ROWS[0:1, S:2 * S], in_=ROWS[0:1, S:2 * S]))
            wait(vector, trec)
            sV.inc(vector.tensor_mul(out=ROWS[0:1, 2 * S:3 * S], in0=ROWS[0:1, 0:S],
                                     in1=ROWS[0:1, S:2 * S]))
        vB = sV.n

    stStat.close()
    PB = st2.enter_context(nc.psum_tensor(tag + "_pb", [128, 512], F32))
    PC = st2.enter_context(nc.psum_tensor(tag + "_pc", [128, 512], F32))
    tB = sT.n
    vC = sV.n
    with p.blockx() as blk:
        @blk.tensor
        def _(tensor):
            wait(tensor, (sV, vB))
            for sb in range(4):
                if sb >= 1:
                    wait(tensor, (sV, vC + 2 * (sb - 1) + 2))
                sT.inc(tensor.matmul(out=PB[:, 0:512], lhsT=p.ones_row_f[:],
                                     rhs=ROWS[0:1, S + 512 * sb:S + 512 * (sb + 1)],
                                     start=True, stop=True))
                sT.inc(tensor.matmul(out=PC[:, 0:512], lhsT=p.ones_row_f[:],
                                     rhs=ROWS[0:1, 2 * S + 512 * sb:2 * S + 512 * (sb + 1)],
                                     start=True, stop=True))

        @blk.vector
        def _(vector):
            for sb in range(4):
                wait(vector, (sT, tB + 2 * sb + 1))
                sV.inc(vector.tensor_copy(out=BCA[:, 512 * sb:512 * (sb + 1)], in_=PB[:, 0:512]))
                wait(vector, (sT, tB + 2 * sb + 2))
                sV.inc(vector.tensor_copy(out=BCB[:, 512 * sb:512 * (sb + 1)], in_=PC[:, 0:512]))

    vD = sV.n
    gD = sG.n
    with p.blockx() as blk:
        @blk.vector
        def _(vector):
            wait(vector, (sV, vD))
            for i in range(32):
                c, sb = divmod(i, 4)
                if i >= 2:
                    wait(vector, (sG, gD + (i - 2) + 1))
                sV.inc(vector.tensor_mul(
                    out=TMP[:, (i % 2) * 512:(i % 2) * 512 + 512],
                    in0=LNB[:, c * S + 512 * sb: c * S + 512 * (sb + 1)],
                    in1=BCA[:, 512 * sb:512 * (sb + 1)]))

        @blk.gpsimd
        def _(gpsimd):
            for i in range(32):
                c, sb = divmod(i, 4)
                wait(gpsimd, (sV, vD + i + 1))
                sG.inc(gpsimd.tensor_sub(
                    out=LNB[:, c * S + 512 * sb: c * S + 512 * (sb + 1)],
                    in0=TMP[:, (i % 2) * 512:(i % 2) * 512 + 512],
                    in1=BCB[:, 512 * sb:512 * (sb + 1)]))

    done = sG.tok()
    if affine is not None:
        Wc, Bc = affine
        gE = sG.n - 32
        with p.blockx() as blk:
            @blk.vector
            def _(vector):
                for c in range(NCH):
                    wait(vector, (sG, gE + 4 * (c + 1)))
                    sV.inc(vector.tensor_scalar(out=chunk(LNB, c), in0=chunk(LNB, c),
                                                scalar1=Wc[:, c:c + 1], scalar2=Bc[:, c:c + 1],
                                                op0=ALU.mult, op1=ALU.add))
        done = sV.tok()
    st2.close()
    return done


def emit_store_xT(p, X, dst_ap, deps):
    if not hasattr(p, "_ldctr"):
        p._ldctr = 0
    gd = p.sem("uld%d" % p._ldctr)
    p._ldctr += 1
    with p.blockx() as blk:
        @blk.gpsimd
        def _(gpsimd):
            wait(gpsimd, *deps)
            t = gd.inc(gpsimd.dma_start(
                out=dst_ap.rearrange("(c p) s -> p c s", p=128),
                in_=X[:].rearrange("p (c s) -> p c s", c=NCH)), 16)
    return gd.tok()


def emit_final_wait(p, toks):
    all_toks = list(toks) + getattr(p, "load_toks", [])
    with p.blockx() as blk:
        @blk.sync
        def _(sync):
            wait(sync, *all_toks)


def emit_l0_ffn(p, X, deps):
    nc = p.nc
    sV, sT, sS, sG = p.sem("v"), p.sem("t"), p.sem("s"), p.sem("g")
    st2 = ExitStack()
    sbuf = lambda n, sh, dt: st2.enter_context(nc.sbuf_tensor(n, sh, dt))
    psum = lambda n, sh: st2.enter_context(nc.psum_tensor(n, sh, F32))

    wt = p.din("l0f_wt", [1024, 1024])
    wpt = p.din("l0f_wpt", [1024, 256])
    bp = p.din("l0f_bp", [128, 2])
    simc = p.din("l0f_sim", [128, 8])
    temp = p.din("l0f_temp", [1, 1])
    wdt = p.din("l0f_wdt", [128, 120])
    wut = p.din("l0f_wut", [15, 1024])

    LNB = sbuf("bf_lnb", [128, NCH * S], BF16)
    WT16 = sbuf("bf_wt", [128, NCH * 1024], BF16)
    WPT16 = sbuf("bf_wpt", [128, NCH * 256], BF16)
    BPC = sbuf("bf_bp", [128, 2], F32)
    SIMC = sbuf("bf_simc", [128, 8], F32)
    SIMSQ = sbuf("bf_simsq", [128, 8], F32)
    SIMN = sbuf("bf_simn", [128, 8], BF16)
    TEMP = sbuf("bf_temp", [1, 1], F32)
    WDT16 = sbuf("bf_wdt", [128, 120], BF16)
    WUT16 = sbuf("bf_wut", [15, 1024], BF16)
    tok_wt = emit_big_load(p, WT16[:].rearrange("p (c f) -> p c f", c=NCH),
                           wt.rearrange("(c p) f -> p c f", p=128), cast=True,
                           deps=deps)
    tok_wpt = emit_big_load(p, WPT16[:].rearrange("p (c f) -> p c f", c=NCH),
                            wpt.rearrange("(c p) f -> p c f", p=128), cast=True,
                            deps=deps)
    tok_wdt = emit_big_load(p, WDT16[:], wdt, cast=True, deps=deps)
    tok_wut = emit_big_load(p, WUT16[:], wut, cast=True, deps=deps)
    tok_bp = emit_big_load(p, BPC[:], bp, deps=deps)
    tok_sim = emit_big_load(p, SIMC[:], simc, deps=deps)
    tok_temp = emit_big_load(p, TEMP[:], temp, deps=deps)

    tok_ln = emit_layernorm(p, X, LNB, "bf_ln", deps=deps)

    PRJ = sbuf("bf_prj", [128, 2 * S], BF16)
    PSQ = sbuf("bf_psq", [128, 2 * 512], BF16)
    R1 = sbuf("bf_r1", [1, S], F32)
    R4 = sbuf("bf_r4", [1, 16], F32)
    IV4 = sbuf("bf_iv4", [128, 4], F32)
    INVC = sbuf("bf_invc", [128, 16], F32)
    GT = sbuf("bf_gt", [128, 64], F32)
    GM = sbuf("bf_gm", [128, 16], F32)
    GMK = sbuf("bf_gmk", [128, 64], F32)
    RETT = sbuf("bf_rett", [4, S], BF16)
    RETX = sbuf("bf_retx", [15, S], BF16)
    DT = sbuf("bf_dt", [15, S], BF16)

    stPN = ExitStack()
    PN = stPN.enter_context(nc.psum_tensor("bf_pn", [1, S], F32))
    stPP = ExitStack()
    PP = stPP.enter_context(nc.psum_tensor("bf_pp", [128, 512], F32))
    PP2 = stPP.enter_context(nc.psum_tensor("bf_pp2", [128, 512], F32))
    t0 = sT.n
    s0 = sS.n
    sched = []
    for i in range(8):
        sched.append(("P", i))
        if i >= 2:
            sched.append(("N", i - 2))
    sched += [("N", 6), ("N", 7)]
    t_at = {}
    for k, ev in enumerate(sched):
        t_at[ev] = t0 + k + 1
    with p.blockx() as blk:
        @blk.tensor
        def _(tensor):
            wait(tensor, tok_ln, tok_wpt, tok_bp)
            for ev in sched:
                kind, i = ev
                if kind == "P":
                    pc, sb = divmod(i, 4)
                    PPx = PP if i % 2 == 0 else PP2
                    if i >= 2:
                        wait(tensor, (sS, s0 + 2 * (i - 2) + 2))
                    for dc in range(NCH):
                        mm = tensor.matmul(
                            out=PPx[:, 0:512],
                            lhsT=WPT16[:, dc * 256 + 128 * pc: dc * 256 + 128 * (pc + 1)],
                            rhs=LNB[:, dc * S + 512 * sb: dc * S + 512 * (sb + 1)],
                            start=(dc == 0), stop=(dc == NCH - 1))
                    sT.inc(mm)
                else:
                    pc, sb = divmod(i, 4)
                    wait(tensor, (sS, s0 + 2 * i + 2))
                    mm = tensor.matmul(out=PN[0:1, 512 * sb:512 * (sb + 1)],
                                       lhsT=p.ones_bf[:],
                                       rhs=PSQ[:, (i % 2) * 512:(i % 2) * 512 + 512],
                                       start=(pc == 0), stop=(pc == 1))
                    sT.inc(mm)

        @blk.scalar
        def _(scalar):
            for i in range(8):
                pc, sb = divmod(i, 4)
                wait(scalar, (sT, t_at[("P", i)]))
                if i >= 2:
                    wait(scalar, (sT, t_at[("N", i - 2)]))
                tevac = sS.inc(scalar.activation(
                    out=PRJ[:, pc * S + 512 * sb: pc * S + 512 * (sb + 1)],
                    in_=(PP if i % 2 == 0 else PP2)[:, 0:512],
                    func=AF.Identity, bias=BPC[:, pc:pc + 1]))
                wait(scalar, tevac)
                sS.inc(scalar.activation(
                    out=PSQ[:, (i % 2) * 512:(i % 2) * 512 + 512],
                    in_=PRJ[:, pc * S + 512 * sb: pc * S + 512 * (sb + 1)],
                    func=AF.Square))
    tok_projmm = sT.tok()
    stPP.close()

    sB0 = sS.n
    with p.blockx() as blk:
        @blk.scalar
        def _(scalar):
            wait(scalar, tok_projmm)
            sS.inc(scalar.activation(out=R1[0:1, :], in_=PN[0:1, :], func=AF.Sqrt))

        @blk.vector
        def _(vector):
            wait(vector, (sS, sB0 + 1))
            tmx = sV.inc(vector.tensor_scalar_max(out=R1[0:1, :], in0=R1[0:1, :], scalar1=1e-12))
            wait(vector, tmx)
            sV.inc(vector.reciprocal(out=R1[0:1, :], in_=R1[0:1, :]))
    tok_invn = sV.tok()
    stPN.close()

    stPIC = ExitStack()
    PIC = stPIC.enter_context(nc.psum_tensor("bf_pic", [128, 16], F32))
    tI = sT.n
    with p.blockx() as blk:
        @blk.tensor
        def _(tensor):
            wait(tensor, tok_invn)
            for t in range(16):
                tr = tensor.transpose(out=PIC[:, t:t + 1], in_=R1[0:1, 128 * t:128 * (t + 1)],
                                      identity=p.ones_row_f[0:1, 0:1])
            sT.inc(tr)

        @blk.vector
        def _(vector):
            wait(vector, (sT, tI + 1))
            sV.inc(vector.tensor_copy(out=INVC[:], in_=PIC[:]))
    tok_invc = sV.tok()
    stPIC.close()

    sC0, vC0, tC0, gC0 = sS.n, sV.n, sT.n, sG.n
    stS8 = ExitStack()
    PS8 = stS8.enter_context(nc.psum_tensor("bf_ps8", [1, 16], F32))
    PIV = stS8.enter_context(nc.psum_tensor("bf_piv", [128, 4], F32))
    with p.blockx() as blk:
        @blk.scalar
        def _(scalar):
            wait(scalar, tok_sim)
            sS.inc(scalar.activation(out=SIMSQ[:], in_=SIMC[:], func=AF.Square))
            wait(scalar, tok_temp, (sG, gC0 + 1))
            sS.inc(scalar.activation(out=TEMP[:], in_=TEMP[:], func=AF.Exp))

        @blk.gpsimd
        def _(gpsimd):
            wait(gpsimd, tok_temp)
            sG.inc(gpsimd.tensor_scalar_min(out=TEMP[:], in0=TEMP[:],
                                            scalar1=float(np.log(1.0 / 0.01))))

        @blk.tensor
        def _(tensor):
            wait(tensor, (sS, sC0 + 1), tok_invc)
            sT.inc(tensor.matmul(out=PS8[0:1, 0:8], lhsT=p.ones_f32col[:],
                                 rhs=SIMSQ[:], start=True, stop=True))

        @blk.vector
        def _(vector):
            wait(vector, (sT, tC0 + 1))
            tcp = sV.inc(vector.tensor_copy(out=R4[0:1, 8:16], in_=PS8[0:1, 0:8]))
            wait(vector, tcp)
            sV.inc(vector.tensor_add(out=R4[0:1, 0:4], in0=R4[0:1, 8:12], in1=R4[0:1, 12:16]))
    sD0, vD0, tD0 = sS.n, sV.n, sT.n
    with p.blockx() as blk:
        @blk.scalar
        def _(scalar):
            wait(scalar, (sV, vD0))
            sS.inc(scalar.activation(out=R4[0:1, 4:8], in_=R4[0:1, 0:4], func=AF.Sqrt))

        @blk.vector
        def _(vector):
            wait(vector, (sS, sD0 + 1))
            t1 = sV.inc(vector.tensor_scalar_max(out=R4[0:1, 4:8], in0=R4[0:1, 4:8], scalar1=1e-12))
            wait(vector, t1)
            t2 = sV.inc(vector.reciprocal(out=R4[0:1, 4:8], in_=R4[0:1, 4:8]))
            wait(vector, t2)
            sV.inc(vector.tensor_scalar_mul(out=R4[0:1, 4:8], in0=R4[0:1, 4:8],
                                            scalar1=TEMP[0:1, 0:1]))

        @blk.tensor
        def _(tensor):
            wait(tensor, (sV, vD0 + 3))
            sT.inc(tensor.matmul(out=PIV[:], lhsT=p.ones_row_f[:], rhs=R4[0:1, 4:8],
                                 start=True, stop=True))

        @blk.scalar
        def _(scalar):
            wait(scalar, (sT, tD0 + 1), (sS, sD0 + 1))
            sS.inc(scalar.activation(out=IV4[:], in_=PIV[:], func=AF.Copy))
    sE0 = sS.n
    with p.blockx() as blk:
        @blk.vector
        def _(vector):
            wait(vector, (sS, sE0))
            sV.inc(vector.tensor_tensor(
                out=SIMN[:].rearrange("p (c e) -> p c e", c=2),
                in0=SIMC[:].rearrange("p (c e) -> p c e", c=2),
                in1=bass.AP(IV4, 0, [[4, 128], [0, 2], [1, 4]]),
                op=ALU.mult))
    tok_simn = sV.tok()
    stS8.close()

    stPG = ExitStack()
    PG = stPG.enter_context(nc.psum_tensor("bf_pg", [128, 64], F32))
    tG0 = sT.n
    with p.blockx() as blk:
        @blk.tensor
        def _(tensor):
            wait(tensor, tok_simn, tok_projmm)
            for t in range(16):
                for pc in range(2):
                    mm = tensor.matmul(out=PG[:, 4 * t:4 * (t + 1)],
                                       lhsT=PRJ[:, pc * S + 128 * t: pc * S + 128 * (t + 1)],
                                       rhs=SIMN[:, 4 * pc:4 * (pc + 1)],
                                       start=(pc == 0), stop=(pc == 1))
            sT.inc(mm)

        @blk.vector
        def _(vector):
            wait(vector, (sT, tG0 + 1), tok_invc)
            t1 = sV.inc(vector.tensor_tensor(
                out=GT[:].rearrange("p (t e) -> p t e", t=16),
                in0=PG[:].rearrange("p (t e) -> p t e", t=16),
                in1=bass.AP(INVC, 0, [[16, 128], [1, 16], [0, 4]]),
                op=ALU.mult))
            wait(vector, t1)
            t2 = sV.inc(vector.tensor_reduce(out=GM[:], in_=GT[:].rearrange("p (t e) -> p t e", t=16),
                                             axis=AX.X, op=ALU.max, negate=True))
            wait(vector, t2)
            sV.inc(vector.tensor_tensor(
                out=GT[:].rearrange("p (t e) -> p t e", t=16),
                in0=GT[:].rearrange("p (t e) -> p t e", t=16),
                in1=bass.AP(GM, 0, [[16, 128], [1, 16], [0, 4]]),
                op=ALU.add))
        vG0 = sV.n

        @blk.scalar
        def _(scalar):
            wait(scalar, (sV, vG0))
            sS.inc(scalar.activation(out=GT[:], in_=GT[:], func=AF.Exp))
        sG0 = sS.n

    vH0 = sV.n
    sH0 = sS.n
    with p.blockx() as blk:
        @blk.vector
        def _(vector):
            wait(vector, (sS, sG0))
            t1 = sV.inc(vector.tensor_reduce(out=GM[:], in_=GT[:].rearrange("p (t e) -> p t e", t=16),
                                             axis=AX.X, op=ALU.add))
            wait(vector, t1)
            t2 = sV.inc(vector.reciprocal(out=GM[:], in_=GM[:]))
            wait(vector, t2)
            t3 = sV.inc(vector.tensor_tensor(
                out=GT[:].rearrange("p (t e) -> p t e", t=16),
                in0=GT[:].rearrange("p (t e) -> p t e", t=16),
                in1=bass.AP(GM, 0, [[16, 128], [1, 16], [0, 4]]),
                op=ALU.mult))
            wait(vector, t3)
            t4 = sV.inc(vector.tensor_reduce(out=GM[:], in_=GT[:].rearrange("p (t e) -> p t e", t=16),
                                             axis=AX.X, op=ALU.max))
            wait(vector, t4)
            t5 = sV.inc(vector.tensor_tensor(
                out=GMK[:].rearrange("p (t e) -> p t e", t=16),
                in0=GT[:].rearrange("p (t e) -> p t e", t=16),
                in1=bass.AP(GM, 0, [[16, 128], [1, 16], [0, 4]]),
                op=ALU.is_ge))
            wait(vector, t5)
            sV.inc(vector.tensor_mul(out=GT[:], in0=GT[:], in1=GMK[:]))
    tok_ret = sV.tok()
    stPG.close()

    stPRT = ExitStack()
    PRT = stPRT.enter_context(nc.psum_tensor("bf_prt", [4, 128], F32))
    PRT2 = stPRT.enter_context(nc.psum_tensor("bf_prt2", [4, 128], F32))
    tR0 = sT.n
    vR0 = sV.n
    with p.blockx() as blk:
        @blk.tensor
        def _(tensor):
            wait(tensor, tok_ret, p.tok_idf)
            for t in range(16):
                if t >= 2:
                    wait(tensor, (sV, vR0 + (t - 2) + 1))
                sT.inc(tensor.transpose(out=(PRT if t % 2 == 0 else PRT2)[:],
                                        in_=GT[:, 4 * t:4 * (t + 1)],
                                        identity=p.IDF[:]))

        @blk.vector
        def _(vector):
            for t in range(16):
                wait(vector, (sT, tR0 + t + 1))
                sV.inc(vector.tensor_copy(out=RETT[0:4, 128 * t:128 * (t + 1)],
                                          in_=(PRT if t % 2 == 0 else PRT2)[:]))
    tok_rett = sV.tok()

    stPRT.close()
    stPRX = ExitStack()
    PRX = stPRX.enter_context(nc.psum_tensor("bf_prx", [15, 512], F32))
    tX0 = sT.n
    vX0 = sV.n
    with p.blockx() as blk:
        @blk.tensor
        def _(tensor):
            wait(tensor, tok_rett, p.tok_e415)
            for sb in range(4):
                if sb >= 1:
                    wait(tensor, (sV, vX0 + sb))
                sT.inc(tensor.matmul(out=PRX[:], lhsT=p.E415[:],
                                     rhs=RETT[0:4, 512 * sb:512 * (sb + 1)],
                                     start=True, stop=True))

        @blk.vector
        def _(vector):
            for sb in range(4):
                wait(vector, (sT, tX0 + sb + 1))
                sV.inc(vector.tensor_copy(out=RETX[0:15, 512 * sb:512 * (sb + 1)], in_=PRX[:]))
    tok_retx = sV.tok()

    stPRX.close()
    stPD = ExitStack()
    PD = stPD.enter_context(nc.psum_tensor("bf_pd", [15, 512], F32))
    tD1 = sT.n
    vD1 = sV.n
    with p.blockx() as blk:
        @blk.tensor
        def _(tensor):
            wait(tensor, tok_wdt, tok_retx)
            for sb in range(4):
                if sb >= 1:
                    wait(tensor, (sV, vD1 + sb))
                for dc in range(NCH):
                    mm = tensor.matmul(out=PD[:],
                                       lhsT=WDT16[:, 15 * dc:15 * (dc + 1)],
                                       rhs=LNB[:, dc * S + 512 * sb: dc * S + 512 * (sb + 1)],
                                       start=(dc == 0), stop=(dc == NCH - 1))
                sT.inc(mm)

        @blk.vector
        def _(vector):
            for sb in range(4):
                wait(vector, (sT, tD1 + sb + 1))
                sV.inc(vector.tensor_tensor(out=DT[0:15, 512 * sb:512 * (sb + 1)], in0=PD[:],
                                            in1=RETX[0:15, 512 * sb:512 * (sb + 1)],
                                            op=ALU.mult))
    tok_dt = sV.tok()

    stPD.close()
    stPO = ExitStack()
    PO = stPO.enter_context(nc.psum_tensor("bf_po", [128, 512], F32))
    PO2 = stPO.enter_context(nc.psum_tensor("bf_po2", [128, 512], F32))
    tF0 = sT.n
    vF0 = sV.n
    with p.blockx() as blk:
        @blk.tensor
        def _(tensor):
            wait(tensor, tok_wt, tok_wut, tok_dt)
            i = 0
            for oc in range(NCH):
                for sb in range(4):
                    POx = PO if i % 2 == 0 else PO2
                    if i >= 2:
                        wait(tensor, (sV, vF0 + (i - 2) + 1))
                    for dc in range(NCH):
                        tensor.matmul(
                            out=POx[:, 0:512],
                            lhsT=WT16[:, dc * 1024 + 128 * oc: dc * 1024 + 128 * (oc + 1)],
                            rhs=LNB[:, dc * S + 512 * sb: dc * S + 512 * (sb + 1)],
                            start=(dc == 0), stop=False)
                    mm = tensor.matmul(out=POx[:, 0:512],
                                       lhsT=WUT16[0:15, 128 * oc:128 * (oc + 1)],
                                       rhs=DT[0:15, 512 * sb:512 * (sb + 1)],
                                       start=False, stop=True)
                    sT.inc(mm)
                    i += 1

        @blk.vector
        def _(vector):
            i = 0
            for oc in range(NCH):
                for sb in range(4):
                    wait(vector, (sT, tF0 + i + 1))
                    dst = X[:, oc * S + 512 * sb: oc * S + 512 * (sb + 1)]
                    sV.inc(vector.scalar_tensor_tensor(
                        out=dst, in0=(PO if i % 2 == 0 else PO2)[:, 0:512],
                        scalar=1.0, in1=dst, op0=ALU.mult, op1=ALU.add))
                    i += 1
    tok_done = sV.tok()
    stPO.close()
    st2.close()
    return tok_done



def emit_spill(p, X, dram_ap, deps):
    ld = p.sem("uld%d" % p._ldctr); p._ldctr += 1
    with p.blockx() as blk:
        @blk.gpsimd
        def _(gpsimd):
            wait(gpsimd, *deps)
            ld.inc(gpsimd.dma_start(out=dram_ap.rearrange("(c p) s -> p c s", p=128),
                                    in_=X[:].rearrange("p (c s) -> p c s", c=NCH)), 16)
    p.load_toks.append(ld.tok())
    return ld.tok()


def emit_restore(p, X, dram_ap, deps):
    ld = p.sem("uld%d" % p._ldctr); p._ldctr += 1
    with p.blockx() as blk:
        @blk.sync
        def _(sync):
            wait(sync, *deps)
            ld.inc(sync.dma_start(out=X[:].rearrange("p (c s) -> p c s", c=NCH),
                                  in_=dram_ap.rearrange("(c p) s -> p c s", p=128)), 16)
    p.load_toks.append(ld.tok())
    return ld.tok()


def emit_l0_attn(p, X, deps, stop_at=None, dbg_raw_ot=False):
    nc = p.nc
    sV, sT, sS, sG = p.sem("v"), p.sem("t"), p.sem("s"), p.sem("g")
    st2 = ExitStack()
    sbuf = lambda n, sh, dt: st2.enter_context(nc.sbuf_tensor(n, sh, dt))

    w1t = p.din("l0a_w1t", [1024, 1024])
    w2t = p.din("l0a_w2t", [1024, 1024])
    w3t = p.din("l0a_w3t", [1024, 1024])
    wot = p.din("l0a_wot", [1024, 1024])
    cost = p.din("l0a_cos", [128, 2048])
    sint = p.din("l0a_sina", [128, 2048])
    xsp = p.dtmp("l0a_xsp", [1024, S])

    LNB = sbuf("af_lnb", [128, NCH * S], BF16)
    A16 = sbuf("af_a16", [128, NT * 1024], BF16)
    B16 = sbuf("af_b16", [128, NCH * S], BF16)
    C16 = sbuf("af_c16", [128, NCH * S], BF16)
    TMPA = sbuf("af_tmpa", [128, 2 * 1024], BF16)
    TMPB = sbuf("af_tmpb", [128, 2 * 1024], BF16)
    SQH = sbuf("af_sqh", [128, 2048], BF16)
    XB = X[:].bitcast(BF16)
    XF = X[:]
    D16 = XB
    WTB = XB[:, 16384:24576]
    COS = XF[:, 12288:14336]
    SIN = XF[:, 14336:16384]
    LNBF = LNB[:].bitcast(F32)
    LNBB = LNB[:]
    EXPT = LNBB[:, 0:1024]
    BCS = LNBF[:, 512:1024]
    SR = LNBF[0:1, 1024:1536]
    RLN = LNBF[0:1, 1536:2048]
    ISR = LNBF[0:1, 2048:2560]
    MQ = LNBF[0:1, 2560:2592]
    NMB = LNBF[:, 2592:2600]
    WOB = LNBB[:, 6144:14336]
    B16F = B16[:].bitcast(F32)
    C16F = C16[:].bitcast(F32)
    ln_scratch = (A16[:, 0:1024], C16F[0:1, 0:6144], B16F[:, 0:2048],
                  B16F[:, 2048:4096], A16[:, 1024:2048])

    tok_ln = emit_layernorm(p, X, LNB, "af_ln", deps=deps, scratch=ln_scratch)
    if stop_at == "ln":
        st2.pop_all()
        return ("dump16", LNB[:], tok_ln)
    tok_spill = emit_spill(p, X, xsp, [tok_ln])
    tok_cos = emit_big_load(p, COS, cost, deps=[tok_spill])
    tok_sin = emit_big_load(p, SIN, sint, deps=[tok_spill])

    stPT = ExitStack()
    PT = stPT.enter_context(nc.psum_tensor("af_pt", [128, 512], F32))
    PT2 = stPT.enter_context(nc.psum_tensor("af_pt2", [128, 512], F32))
    PTRa = stPT.enter_context(nc.psum_tensor("af_ptra", [128, 128], BF16))
    PTRb = stPT.enter_context(nc.psum_tensor("af_ptrb", [128, 128], BF16))

    def project_tok(wsrc, tok_w, qscale, prior_tok):
        t0, s0 = sT.n, sS.n
        evac_toks = []
        with p.blockx() as blk:
            @blk.tensor
            def _(tensor):
                wait(tensor, tok_ln, tok_w, prior_tok)
                for i in range(32):
                    t, half = divmod(i, 2)
                    PTx = PT if i % 2 == 0 else PT2
                    if i >= 2:
                        wait(tensor, (sS, s0 + (i - 2) + 1))
                    for ec in range(NCH):
                        mm = tensor.matmul(
                            out=PTx[:, 0:512],
                            lhsT=LNB[:, ec * S + 128 * t: ec * S + 128 * (t + 1)],
                            rhs=WTB[:, ec * 1024 + 512 * half: ec * 1024 + 512 * (half + 1)],
                            start=(ec == 0), stop=(ec == NCH - 1))
                    sT.inc(mm)

            @blk.scalar
            def _(scalar):
                for i in range(32):
                    t, half = divmod(i, 2)
                    wait(scalar, (sT, t0 + i + 1))
                    tk = sS.inc(scalar.activation(
                        out=A16[:, t * 1024 + 512 * half: t * 1024 + 512 * (half + 1)],
                        in_=(PT if i % 2 == 0 else PT2)[:, 0:512],
                        func=AF.Identity, scale=qscale))
                    evac_toks.append(tk)
        return evac_toks, sS.tok()

    def rope_inplace(tok_proj_evacs):
        v0, g0 = sV.n, sG.n
        add_toks = []
        with p.blockx() as blk:
            @blk.vector
            def _(vector):
                wait(vector, tok_cos, tok_sin)
                for t in range(NT):
                    tv = t % 2
                    wait(vector, tok_proj_evacs[2 * t + 1])
                    if t >= 2:
                        wait(vector, (sV, v0 + 3 * (t - 2) + 3))
                    tm = sV.inc(vector.tensor_mul(
                        out=TMPA[:, (t % 2) * 1024:(t % 2) * 1024 + 1024],
                        in0=A16[:, t * 1024:(t + 1) * 1024],
                        in1=COS[:, tv * 1024:(tv + 1) * 1024]))
                    swap = bass.AP(A16, t * 1024 + 1,
                                   [[NT * 1024, 128], [2, 512], [-1, 2]])
                    tw = sV.inc(vector.tensor_mul(
                        out=TMPB[:, (t % 2) * 1024:(t % 2) * 1024 + 1024],
                        in0=swap.rearrange("p a b -> p (a b)") if False else swap,
                        in1=SIN[:, tv * 1024:(tv + 1) * 1024].rearrange("p (a b) -> p a b", b=2)))
                    wait(vector, tw)
                    tk = sV.inc(vector.tensor_add(
                        out=A16[:, t * 1024:(t + 1) * 1024],
                        in0=TMPA[:, (t % 2) * 1024:(t % 2) * 1024 + 1024],
                        in1=TMPB[:, (t % 2) * 1024:(t % 2) * 1024 + 1024]))
                    add_toks.append(tk)
        return add_toks

    def transpose_to(dst, truej, tile_toks):
        t0, v0 = sT.n, sV.n
        with p.blockx() as blk:
            @blk.tensor
            def _(tensor):
                wait(tensor, p.tok_idb)
                for i in range(128):
                    t, c = divmod(i, 8)
                    wait(tensor, tile_toks[t])
                    if i >= 2:
                        wait(tensor, (sV, v0 + (i - 2) + 1))
                    sT.inc(tensor.transpose(
                        out=(PTRa if i % 2 == 0 else PTRb)[:],
                        in_=A16[:, t * 1024 + 128 * c: t * 1024 + 128 * (c + 1)],
                        identity=p.IDB[:]))

            @blk.vector
            def _(vector):
                for i in range(128):
                    t, c = divmod(i, 8)
                    h, trel = divmod(t, 2)
                    wait(vector, (sT, t0 + i + 1))
                    if truej:
                        dstap = bass.AP(dst, h * 2048 + 1024 * trel + c,
                                        [[NCH * S, 128], [8, 128]])
                    else:
                        dstap = dst[:, h * 2048 + 256 * c + 128 * trel:
                                    h * 2048 + 256 * c + 128 * trel + 128]
                    sV.inc(vector.tensor_copy(
                        out=dstap, in_=(PTRa if i % 2 == 0 else PTRb)[:]))
        return sV.tok()

    tok_w = emit_big_load(p, WTB.rearrange("p (c f) -> p c f", c=NCH),
                          w1t.rearrange("(c p) f -> p c f", p=128), cast=True,
                          deps=[tok_spill])
    ev, tok_pq = project_tok(w1t, tok_w, 1.0 / math.sqrt(8.0), None)
    if stop_at == "qproj":
        st2.pop_all()
        return ("dump16", A16[:], tok_pq)
    rtoks = rope_inplace(ev)
    if stop_at == "qrope":
        st2.pop_all()
        return ("dump16", A16[:], rtoks[-1])
    tok_qf = transpose_to(B16, True, rtoks)
    if stop_at == "qf":
        st2.pop_all()
        return ("dump16", B16[:], tok_qf)
    tok_w = emit_big_load(p, WTB.rearrange("p (c f) -> p c f", c=NCH),
                          w2t.rearrange("(c p) f -> p c f", p=128), cast=True,
                          deps=[tok_pq])
    ev, tok_pk = project_tok(w2t, tok_w, 1.0, tok_qf)
    rtoks = rope_inplace(ev)
    tok_kf = transpose_to(C16, False, rtoks)
    if stop_at == "kf":
        st2.pop_all()
        return ("dump16", C16[:], tok_kf)
    tok_w = emit_big_load(p, WTB.rearrange("p (c f) -> p c f", c=NCH),
                          w3t.rearrange("(c p) f -> p c f", p=128), cast=True,
                          deps=[tok_pk])
    ev, tok_pv = project_tok(w3t, tok_w, 1.0, tok_kf)
    tok_v = tok_pv
    if stop_at == "v":
        st2.pop_all()
        return ("dump16", A16[:], tok_v)

    stPT.close()
    stNR = ExitStack()
    PNR = stNR.enter_context(nc.psum_tensor("af_pnr", [1, S], F32))
    tN0, sN0, vN0 = sT.n, sS.n, sV.n
    with p.blockx() as blk:
        @blk.scalar
        def _(scalar):
            wait(scalar, tok_qf, tok_kf)
            for it in range(16):
                src = B16 if it < 8 else C16
                h = it % 8
                if it > 0:
                    wait(scalar, (sT, tN0 + 4 * it))
                sS.inc(scalar.activation(out=SQH[:], in_=src[:, h * S:(h + 1) * S],
                                         func=AF.Square))

        @blk.tensor
        def _(tensor):
            wait(tensor, p.consts_ready)
            for it in range(16):
                wait(tensor, (sS, sN0 + it + 1))
                if it > 0:
                    wait(tensor, (sV, vN0 + it))
                for sb in range(4):
                    mm = tensor.matmul(out=PNR[0:1, 512 * sb:512 * (sb + 1)],
                                       lhsT=p.ones_bf[:],
                                       rhs=SQH[:, 512 * sb:512 * (sb + 1)],
                                       start=True, stop=True)
                    sT.inc(mm)

        @blk.vector
        def _(vector):
            for it in range(16):
                wait(vector, (sT, tN0 + 4 * it + 4))
                sV.inc(vector.tensor_reduce(out=MQ[0:1, it:it + 1], in_=PNR[0:1, :],
                                            axis=AX.X, op=ALU.max))
    vN1, sN1, tN1 = sV.n, sS.n, sT.n
    stBQ = ExitStack()
    PBQ = stBQ.enter_context(nc.psum_tensor("af_pbq", [128, 8], F32))
    with p.blockx() as blk:
        @blk.vector
        def _(vector):
            wait(vector, (sV, vN1))
            tk = sV.inc(vector.tensor_mul(out=MQ[0:1, 16:24], in0=MQ[0:1, 0:8],
                                          in1=MQ[0:1, 8:16]))
        @blk.scalar
        def _(scalar):
            wait(scalar, (sV, vN1 + 1))
            sS.inc(scalar.activation(out=MQ[0:1, 24:32], in_=MQ[0:1, 16:24],
                                     func=AF.Sqrt))
        @blk.gpsimd
        def _(gpsimd):
            wait(gpsimd, (sS, sN1 + 1))
            sG.inc(gpsimd.tensor_scalar_mul(out=MQ[0:1, 24:32], in0=MQ[0:1, 24:32],
                                            scalar1=-1.0))
    gN1 = sG.n
    with p.blockx() as blk:
        @blk.tensor
        def _(tensor):
            wait(tensor, (sG, gN1))
            sT.inc(tensor.matmul(out=PBQ[:], lhsT=p.ones_row_f[:], rhs=MQ[0:1, 24:32],
                                 start=True, stop=True))
        @blk.scalar
        def _(scalar):
            wait(scalar, (sT, tN1 + 1))
            sS.inc(scalar.activation(out=NMB[:], in_=PBQ[:], func=AF.Copy))
    tok_nmb = sS.tok()
    stBQ.close()
    stNR.close()
    if stop_at == "nmb":
        st2.pop_all()
        return ("dumpf", NMB, tok_nmb)

    stAT = ExitStack()
    OT = stAT.enter_context(nc.psum_tensor("af_ot", [128, S], F32))
    STa = stAT.enter_context(nc.psum_tensor("af_sta", [128, 512], F32))
    STb = stAT.enter_context(nc.psum_tensor("af_stb", [128, 512], F32))
    SIG = stAT.enter_context(nc.psum_tensor("af_sig", [1, 512], F32))
    BCp = stAT.enter_context(nc.psum_tensor("af_bcp", [128, 512], F32))

    t0, s0, v0 = sT.n, sS.n, sV.n
    def t_s(h, qb, k):
        base = t0 + (h * 4 + qb) * 33
        return base + (1 if k == 0 else 2 * k)

    def t_sig(h, qb, k):
        base = t0 + (h * 4 + qb) * 33
        return base + (2 * k + 3 if k < 15 else 32)

    def t_bc(h, qb):
        return t0 + (h * 4 + qb) * 33 + 33

    def s_exp(h, qb, k):
        return s0 + (h * 4 + qb) * 18 + k + 1

    def s_row(h, qb, j):
        return s0 + (h * 4 + qb) * 18 + 16 + j

    def v_sig(h, qb):
        return v0 + (h * 4 + qb) * 3 + 1

    def v_bcs(h, qb):
        return v0 + (h * 4 + qb) * 3 + 2

    def v_ot(h, qb):
        return v0 + (h * 4 + qb) * 3 + 3

    with p.blockx() as blk:
        @blk.tensor
        def _(tensor):
            wait(tensor, tok_nmb, tok_v)
            for h in range(8):
                for qb in range(4):
                    idx = h * 4 + qb
                    if idx > 0:
                        wait(tensor, (sV, v_sig(h if qb else h - 1, qb - 1 if qb else 3)))
                    if h > 0:
                        wait(tensor, (sV, v_ot(h - 1, qb)))
                    for k in range(17):
                        if k < 16:
                            g_ = idx * 16 + k
                            if g_ >= 2:
                                i2, k2 = divmod(g_ - 2, 16)
                                wait(tensor, (sS, s_exp(i2 // 4, i2 % 4, k2)))
                            STx = STa if k % 2 == 0 else STb
                            sT.inc(tensor.matmul(
                                out=STx[:, 0:512],
                                lhsT=C16[:, h * S + 128 * k: h * S + 128 * (k + 1)],
                                rhs=B16[:, h * S + 512 * qb: h * S + 512 * (qb + 1)],
                                start=True, stop=True))
                        if k >= 1:
                            kk = k - 1
                            wait(tensor, (sS, s_exp(h, qb, kk)))
                            tile = 2 * h + (kk % 2)
                            jlo = kk // 2
                            tensor.matmul(
                                out=OT[:, 512 * qb:512 * (qb + 1)],
                                lhsT=A16[:, tile * 1024 + 128 * jlo:
                                         tile * 1024 + 128 * (jlo + 1)],
                                rhs=EXPT[:, (kk % 2) * 512:(kk % 2) * 512 + 512],
                                start=(kk == 0), stop=(kk == 15))
                            sT.inc(tensor.matmul(
                                out=SIG[0:1, 0:512],
                                lhsT=p.ones_bf[:],
                                rhs=EXPT[:, (kk % 2) * 512:(kk % 2) * 512 + 512],
                                start=(kk == 0), stop=(kk == 15)))
                    wait(tensor, (sS, s_row(h, qb, 2)))
                    if idx > 0:
                        wait(tensor, (sV, v_bcs(h if qb else h - 1, qb - 1 if qb else 3)))
                    sT.inc(tensor.matmul(out=BCp[:, 0:512], lhsT=p.ones_row_f[:],
                                         rhs=ISR[0:1, 0:512], start=True, stop=True))

        @blk.scalar
        def _(scalar):
            for h in range(8):
                for qb in range(4):
                    for k in range(16):
                        wait(scalar, (sT, t_s(h, qb, k)))
                        g_ = (h * 4 + qb) * 16 + k
                        if g_ >= 2:
                            i2, k2 = divmod(g_ - 2, 16)
                            wait(scalar, (sT, t_sig(i2 // 4, i2 % 4, k2)))
                        sS.inc(scalar.activation(
                            out=EXPT[:, (k % 2) * 512:(k % 2) * 512 + 512],
                            in_=(STa if k % 2 == 0 else STb)[:, 0:512],
                            func=AF.Exp, bias=NMB[:, h:h + 1]))
                    wait(scalar, (sV, v_sig(h, qb)))
                    if h * 4 + qb > 0:
                        pv = (h, qb - 1) if qb else (h - 1, 3)
                        wait(scalar, (sT, t_bc(*pv)))
                    t1 = sS.inc(scalar.activation(out=RLN[0:1, 0:512], in_=SR[0:1, 0:512],
                                                  func=AF.Ln))
                    wait(scalar, t1)
                    sS.inc(scalar.activation(out=ISR[0:1, 0:512], in_=RLN[0:1, 0:512],
                                             func=AF.Exp, scale=-1.0,
                                             bias=p.cpool[0:1, 1:2]))

        @blk.gpsimd
        def _(gpsimd):
            if dbg_raw_ot == "rows":
                wait(gpsimd, (sV, v0 + 1))
                gpsimd.tensor_copy(out=XF[0:1, 12288:12800], in_=SR[0:1, 0:512])
                wait(gpsimd, (sS, s0 + 18))
                gpsimd.tensor_copy(out=XF[0:1, 12800:13312], in_=RLN[0:1, 0:512])
                sG.inc(gpsimd.tensor_copy(out=XF[0:1, 13312:13824], in_=ISR[0:1, 0:512]))

        @blk.vector
        def _(vector):
            wait(vector, tok_spill)
            for h in range(8):
                for qb in range(4):
                    wait(vector, (sT, t_sig(h, qb, 15)))
                    if h * 4 + qb > 0:
                        pv = (h, qb - 1) if qb else (h - 1, 3)
                        wait(vector, (sS, s_row(*pv, 1)))
                    sV.inc(vector.tensor_scalar_mul(out=SR[0:1, 0:512],
                                                    in0=SIG[0:1, 0:512],
                                                    scalar1=26489122129.84347))
                    wait(vector, (sT, t_bc(h, qb)))
                    tb = sV.inc(vector.tensor_copy(out=BCS[:], in_=BCp[:, 0:512]))
                    wait(vector, tb)
                    if dbg_raw_ot == "bcs":
                        sV.inc(vector.tensor_copy(
                            out=D16[:, h * S + 512 * qb: h * S + 512 * (qb + 1)],
                            in_=BCS[:]))
                    elif dbg_raw_ot:
                        sV.inc(vector.tensor_copy(
                            out=D16[:, h * S + 512 * qb: h * S + 512 * (qb + 1)],
                            in_=OT[:, 512 * qb:512 * (qb + 1)]))
                    else:
                        sV.inc(vector.scalar_tensor_tensor(
                            out=D16[:, h * S + 512 * qb: h * S + 512 * (qb + 1)],
                            in0=OT[:, 512 * qb:512 * (qb + 1)],
                            scalar=1.0, in1=BCS[:], op0=ALU.mult, op1=ALU.mult))
    tok_attn = sV.tok()
    stAT.close()
    if stop_at == "attn":
        st2.pop_all()
        return ("dump16", D16[:, 0:16384], tok_attn)
    if stop_at == "rows":
        st2.pop_all()
        return ("dumpf", XF[0:1, 12288:13824], [tok_attn, sG.tok()])

    stYT = ExitStack()
    PYa = stYT.enter_context(nc.psum_tensor("af_pya", [128, 128], BF16))
    PYb = stYT.enter_context(nc.psum_tensor("af_pyb", [128, 128], BF16))
    tY0, vY0 = sT.n, sV.n
    with p.blockx() as blk:
        @blk.tensor
        def _(tensor):
            wait(tensor, tok_attn)
            for i in range(128):
                h, jb = divmod(i, 16)
                if i >= 2:
                    wait(tensor, (sV, vY0 + (i - 2) + 1))
                sT.inc(tensor.transpose(
                    out=(PYa if i % 2 == 0 else PYb)[:],
                    in_=D16[:, h * S + 128 * jb: h * S + 128 * (jb + 1)],
                    identity=p.IDB[:]))

        @blk.vector
        def _(vector):
            for i in range(128):
                h, jb = divmod(i, 16)
                wait(vector, (sT, tY0 + i + 1))
                sV.inc(vector.tensor_copy(
                    out=B16[:, jb * 1024 + 128 * h: jb * 1024 + 128 * (h + 1)],
                    in_=(PYa if i % 2 == 0 else PYb)[:]))
    tok_y = sV.tok()
    stYT.close()
    if stop_at == "y":
        st2.pop_all()
        return ("dump16", B16[:], tok_y)

    tok_wo = emit_big_load(p, WOB.rearrange("p (c f) -> p c f", c=NCH),
                           wot.rearrange("(c p) f -> p c f", p=128), cast=True,
                           deps=[tok_attn])
    tok_xr = emit_restore(p, X, xsp, [tok_y])
    stRO = ExitStack()
    RO = stRO.enter_context(nc.psum_tensor("af_ro", [128, 512], F32))
    RO2 = stRO.enter_context(nc.psum_tensor("af_ro2", [128, 512], F32))
    tR0, vR0 = sT.n, sV.n
    with p.blockx() as blk:
        @blk.tensor
        def _(tensor):
            wait(tensor, tok_y, tok_wo, tok_xr)
            i = 0
            for half in range(2):
                for cb in range(NCH):
                    for eb in range(2):
                        ROx = RO if i % 2 == 0 else RO2
                        if i >= 2:
                            wait(tensor, (sV, vR0 + (i - 2) + 1))
                        for ncd in range(NCH):
                            mm = tensor.matmul(
                                out=ROx[:, 0:512],
                                lhsT=B16[:, (8 * half + ncd) * 1024 + 128 * cb:
                                         (8 * half + ncd) * 1024 + 128 * (cb + 1)],
                                rhs=WOB[:, ncd * 1024 + 512 * eb: ncd * 1024 + 512 * (eb + 1)],
                                start=(ncd == 0), stop=(ncd == NCH - 1))
                        sT.inc(mm)
                        i += 1

        @blk.vector
        def _(vector):
            i = 0
            for half in range(2):
                for cb in range(NCH):
                    for eb in range(2):
                        wait(vector, (sT, tR0 + i + 1))
                        dst = X[:, cb * S + 1024 * half + 512 * eb:
                                cb * S + 1024 * half + 512 * eb + 512]
                        sV.inc(vector.scalar_tensor_tensor(
                            out=dst, in0=(RO if i % 2 == 0 else RO2)[:, 0:512],
                            scalar=1.0, in1=dst, op0=ALU.mult, op1=ALU.add))
                        i += 1
    tok_done = sV.tok()
    stRO.close()
    st2.close()
    return tok_done


def build_l0attn_dbg(stop_at, dbg_raw_ot=False):
    p = Prog()
    xin = p.din("xin", [1024, S])
    p.init_consts()
    _load_phase_consts(p, need=("idb",))
    X = p.sb("X", [128, NCH * S], F32)
    tok_x = emit_big_load(p, X[:].rearrange("p (c s) -> p c s", c=NCH),
                          xin.rearrange("(c p) s -> p c s", p=128))
    kind, ap, tok = emit_l0_attn(p, X, deps=[tok_x], stop_at=stop_at, dbg_raw_ot=dbg_raw_ot)
    if kind == "dump16":
        dbg = p.dout("dbg", list(ap.shape), BF16)
    else:
        dbg = p.dout("dbg", list(ap.shape), F32)
    ld = p.sem("uld%d" % p._ldctr); p._ldctr += 1
    toks = tok if isinstance(tok, list) else [tok]
    with p.blockx() as blk:
        @blk.gpsimd
        def _(gpsimd):
            wait(gpsimd, *toks)
            ld.inc(gpsimd.dma_start(out=dbg, in_=ap), 16)
    p.load_toks.append(ld.tok())
    emit_final_wait(p, [])
    return p.nc, p


def build_l0attn_prog():
    if "l0attn" in _prog_cache:
        return _prog_cache["l0attn"]
    p = Prog()
    xin = p.din("xin", [1024, S])
    xout = p.dout("xout", [1024, S])
    p.init_consts()
    _load_phase_consts(p, need=("idb",))
    X = p.sb("X", [128, NCH * S], F32)
    tok_x = emit_big_load(p, X[:].rearrange("p (c s) -> p c s", c=NCH),
                          xin.rearrange("(c p) s -> p c s", p=128))
    tok = emit_l0_attn(p, X, deps=[tok_x])
    tok_st = emit_store_xT(p, X, xout, [tok])
    emit_final_wait(p, [tok_st])
    nc = p.close()
    _prog_cache["l0attn"] = (nc, p)
    return nc, p


def _prep_l0attn(inputs):
    d = {}
    d["l0a_w1t"] = np.ascontiguousarray(inputs["L0_attn_W1"].T)
    d["l0a_w2t"] = np.ascontiguousarray(inputs["L0_attn_W2"].T)
    d["l0a_w3t"] = np.ascontiguousarray(inputs["L0_attn_W3"].T)
    d["l0a_wot"] = np.ascontiguousarray(inputs["L0_attn_Wo"].T)
    inv = 1.0 / 10000.0 ** (np.arange(0, 128, 2, dtype=np.float64) / 128.0)
    srow = np.arange(128)
    tvar = np.arange(2)
    pidx = np.arange(1024)
    jlo = pidx // 128
    dloc = pidx % 128
    ii = dloc // 2
    cos = np.zeros((128, 2, 1024), np.float64)
    sina = np.zeros((128, 2, 1024), np.float64)
    for tv in range(2):
        j = 8 * (128 * tv + srow)[:, None] + jlo[None, :]
        th = j * inv[ii][None, :]
        cos[:, tv, :] = np.cos(th)
        sg = np.where(dloc % 2 == 0, -1.0, 1.0)
        sina[:, tv, :] = np.sin(th) * sg[None, :]
    d["l0a_cos"] = cos.reshape(128, 2048).astype(np.float32)
    d["l0a_sina"] = sina.reshape(128, 2048).astype(np.float32)
    return d



def emit_l1_attn(p, X, deps):
    nc = p.nc
    sV, sT, sS, sG = p.sem("v"), p.sem("t"), p.sem("s"), p.sem("g")
    st2 = ExitStack()
    sbuf = lambda n, sh, dt: st2.enter_context(nc.sbuf_tensor(n, sh, dt))

    wqt = p.din("l1c_wqt", [1024, 3072])
    wot = p.din("l1c_wot", [1024, 1024])
    hid = p.din("c_hid16", [128, 128], BF16)
    expd = p.din("c_expd", [16, 1024], BF16)
    xsp = p.dtmp("l1c_xsp", [1024, S])

    LNB = sbuf("cf_lnb", [128, NCH * S], BF16)
    A16 = sbuf("cf_a16", [128, NCH * S], BF16)
    B16 = sbuf("cf_b16", [128, NT * 1024], BF16)
    C16 = sbuf("cf_c16", [128, NCH * S], BF16)
    BKS = sbuf("cf_bks", [128, 256], F32)
    BKC = sbuf("cf_bkc", [128, 264], F32)
    SCNA = sbuf("cf_scna", [128, 32], F32)
    SCNB = sbuf("cf_scnb", [128, 32], F32)
    HID = sbuf("cf_hid", [128, 128], BF16)
    EXPD = sbuf("cf_expd", [16, 1024], BF16)
    CTXR = sbuf("cf_ctxr", [128, 64], F32)
    CTXB = sbuf("cf_ctxb", [128, 64], BF16)
    XB = X[:].bitcast(BF16)
    VTOK = XB
    WTB = XB[:, 16384:24576]
    LNBF = LNB[:].bitcast(F32)
    QIS = LNBF[0:16, 0:2048]
    QISB16 = LNB[0:16, 4096:6144]
    DINVB = LNB[0:16, 6144:8192]
    DINV = LNBF[0:16, 4096:6144]
    DVX = LNB[:, 14336:16384]
    WOT = LNB[:, 0:8192]
    B16F = B16[:].bitcast(F32)
    C16F = C16[:].bitcast(F32)
    ln_scratch = (A16[:, 0:1024], C16F[0:1, 0:6144], B16F[:, 0:2048],
                  B16F[:, 2048:4096], A16[:, 1024:2048])

    tok_ln = emit_layernorm(p, X, LNB, "cf_ln", deps=deps, scratch=ln_scratch)
    tok_spill = emit_spill(p, X, xsp, [tok_ln])
    tok_hid = emit_big_load(p, HID[:], hid, deps=deps)
    tok_expd = emit_big_load(p, EXPD[:], expd, deps=deps)

    stPT = ExitStack()
    PT = stPT.enter_context(nc.psum_tensor("cf_pt", [128, 512], F32))
    PT2 = stPT.enter_context(nc.psum_tensor("cf_pt2", [128, 512], F32))

    def project(col_off, dst_sl, func, prior_tok, featmaj):
        tok_w = emit_big_load(p, WTB.rearrange("p (c f) -> p c f", c=NCH),
                              wqt[:, col_off:col_off + 1024].rearrange("(c p) f -> p c f", p=128),
                              cast=True, deps=[tok_spill] + ([prior_tok] if prior_tok else []))
        t0, s0 = sT.n, sS.n
        with p.blockx() as blk:
            @blk.tensor
            def _(tensor):
                wait(tensor, tok_ln, tok_w)
                for i in range(32):
                    PTx = PT if i % 2 == 0 else PT2
                    if i >= 2:
                        wait(tensor, (sS, s0 + (i - 2) + 1))
                    if featmaj:
                        oc, sb = divmod(i, 4)
                        for ec in range(NCH):
                            mm = tensor.matmul(
                                out=PTx[:, 0:512],
                                lhsT=WTB[:, ec * 1024 + 128 * oc: ec * 1024 + 128 * (oc + 1)],
                                rhs=LNB[:, ec * S + 512 * sb: ec * S + 512 * (sb + 1)],
                                start=(ec == 0), stop=(ec == NCH - 1))
                    else:
                        t, half = divmod(i, 2)
                        for ec in range(NCH):
                            mm = tensor.matmul(
                                out=PTx[:, 0:512],
                                lhsT=LNB[:, ec * S + 128 * t: ec * S + 128 * (t + 1)],
                                rhs=WTB[:, ec * 1024 + 512 * half: ec * 1024 + 512 * (half + 1)],
                                start=(ec == 0), stop=(ec == NCH - 1))
                    sT.inc(mm)

            @blk.scalar
            def _(scalar):
                for i in range(32):
                    wait(scalar, (sT, t0 + i + 1))
                    if featmaj:
                        oc, sb = divmod(i, 4)
                        dst = dst_sl[:, oc * S + 512 * sb: oc * S + 512 * (sb + 1)]
                    else:
                        t, half = divmod(i, 2)
                        dst = dst_sl[:, t * 1024 + 512 * half: t * 1024 + 512 * (half + 1)]
                    sS.inc(scalar.activation(out=dst, in_=(PT if i % 2 == 0 else PT2)[:, 0:512],
                                             func=func))
        return sS.tok()

    tok_q = project(0, A16[:], AF.Exp, None, True)
    tok_k = project(1024, C16[:], AF.Exp, tok_q, True)
    tok_v = project(2048, VTOK, AF.Identity, tok_k, False)

    stQS = ExitStack()
    QSUMP = stQS.enter_context(nc.psum_tensor("cf_qsum", [16, S], F32))
    t1, v1, s1 = sT.n, sV.n, sS.n
    with p.blockx() as blk:
        @blk.tensor
        def _(tensor):
            wait(tensor, tok_q, tok_hid)
            for c in range(NCH):
                for sb in range(4):
                    mm = tensor.matmul(out=QSUMP[0:16, 512 * sb:512 * (sb + 1)],
                                       lhsT=HID[:, 16 * c:16 * (c + 1)],
                                       rhs=A16[:, c * S + 512 * sb: c * S + 512 * (sb + 1)],
                                       start=(c == 0), stop=(c == NCH - 1))
            sT.inc(mm)

        @blk.scalar
        def _(scalar):
            wait(scalar, (sT, t1 + 1))
            tq = sS.inc(scalar.activation(out=QIS, in_=QSUMP[0:16, :], func=AF.Ln))
            wait(scalar, tq)
            sS.inc(scalar.activation(out=QIS, in_=QIS, func=AF.Exp, scale=-1.0,
                                     bias=p.cpool[0:16, 2:3]))

        @blk.vector
        def _(vector):
            wait(vector, (sS, s1 + 2))
            sV.inc(vector.tensor_copy(out=QISB16, in_=QIS))
    tok_qis = sV.tok()
    stQS.close()

    stQX = ExitStack()
    PXQ = stQX.enter_context(nc.psum_tensor("cf_pxq", [128, 512], F32))
    t2, v2 = sT.n, sV.n
    with p.blockx() as blk:
        @blk.tensor
        def _(tensor):
            wait(tensor, tok_qis, tok_expd)
            for i in range(32):
                c, sb = divmod(i, 4)
                if i >= 1:
                    wait(tensor, (sV, v2 + i))
                sT.inc(tensor.matmul(out=PXQ[:, 0:512],
                                     lhsT=EXPD[0:16, 128 * c:128 * (c + 1)],
                                     rhs=QISB16[0:16, 512 * sb:512 * (sb + 1)],
                                     start=True, stop=True))

        @blk.vector
        def _(vector):
            for i in range(32):
                c, sb = divmod(i, 4)
                wait(vector, (sT, t2 + i + 1))
                sl = A16[:, c * S + 512 * sb: c * S + 512 * (sb + 1)]
                sV.inc(vector.tensor_tensor(out=sl, in0=sl, in1=PXQ[:, 0:512],
                                            op=ALU.mult))
    tok_qsm = sV.tok()
    stQX.close()

    v3 = sV.n
    with p.blockx() as blk:
        @blk.vector
        def _(vector):
            wait(vector, tok_k)
            for c in range(NCH):
                tr = sV.inc(vector.tensor_reduce(
                    out=BKS[:, 32 * c:32 * (c + 1)],
                    in_=C16[:, c * S:(c + 1) * S].rearrange("p (n s) -> p n s", s=64),
                    axis=AX.X, op=ALU.add))
                wait(vector, tr)
                tz = sV.inc(vector.memset(BKC[:, 33 * c:33 * c + 1], 0.0))
                wait(vector, tz)
                sa_, sb_ = SCNA, SCNB
                tc = sV.inc(vector.tensor_copy(out=sa_[:], in_=BKS[:, 32 * c:32 * (c + 1)]))
                wait(vector, tc)
                for k in [1, 2, 4, 8, 16]:
                    ta = sV.inc(vector.tensor_copy(out=sb_[:, 0:k], in_=sa_[:, 0:k]))
                    wait(vector, ta)
                    tb = sV.inc(vector.tensor_add(out=sb_[:, k:32], in0=sa_[:, k:32],
                                                  in1=sa_[:, 0:32 - k]))
                    wait(vector, tb)
                    sa_, sb_ = sb_, sa_
                tf = sV.inc(vector.tensor_copy(out=BKC[:, 33 * c + 1:33 * c + 33], in_=sa_[:]))
                wait(vector, tf)
    tok_bkc = sV.tok()

    stKT = ExitStack()
    PTRa = stKT.enter_context(nc.psum_tensor("cf_ptra", [128, 128], BF16))
    PTRb = stKT.enter_context(nc.psum_tensor("cf_ptrb", [128, 128], BF16))
    t4, v4 = sT.n, sV.n
    with p.blockx() as blk:
        @blk.tensor
        def _(tensor):
            wait(tensor, tok_k, p.tok_idb, tok_qsm)
            for i in range(128):
                c, t = divmod(i, 16)
                if i >= 2:
                    wait(tensor, (sV, v4 + (i - 2) + 1))
                sT.inc(tensor.transpose(
                    out=(PTRa if i % 2 == 0 else PTRb)[:],
                    in_=C16[:, c * S + 128 * t: c * S + 128 * (t + 1)],
                    identity=p.IDB[:]))

        @blk.vector
        def _(vector):
            for i in range(128):
                c, t = divmod(i, 16)
                wait(vector, (sT, t4 + i + 1))
                sV.inc(vector.tensor_copy(
                    out=B16[:, t * 1024 + 128 * c: t * 1024 + 128 * (c + 1)],
                    in_=(PTRa if i % 2 == 0 else PTRb)[:]))
    tok_ktok = sV.tok()
    stKT.close()

    DP16 = LNB[:, 12288:14336]
    stDN = ExitStack()
    PDEN = stDN.enter_context(nc.psum_tensor("cf_pden", [16, S], F32))
    t5, v5, s5, g5 = sT.n, sV.n, sS.n, sG.n
    with p.blockx() as blk:
        @blk.gpsimd
        def _(gpsimd):
            wait(gpsimd, tok_qsm, tok_bkc, tok_ktok)
            for c in range(NCH):
                if c >= 1:
                    wait(gpsimd, (sT, t5 + c))
                tg = sG.inc(gpsimd.tensor_tensor(
                    out=DP16.rearrange("p (n s) -> p n s", s=64),
                    in0=A16[:, c * S:(c + 1) * S].rearrange("p (n s) -> p n s", s=64),
                    in1=bass.AP(BKC, 33 * c + 1, [[264, 128], [1, 32], [0, 64]]),
                    op=ALU.mult))

        @blk.tensor
        def _(tensor):
            wait(tensor, tok_hid)
            for c in range(NCH):
                wait(tensor, (sG, g5 + c + 1))
                for sb in range(4):
                    mm = tensor.matmul(out=PDEN[0:16, 512 * sb:512 * (sb + 1)],
                                       lhsT=HID[:, 16 * c:16 * (c + 1)],
                                       rhs=DP16[:, 512 * sb:512 * (sb + 1)],
                                       start=(c == 0), stop=(c == NCH - 1))
                sT.inc(mm)

        @blk.vector
        def _(vector):
            wait(vector, (sT, t5 + NCH))
            sV.inc(vector.tensor_scalar(out=DINV, in0=PDEN[0:16, :],
                                        scalar1=1e-6, scalar2=float(math.exp(-10.0)),
                                        op0=ALU.max, op1=ALU.mult))

        @blk.scalar
        def _(scalar):
            wait(scalar, (sV, v5 + 1))
            tl = sS.inc(scalar.activation(out=DINV, in_=DINV, func=AF.Ln))
            wait(scalar, tl)
            sS.inc(scalar.activation(out=DINV, in_=DINV, func=AF.Exp, scale=-1.0,
                                     bias=p.cpool[0:16, 3:4]))
    v6 = sV.n
    with p.blockx() as blk:
        @blk.vector
        def _(vector):
            wait(vector, (sS, s5 + 2))
            sV.inc(vector.tensor_copy(out=DINVB, in_=DINV))
    tok_dinv = sV.tok()
    stDN.close()

    stCT = ExitStack()
    PDV = stCT.enter_context(nc.psum_tensor("cf_pdv", [128, 512], F32))
    CTX = stCT.enter_context(nc.psum_tensor("cf_ctx", [64, 64], F32))
    ATa = stCT.enter_context(nc.psum_tensor("cf_ata", [64, 64], F32))
    ATb = stCT.enter_context(nc.psum_tensor("cf_atb", [64, 64], F32))
    XO = C16
    for c in range(NCH):
        t7, v7 = sT.n, sV.n
        with p.blockx() as blk:
            @blk.tensor
            def _(tensor):
                wait(tensor, tok_dinv, tok_expd)
                if c == 0:
                    wait(tensor, tok_ktok, tok_v)
                for sb in range(4):
                    if sb >= 1:
                        wait(tensor, (sV, v7 + sb))
                    sT.inc(tensor.matmul(out=PDV[:, 0:512],
                                         lhsT=EXPD[0:16, 128 * c:128 * (c + 1)],
                                         rhs=DINVB[0:16, 512 * sb:512 * (sb + 1)],
                                         start=True, stop=True))

            @blk.vector
            def _(vector):
                for sb in range(4):
                    wait(vector, (sT, t7 + sb + 1))
                    sV.inc(vector.tensor_copy(out=DVX[:, 512 * sb:512 * (sb + 1)],
                                              in_=PDV[:, 0:512]))
        tok_dvx = sV.tok()
        for hh in range(2):
            h = 2 * c + hh
            rb = 64 * hh
            tH, vH = sT.n, sV.n
            with p.blockx() as blk:
                @blk.tensor
                def _(tensor):
                    wait(tensor, tok_dvx, (sV, vH + 2))

                    def dctx(n):
                        tile, par = divmod(n, 2)
                        return sT.inc(tensor.matmul(
                            out=CTX[:],
                            lhsT=B16[64 * par:64 * par + 64,
                                     tile * 1024 + 64 * h: tile * 1024 + 64 * (h + 1)],
                            rhs=VTOK[64 * par:64 * par + 64,
                                     tile * 1024 + 64 * h: tile * 1024 + 64 * (h + 1)],
                            start=True, stop=True))
                    dctx(0)
                    for n in range(1, 32):
                        wait(tensor, (sV, vH + 3 * n + 1))
                        sT.inc(tensor.matmul(
                            out=(ATa if n % 2 == 0 else ATb)[:],
                            lhsT=CTXB[rb:rb + 64, :],
                            rhs=A16[rb:rb + 64, c * S + 64 * n: c * S + 64 * (n + 1)],
                            start=True, stop=True))
                        if n <= 30:
                            wait(tensor, (sV, vH + 3 * n))
                            dctx(n)

                @blk.vector
                def _(vector):
                    sV.inc(vector.memset(XO[rb:rb + 64, c * S: c * S + 64], 0.0))
                    sV.inc(vector.memset(CTXR[rb:rb + 64, :], 0.0))
                    for n in range(1, 32):
                        wait(vector, (sT, tH + (1 if n == 1 else 2 * (n - 1) + 1)))
                        ta = sV.inc(vector.tensor_add(out=CTXR[rb:rb + 64, :],
                                                      in0=CTX[:],
                                                      in1=CTXR[rb:rb + 64, :]))
                        wait(vector, ta)
                        sV.inc(vector.tensor_copy(out=CTXB[rb:rb + 64, :],
                                                  in_=CTXR[rb:rb + 64, :]))
                        wait(vector, (sT, tH + 2 * n))
                        sV.inc(vector.scalar_tensor_tensor(
                            out=XO[rb:rb + 64, c * S + 64 * n: c * S + 64 * (n + 1)],
                            in0=(ATa if n % 2 == 0 else ATb)[:],
                            scalar=1.0,
                            in1=DVX[rb:rb + 64, 64 * n:64 * (n + 1)],
                            op0=ALU.mult, op1=ALU.mult))
    tok_ctx = sV.tok()
    stCT.close()

    tok_wo = emit_big_load(p, WOT.rearrange("p (c f) -> p c f", c=NCH),
                           wot.rearrange("(c p) f -> p c f", p=128), cast=True,
                           deps=[tok_ctx])
    tok_xr = emit_restore(p, X, xsp, [tok_ctx])
    stRO = ExitStack()
    RO = stRO.enter_context(nc.psum_tensor("cf_ro", [128, 512], F32))
    RO2 = stRO.enter_context(nc.psum_tensor("cf_ro2", [128, 512], F32))
    tR0, vR0 = sT.n, sV.n
    with p.blockx() as blk:
        @blk.tensor
        def _(tensor):
            wait(tensor, tok_wo, tok_xr)
            for i in range(32):
                oc, sb = divmod(i, 4)
                ROx = RO if i % 2 == 0 else RO2
                if i >= 2:
                    wait(tensor, (sV, vR0 + (i - 2) + 1))
                for pc in range(NCH):
                    mm = tensor.matmul(
                        out=ROx[:, 0:512],
                        lhsT=WOT[:, pc * 1024 + 128 * oc: pc * 1024 + 128 * (oc + 1)],
                        rhs=XO[:, pc * S + 512 * sb: pc * S + 512 * (sb + 1)],
                        start=(pc == 0), stop=(pc == NCH - 1))
                sT.inc(mm)

        @blk.vector
        def _(vector):
            for i in range(32):
                oc, sb = divmod(i, 4)
                wait(vector, (sT, tR0 + i + 1))
                dst = X[:, oc * S + 512 * sb: oc * S + 512 * (sb + 1)]
                sV.inc(vector.scalar_tensor_tensor(
                    out=dst, in0=(RO if i % 2 == 0 else RO2)[:, 0:512],
                    scalar=1.0, in1=dst, op0=ALU.mult, op1=ALU.add))
    tok_done = sV.tok()
    stRO.close()
    stPT.close()
    st2.close()
    return tok_done


def build_l1attn_prog():
    if "l1attn" in _prog_cache:
        return _prog_cache["l1attn"]
    p = Prog()
    xin = p.din("xin", [1024, S])
    xout = p.dout("xout", [1024, S])
    p.init_consts()
    _load_phase_consts(p, need=("idb",))
    X = p.sb("X", [128, NCH * S], F32)
    tok_x = emit_big_load(p, X[:].rearrange("p (c s) -> p c s", c=NCH),
                          xin.rearrange("(c p) s -> p c s", p=128))
    tok = emit_l1_attn(p, X, deps=[tok_x])
    tok_st = emit_store_xT(p, X, xout, [tok])
    emit_final_wait(p, [tok_st])
    nc = p.close()
    _prog_cache["l1attn"] = (nc, p)
    return nc, p


def _prep_l1attn(inputs):
    d = {}
    d["l1c_wqt"] = np.ascontiguousarray(inputs["L1_attn_W_qkv"].T)
    d["l1c_wot"] = np.ascontiguousarray(inputs["L1_attn_W_out"].T)
    hid = np.zeros((128, 8, 16), np.float32)
    expd = np.zeros((16, 8, 128), np.float32)
    for c in range(8):
        for pp in range(128):
            hid[pp, c, 2 * c + pp // 64] = 1.0
            expd[2 * c + pp // 64, c, pp] = 1.0
    d["c_hid16"] = hid.reshape(128, 128).astype(BF)
    d["c_expd"] = expd.reshape(16, 1024).astype(BF)
    return d



def spill_flat(p, buf_ap, dram_ap, deps):
    ld = p.sem("uld%d" % p._ldctr); p._ldctr += 1
    with p.blockx() as blk:
        @blk.gpsimd
        def _(gpsimd):
            wait(gpsimd, *deps)
            ld.inc(gpsimd.dma_start(out=dram_ap, in_=buf_ap), 16)
    p.load_toks.append(ld.tok())
    return ld.tok()


def restore_flat(p, buf_ap, dram_ap, deps):
    ld = p.sem("uld%d" % p._ldctr); p._ldctr += 1
    with p.blockx() as blk:
        @blk.sync
        def _(sync):
            wait(sync, *deps)
            ld.inc(sync.dma_start(out=buf_ap, in_=dram_ap), 16)
    p.load_toks.append(ld.tok())
    return ld.tok()


def emit_moe_conv(p):
    w1 = p.din("l1f_w1", [8, 1024, 4096])
    w2 = p.din("l1f_w2", [8, 4096, 1024])
    w1b = p.dtmp("l1f_w1b", [8, 1024, 4096], BF16)
    w2b = p.dtmp("l1f_w2b", [8, 4096, 1024], BF16)
    cv = p.sem("moeconv")
    with p.blockx() as blk:
        @blk.gpsimd
        def _(gpsimd):
            for e in range(8):
                cv.inc(gpsimd.dma_start(out=w1b[e:e + 1], in_=w1[e:e + 1]), 16)
                cv.inc(gpsimd.dma_start(out=w2b[e:e + 1], in_=w2[e:e + 1]), 16)
    tok = cv.tok()
    p.load_toks.append(tok)
    return w1b, w2b, tok


def emit_l1_ffn(p, X, deps, y_out, moe_conv=None):
    nc = p.nc
    sV, sT, sS, sG = p.sem("v"), p.sem("t"), p.sem("s"), p.sem("g")
    st2 = ExitStack()
    sbuf = lambda n, sh, dt: st2.enter_context(nc.sbuf_tensor(n, sh, dt))

    wqkv = p.din("l1f_wqkv", [1024, 3072])
    wo = p.din("l1f_wo", [1024, 1024])
    ln1w = p.din("l1f_ln1w", [128, 8])
    ln1b = p.din("l1f_ln1b", [128, 8])
    ln2w = p.din("l1f_ln2w", [128, 8])
    ln2b = p.din("l1f_ln2b", [128, 8])
    bqkv = p.din("l1f_bqkv", [128, 24])
    bqv = p.din("l1f_bqv", [1, 1024])
    bo = p.din("l1f_bo", [128, 8])
    wg = p.din("l1f_wgate", [128, 64])
    b1c = p.din("l1f_b1c", [128, 256])
    b2r = p.din("l1f_b2r", [1, 8192])
    if moe_conv is None:
        moe_conv = emit_moe_conv(p)
    w1b, w2b, tok_conv = moe_conv
    xsp = p.dtmp("l1f_xsp", [128, NCH * S])
    tsp = p.dtmp("l1f_tsp", [128, NCH * S], BF16)
    hsp = p.dtmp("l1f_hsp", [128, NCH * S], BF16)

    LNB = sbuf("df_lnb", [128, NCH * S], BF16)
    LN1B = sbuf("df_ln1b", [128, NCH * S], BF16)
    A16 = sbuf("df_a16", [128, NCH * S], BF16)
    B16 = sbuf("df_b16", [128, NCH * S], BF16)
    LNP = sbuf("df_lnp", [128, 32], F32)
    BQ = sbuf("df_bq", [128, 32], F32)
    BO = sbuf("df_bo", [128, 8], F32)
    WG16 = sbuf("df_wg", [128, 64], BF16)
    WE = sbuf("df_we", [128, 128], F32)
    GT = sbuf("df_gt", [128, 128], F32)
    GM = sbuf("df_gm", [128, 48], F32)
    GMK = sbuf("df_gmk", [128, 128], F32)
    B1C = sbuf("df_b1c", [128, 256], F32)
    B2RE = sbuf("df_b2re", [1, 1024], BF16)
    TMX = sbuf("df_tmx", [128, 65], F32)

    XB = X[:].bitcast(BF16)
    XF = X[:]
    VTOK = XB
    WQ = XB[:, 16384:24576]
    BQVR = XB[0:1, 28672:29696]
    W2G = [XB[:, 16384 + 4096 * j:16384 + 4096 * (j + 1)] for j in range(3)]
    CTXF = LNB[:]
    LNBF32 = LNB[:].bitcast(F32)
    L1F = LN1B[:].bitcast(F32)
    L1B = LN1B[:]
    EXPT = L1B[:, 0:1024]
    BCS = L1F[:, 512:1024]
    SR = L1F[0:1, 1024:1536]
    RLN = L1F[0:1, 1536:2048]
    ISR = L1F[0:1, 2048:2560]
    NMB16 = L1F[:, 2592:2608]
    MROW = L1F[0:1, 2608:2624]
    WOTMP = L1B[:, 4096:5120]
    T1 = LN1B[:]
    HBS = LNB[:]
    ACC = XB[:, 0:16384]
    HR = LNB[:]
    A16F = A16[:].bitcast(F32)
    B16F = B16[:].bitcast(F32)
    OUTSa = L1F
    OUTSb = LNBF32

    tok_l1w = emit_big_load(p, LNP[:, 0:8], ln1w, deps=deps)
    tok_l1b = emit_big_load(p, LNP[:, 8:16], ln1b, deps=deps)
    tok_l2w = emit_big_load(p, LNP[:, 16:24], ln2w, deps=deps)
    tok_l2b = emit_big_load(p, LNP[:, 24:32], ln2b, deps=deps)
    tok_bq = emit_big_load(p, BQ[:, 0:24], bqkv, deps=deps)
    tok_bqv = None
    tok_bo = emit_big_load(p, BO[:], bo, deps=deps)
    tok_wg = emit_big_load(p, WG16[:], wg, cast=True, deps=deps)
    tok_b1c = emit_big_load(p, B1C[:], b1c, deps=deps)

    ln_scr1 = (A16[:, 0:1024], B16F[0:1, 0:6144], A16F[:, 1024:3072],
               A16F[:, 3072:5120], A16[:, 10240:11264])
    tok_t0 = emit_layernorm(p, X, LNB, "df_ln3", deps=deps, scratch=ln_scr1)
    tok_xspill = spill_flat(p, XF, xsp, [tok_t0])
    tok_bqv = emit_big_load(p, BQVR, bqv, cast=True, deps=[tok_xspill])
    tok_ln1 = emit_layernorm(p, LNB, LN1B, "df_ln1",
                             deps=[tok_t0, tok_l1w, tok_l1b],
                             affine=(LNP[:, 0:8], LNP[:, 8:16]), scratch=ln_scr1)
    tok_tspill = spill_flat(p, LNB[:], tsp, [tok_ln1])

    with p.blockx() as blk:
        @blk.vector
        def _(vector):
            wait(vector, tok_bq)
            sV.inc(vector.tensor_scalar_mul(out=BQ[:, 24:32], in0=BQ[:, 0:8],
                                            scalar1=0.125))
    tok_bqs = sV.tok()

    stPT = ExitStack()
    PT = stPT.enter_context(nc.psum_tensor("df_pt", [128, 512], F32))
    PT2 = stPT.enter_context(nc.psum_tensor("df_pt2", [128, 512], F32))

    def project(col_off, kind, prior_tok):
        tok_w = emit_big_load(p, WQ.rearrange("p (c f) -> p c f", c=NCH),
                              wqkv[:, col_off:col_off + 1024].rearrange("(c p) f -> p c f", p=128),
                              cast=True, deps=[tok_xspill] + ([prior_tok] if prior_tok else []))
        t0_, s0_ = sT.n, sS.n
        with p.blockx() as blk:
            @blk.tensor
            def _(tensor):
                wait(tensor, tok_ln1, tok_w, tok_bqs, tok_bqv, p.consts_ready)
                for i in range(32):
                    PTx = PT if i % 2 == 0 else PT2
                    if i >= 2:
                        wait(tensor, (sS, s0_ + (i - 2) + 1))
                    if kind != "v":
                        oc, sb = divmod(i, 4)
                        for ec in range(NCH):
                            mm = tensor.matmul(
                                out=PTx[:, 0:512],
                                lhsT=WQ[:, ec * 1024 + 128 * oc: ec * 1024 + 128 * (oc + 1)],
                                rhs=LN1B[:, ec * S + 512 * sb: ec * S + 512 * (sb + 1)],
                                start=(ec == 0), stop=(ec == NCH - 1))
                        sT.inc(mm)
                    else:
                        t, half = divmod(i, 2)
                        for ec in range(NCH):
                            tensor.matmul(
                                out=PTx[:, 0:512],
                                lhsT=LN1B[:, ec * S + 128 * t: ec * S + 128 * (t + 1)],
                                rhs=WQ[:, ec * 1024 + 512 * half: ec * 1024 + 512 * (half + 1)],
                                start=(ec == 0), stop=False)
                        mm = tensor.matmul(
                            out=PTx[:, 0:512],
                            lhsT=p.ones_row_bf16[0:1, 0:128],
                            rhs=BQVR[0:1, 512 * half:512 * (half + 1)],
                            start=False, stop=True)
                        sT.inc(mm)

            @blk.scalar
            def _(scalar):
                for i in range(32):
                    wait(scalar, (sT, t0_ + i + 1))
                    src_ = (PT if i % 2 == 0 else PT2)[:, 0:512]
                    if kind == "q":
                        oc, sb = divmod(i, 4)
                        sS.inc(scalar.activation(
                            out=A16[:, oc * S + 512 * sb: oc * S + 512 * (sb + 1)],
                            in_=src_, func=AF.Identity, scale=0.125,
                            bias=BQ[:, 24 + oc:25 + oc]))
                    elif kind == "k":
                        oc, sb = divmod(i, 4)
                        sS.inc(scalar.activation(
                            out=B16[:, oc * S + 512 * sb: oc * S + 512 * (sb + 1)],
                            in_=src_, func=AF.Identity, bias=BQ[:, 8 + oc:9 + oc]))
                    else:
                        t, half = divmod(i, 2)
                        sS.inc(scalar.activation(
                            out=VTOK[:, t * 1024 + 512 * half: t * 1024 + 512 * (half + 1)],
                            in_=src_, func=AF.Identity))
        return sS.tok()

    with nc.named_scope("d_qkv"):
        tok_q = project(0, "q", None)
        tok_k = project(1024, "k", tok_q)
        tok_v = project(2048, "v", tok_k)
    stPT.close()

    stMX = ExitStack()
    _scope_mx = nc.enter_named_scope("d_max", False)
    STMa = stMX.enter_context(nc.psum_tensor("df_stma", [128, 512], F32))
    STMb = stMX.enter_context(nc.psum_tensor("df_stmb", [128, 512], F32))
    PM1 = stMX.enter_context(nc.psum_tensor("df_pm1", [1, 128], F32))
    PNB = stMX.enter_context(nc.psum_tensor("df_pnb", [128, 16], F32))
    for h in range(16):
        c, rb = h // 2, 64 * (h % 2)
        tH, vH = sT.n, sV.n
        with p.blockx() as blk:
            @blk.tensor
            def _(tensor):
                if h == 0:
                    wait(tensor, tok_q, tok_k, tok_v, p.tok_idf)
                else:
                    wait(tensor, (sV, vH))
                for i in range(64):
                    qb, kk = divmod(i, 16)
                    if i >= 2:
                        wait(tensor, (sV, vH + (i - 2) + 1))
                    sT.inc(tensor.matmul(
                        out=(STMa if i % 2 == 0 else STMb)[:, 0:512],
                        lhsT=B16[rb:rb + 64, c * S + 128 * kk: c * S + 128 * (kk + 1)],
                        rhs=A16[rb:rb + 64, c * S + 512 * qb: c * S + 512 * (qb + 1)],
                        start=True, stop=True))

            @blk.vector
            def _(vector):
                wait(vector, (sT, tH))
                for i in range(64):
                    wait(vector, (sT, tH + i + 1))
                    sV.inc(vector.tensor_reduce(
                        out=TMX[:, i:i + 1],
                        in_=(STMa if i % 2 == 0 else STMb)[:, 0:512],
                        axis=AX.X, op=ALU.max))
        v2_ = sV.n
        with p.blockx() as blk:
            @blk.vector
            def _(vector):
                wait(vector, (sV, v2_))
                sV.inc(vector.tensor_reduce(out=TMX[:, 64:65], in_=TMX[:, 0:64],
                                            axis=AX.X, op=ALU.max))

            @blk.tensor
            def _(tensor):
                wait(tensor, (sV, v2_ + 1))
                sT.inc(tensor.transpose(out=PM1[0:1, :], in_=TMX[:, 64:65],
                                        identity=p.IDF[:]))
        v3_, t3_ = sV.n, sT.n
        with p.blockx() as blk:
            @blk.vector
            def _(vector):
                wait(vector, (sT, t3_))
                sV.inc(vector.tensor_reduce(out=MROW[0:1, h:h + 1], in_=PM1[0:1, :],
                                            axis=AX.X, op=ALU.max, negate=True))

            @blk.tensor
            def _(tensor):
                wait(tensor, (sV, v3_ + 1))
                sT.inc(tensor.matmul(out=PNB[:, h:h + 1], lhsT=p.ones_row_f[:],
                                     rhs=MROW[0:1, h:h + 1], start=True, stop=True))
    tN_ = sT.n
    with p.blockx() as blk:
        @blk.scalar
        def _(scalar):
            wait(scalar, (sT, tN_))
            sS.inc(scalar.activation(out=NMB16, in_=PNB[:], func=AF.Copy))
    tok_nmb = sS.tok()
    stMX.close()
    nc.leave_named_scope("d_max", _scope_mx[0], False)

    stAT = ExitStack()
    _sc = nc.enter_named_scope("d_attnB", False)
    OT = stAT.enter_context(nc.psum_tensor("df_ot", [64, S], F32))
    STa = stAT.enter_context(nc.psum_tensor("df_sta", [128, 512], F32))
    STb = stAT.enter_context(nc.psum_tensor("df_stb", [128, 512], F32))
    SIG = stAT.enter_context(nc.psum_tensor("df_sig", [1, 512], F32))
    BCp = stAT.enter_context(nc.psum_tensor("df_bcp", [128, 512], F32))
    t0a, s0a, v0a = sT.n, sS.n, sV.n

    def t_s(h, qb, k):
        return t0a + (h * 4 + qb) * 33 + (1 if k == 0 else 2 * k)

    def t_sig(h, qb, k):
        return t0a + (h * 4 + qb) * 33 + (2 * k + 3 if k < 15 else 32)

    def t_bc(h, qb):
        return t0a + (h * 4 + qb) * 33 + 33

    def s_exp(h, qb, k):
        return s0a + (h * 4 + qb) * 18 + k + 1

    def s_row(h, qb, j):
        return s0a + (h * 4 + qb) * 18 + 16 + j

    def v_sig(h, qb):
        return v0a + (h * 4 + qb) * 3 + 1

    def v_bcs(h, qb):
        return v0a + (h * 4 + qb) * 3 + 2

    def v_ot(h, qb):
        return v0a + (h * 4 + qb) * 3 + 3

    with p.blockx() as blk:
        @blk.tensor
        def _(tensor):
            wait(tensor, tok_nmb, tok_v, tok_tspill)
            for h in range(16):
                c, rb = h // 2, 64 * (h % 2)
                for qb in range(4):
                    idx = h * 4 + qb
                    if idx > 0:
                        pv = (h, qb - 1) if qb else (h - 1, 3)
                        wait(tensor, (sV, v_sig(*pv)))
                    if qb == 0 and h > 0:
                        wait(tensor, (sV, v_ot(h - 1, 3)))
                    for k in range(17):
                        if k < 16:
                            g_ = idx * 16 + k
                            if g_ >= 2:
                                i2, k2 = divmod(g_ - 2, 16)
                                wait(tensor, (sS, s_exp(i2 // 4, i2 % 4, k2)))
                            sT.inc(tensor.matmul(
                                out=(STa if k % 2 == 0 else STb)[:, 0:512],
                                lhsT=B16[rb:rb + 64, c * S + 128 * k: c * S + 128 * (k + 1)],
                                rhs=A16[rb:rb + 64, c * S + 512 * qb: c * S + 512 * (qb + 1)],
                                start=True, stop=True))
                        if k >= 1:
                            kk = k - 1
                            wait(tensor, (sS, s_exp(h, qb, kk)))
                            tensor.matmul(
                                out=OT[0:64, 512 * qb:512 * (qb + 1)],
                                lhsT=VTOK[:, kk * 1024 + 64 * h: kk * 1024 + 64 * (h + 1)],
                                rhs=EXPT[:, (kk % 2) * 512:(kk % 2) * 512 + 512],
                                start=(kk == 0), stop=(kk == 15))
                            sT.inc(tensor.matmul(
                                out=SIG[0:1, 0:512],
                                lhsT=p.ones_bf[:],
                                rhs=EXPT[:, (kk % 2) * 512:(kk % 2) * 512 + 512],
                                start=(kk == 0), stop=(kk == 15)))
                    wait(tensor, (sS, s_row(h, qb, 2)))
                    if idx > 0:
                        pv = (h, qb - 1) if qb else (h - 1, 3)
                        wait(tensor, (sV, v_bcs(*pv)))
                    sT.inc(tensor.matmul(out=BCp[:, 0:512], lhsT=p.ones_row_f[:],
                                         rhs=ISR[0:1, 0:512], start=True, stop=True))

        @blk.scalar
        def _(scalar):
            for h in range(16):
                for qb in range(4):
                    for k in range(16):
                        wait(scalar, (sT, t_s(h, qb, k)))
                        g_ = (h * 4 + qb) * 16 + k
                        if g_ >= 2:
                            i2, k2 = divmod(g_ - 2, 16)
                            wait(scalar, (sT, t_sig(i2 // 4, i2 % 4, k2)))
                        sS.inc(scalar.activation(
                            out=EXPT[:, (k % 2) * 512:(k % 2) * 512 + 512],
                            in_=(STa if k % 2 == 0 else STb)[:, 0:512],
                            func=AF.Exp, bias=NMB16[:, h:h + 1]))
                    wait(scalar, (sV, v_sig(h, qb)))
                    if h * 4 + qb > 0:
                        pv = (h, qb - 1) if qb else (h - 1, 3)
                        wait(scalar, (sT, t_bc(*pv)))
                    t1 = sS.inc(scalar.activation(out=RLN[0:1, 0:512], in_=SR[0:1, 0:512],
                                                  func=AF.Ln))
                    wait(scalar, t1)
                    sS.inc(scalar.activation(out=ISR[0:1, 0:512], in_=RLN[0:1, 0:512],
                                             func=AF.Exp, scale=-1.0,
                                             bias=p.cpool[0:1, 1:2]))

        @blk.vector
        def _(vector):
            for h in range(16):
                c, rb = h // 2, 64 * (h % 2)
                for qb in range(4):
                    wait(vector, (sT, t_sig(h, qb, 15)))
                    if h * 4 + qb > 0:
                        pv = (h, qb - 1) if qb else (h - 1, 3)
                        wait(vector, (sS, s_row(*pv, 1)))
                    sV.inc(vector.tensor_scalar_mul(out=SR[0:1, 0:512],
                                                    in0=SIG[0:1, 0:512],
                                                    scalar1=float(math.exp(24.0))))
                    wait(vector, (sT, t_bc(h, qb)))
                    tb = sV.inc(vector.tensor_copy(out=BCS[:, 0:512], in_=BCp[:, 0:512]))
                    wait(vector, tb)
                    sV.inc(vector.scalar_tensor_tensor(
                        out=CTXF[rb:rb + 64, c * S + 512 * qb: c * S + 512 * (qb + 1)],
                        in0=OT[0:64, 512 * qb:512 * (qb + 1)],
                        scalar=1.0, in1=BCS[rb:rb + 64, 0:512],
                        op0=ALU.mult, op1=ALU.mult))
    tok_attn = sV.tok()
    stAT.close()
    nc.leave_named_scope("d_attnB", _sc[0], False)

    tok_wo = emit_big_load(p, WQ.rearrange("p (c f) -> p c f", c=NCH),
                           wo.rearrange("(c p) f -> p c f", p=128), cast=True,
                           deps=[tok_attn])
    tok_t0r = restore_flat(p, A16[:], tsp, [tok_attn])
    stWO = ExitStack()
    RO = stWO.enter_context(nc.psum_tensor("df_ro", [128, 512], F32))
    RO2 = stWO.enter_context(nc.psum_tensor("df_ro2", [128, 512], F32))
    tW0, vW0, sW0 = sT.n, sV.n, sS.n
    with p.blockx() as blk:
        @blk.tensor
        def _(tensor):
            wait(tensor, tok_wo, tok_t0r, tok_bo)
            for i in range(32):
                oc, sb = divmod(i, 4)
                ROx = RO if i % 2 == 0 else RO2
                if i >= 2:
                    wait(tensor, (sV, vW0 + (i - 2) + 1))
                for pc in range(NCH):
                    mm = tensor.matmul(
                        out=ROx[:, 0:512],
                        lhsT=WQ[:, pc * 1024 + 128 * oc: pc * 1024 + 128 * (oc + 1)],
                        rhs=CTXF[:, pc * S + 512 * sb: pc * S + 512 * (sb + 1)],
                        start=(pc == 0), stop=(pc == NCH - 1))
                sT.inc(mm)

        @blk.scalar
        def _(scalar):
            for i in range(32):
                oc, sb = divmod(i, 4)
                wait(scalar, (sT, tW0 + i + 1))
                sS.inc(scalar.activation(
                    out=WOTMP[:, (i % 2) * 512:(i % 2) * 512 + 512],
                    in_=(RO if i % 2 == 0 else RO2)[:, 0:512],
                    func=AF.Identity, bias=BO[:, oc:oc + 1]))

        @blk.vector
        def _(vector):
            for i in range(32):
                oc, sb = divmod(i, 4)
                wait(vector, (sS, sW0 + i + 1))
                sV.inc(vector.tensor_add(
                    out=B16[:, oc * S + 512 * sb: oc * S + 512 * (sb + 1)],
                    in0=WOTMP[:, (i % 2) * 512:(i % 2) * 512 + 512],
                    in1=A16[:, oc * S + 512 * sb: oc * S + 512 * (sb + 1)]))
    tok_h = sV.tok()
    stWO.close()

    ln_scr2 = (A16[:, 0:1024], LNBF32[0:1, 0:6144], A16F[:, 1024:3072],
               A16F[:, 3072:5120], A16[:, 10240:11264])
    tok_ln2 = emit_layernorm(p, B16, LN1B, "df_ln2",
                             deps=[tok_h, tok_l2w, tok_l2b],
                             affine=(LNP[:, 16:24], LNP[:, 24:32]), scratch=ln_scr2)
    tok_hspill = spill_flat(p, B16[:], hsp, [tok_ln2])

    stPG = ExitStack()
    PG = stPG.enter_context(nc.psum_tensor("df_pg", [128, 128], F32))
    tG0 = sT.n
    with p.blockx() as blk:
        @blk.tensor
        def _(tensor):
            wait(tensor, tok_ln2, tok_wg)
            for t in range(16):
                for dc in range(NCH):
                    mm = tensor.matmul(out=PG[:, 8 * t:8 * (t + 1)],
                                       lhsT=LN1B[:, dc * S + 128 * t: dc * S + 128 * (t + 1)],
                                       rhs=WG16[:, 8 * dc:8 * (dc + 1)],
                                       start=(dc == 0), stop=(dc == NCH - 1))
            sT.inc(mm)

        @blk.vector
        def _(vector):
            wait(vector, (sT, tG0 + 1))
            r3 = lambda buf: buf.rearrange("p (t e) -> p t e", t=16)
            gm3 = lambda off: bass.AP(GM, off, [[48, 128], [1, 16], [0, 8]])
            t1 = sV.inc(vector.tensor_copy(out=GT[:], in_=PG[:]))
            wait(vector, t1)
            t2 = sV.inc(vector.tensor_reduce(out=GM[:, 0:16], in_=r3(GT[:]),
                                             axis=AX.X, op=ALU.max))
            wait(vector, t2)
            t3 = sV.inc(vector.tensor_tensor(out=r3(GMK[:]), in0=r3(GT[:]),
                                             in1=gm3(0), op=ALU.is_ge))
            wait(vector, t3)
            t4 = sV.inc(vector.tensor_scalar_mul(out=WE[:], in0=GMK[:], scalar1=1e30))
            wait(vector, t4)
            t5 = sV.inc(vector.tensor_sub(out=WE[:], in0=GT[:], in1=WE[:]))
            wait(vector, t5)
            t6 = sV.inc(vector.tensor_reduce(out=GM[:, 16:32], in_=r3(WE[:]),
                                             axis=AX.X, op=ALU.max))
            wait(vector, t6)
            t7 = sV.inc(vector.tensor_tensor(out=r3(WE[:]), in0=r3(WE[:]),
                                             in1=gm3(16), op=ALU.is_ge))
            wait(vector, t7)
            t9 = sV.inc(vector.tensor_add(out=GMK[:], in0=GMK[:], in1=WE[:]))
            wait(vector, t9)
            t10 = sV.inc(vector.tensor_scalar_mul(out=GM[:, 32:48], in0=GM[:, 0:16],
                                                  scalar1=-1.0))
            wait(vector, t10)
            sV.inc(vector.tensor_tensor(out=r3(GT[:]), in0=r3(GT[:]),
                                        in1=gm3(32), op=ALU.add))
        vG1 = sV.n

        @blk.scalar
        def _(scalar):
            wait(scalar, (sV, vG1))
            sS.inc(scalar.activation(out=GT[:], in_=GT[:], func=AF.Exp))
        sG1 = sS.n
    with p.blockx() as blk:
        @blk.vector
        def _(vector):
            wait(vector, (sS, sG1))
            r3 = lambda buf: buf.rearrange("p (t e) -> p t e", t=16)
            gm3 = lambda off: bass.AP(GM, off, [[48, 128], [1, 16], [0, 8]])
            t1 = sV.inc(vector.tensor_mul(out=WE[:], in0=GT[:], in1=GMK[:]))
            wait(vector, t1)
            t2 = sV.inc(vector.tensor_reduce(out=GM[:, 0:16], in_=r3(WE[:]),
                                             axis=AX.X, op=ALU.add))
            wait(vector, t2)
            t3 = sV.inc(vector.reciprocal(out=GM[:, 0:16], in_=GM[:, 0:16]))
            wait(vector, t3)
            sV.inc(vector.tensor_tensor(out=r3(WE[:]), in0=r3(WE[:]), in1=gm3(0),
                                        op=ALU.mult))
    tok_we = sV.tok()
    stPG.close()

    def w1slice(dc, hc):
        buf = A16 if dc < 4 else B16
        return buf[:, (dc % 4) * 4096 + 128 * hc: (dc % 4) * 4096 + 128 * (hc + 1)]

    _sc2 = nc.enter_named_scope("d_moe", False)
    W1Rtoks = [None] * 8
    W1Rtoks[0] = [
        emit_big_load(p, A16[:].rearrange("p (c f) -> p c f", c=4),
                      w1b[0:1, 0:512, :].rearrange("o (c p) f -> p (o c) f", p=128),
                      deps=[tok_hspill, tok_we, tok_conv]),
        emit_big_load(p, B16[:].rearrange("p (c f) -> p c f", c=4),
                      w1b[0:1, 512:1024, :].rearrange("o (c p) f -> p (o c) f", p=128),
                      deps=[tok_hspill, tok_we, tok_conv])]
    acc_tok = {}
    prev_out_tok = None
    prev_g_toks = None
    prev_g_toks8 = None
    prev_flush_tok = None
    tok_b2e = None
    for e in range(8):
        b2dep = [tok_hspill] if e == 0 else [prev_out_tok]
        tok_b2e = emit_big_load(p, B2RE[:], b2r[0:1, 1024 * e:1024 * (e + 1)],
                                cast=True, deps=b2dep)
        for sg in range(4):
            stPH = ExitStack()
            PHa = stPH.enter_context(nc.psum_tensor("df_pha_%d_%d" % (e, sg), [128, 512], F32))
            PHb = stPH.enter_context(nc.psum_tensor("df_phb_%d_%d" % (e, sg), [128, 512], F32))
            tP0, sP0 = sT.n, sS.n
            with p.blockx() as blk:
                @blk.tensor
                def _(tensor):
                    wait(tensor, *W1Rtoks[e])
                    if prev_flush_tok is not None:
                        wait(tensor, prev_flush_tok)
                    for hc in range(32):
                        PHx = PHa if hc % 2 == 0 else PHb
                        if hc >= 2:
                            wait(tensor, (sS, sP0 + (hc - 2) + 1))
                        for dc in range(NCH):
                            mm = tensor.matmul(
                                out=PHx[:, 0:512],
                                lhsT=w1slice(dc, hc),
                                rhs=T1[:, dc * S + 512 * sg: dc * S + 512 * (sg + 1)],
                                start=(dc == 0), stop=(dc == NCH - 1))
                        sT.inc(mm)

                @blk.scalar
                def _(scalar):
                    wait(scalar, tok_b1c)
                    for hc in range(32):
                        wait(scalar, (sT, tP0 + hc + 1))
                        if prev_g_toks is not None:
                            wait(scalar, prev_g_toks[hc // 4])
                        sS.inc(scalar.activation(
                            out=HBS[:, 512 * hc:512 * (hc + 1)],
                            in_=(PHa if hc % 2 == 0 else PHb)[:, 0:512],
                            func=AF.Gelu, bias=B1C[:, 32 * e + hc:32 * e + hc + 1]))
            tok_gelu = sS.tok()
            stPH.close()

            if sg == 3 and e < 7:
                W1Rtoks[e + 1] = [
                    emit_big_load(p, A16[:].rearrange("p (c f) -> p c f", c=4),
                                  w1b[e + 1:e + 2, 0:512, :].rearrange("o (c p) f -> p (o c) f", p=128),
                                  deps=[tok_gelu]),
                    emit_big_load(p, B16[:].rearrange("p (c f) -> p c f", c=4),
                                  w1b[e + 1:e + 2, 512:1024, :].rearrange("o (c p) f -> p (o c) f", p=128),
                                  deps=[tok_gelu])]

            stPO = ExitStack()
            PO = [stPO.enter_context(nc.psum_tensor("df_po_%d_%d_%d" % (e, sg, j),
                                                    [128, 512], F32))
                  for j in range(8)]
            tO0 = sT.n
            sGel0 = tok_gelu[1] - 32
            g_toks = []
            for g in range(8):
                dstw = W2G[g % 3]
                if g >= 3:
                    deps_w2 = [(sT, tO0 + 32 * (g - 2))]
                elif prev_g_toks8 is not None:
                    deps_w2 = [prev_g_toks8[{0: 6, 1: 7, 2: 5}[g]], tok_conv]
                else:
                    deps_w2 = [tok_conv]
                tok_w2 = emit_big_load(p, dstw.rearrange("p (c f) -> p c f", c=4),
                                       w2b[e:e + 1, 512 * g:512 * (g + 1), :]
                                       .rearrange("o (c p) f -> p (o c) f", p=128),
                                       deps=deps_w2,
                                       sem=p.sem("w2s_%d" % (g % 3)))
                with p.blockx() as blk:
                    @blk.tensor
                    def _(tensor):
                        wait(tensor, tok_w2)
                        if g == 0:
                            wait(tensor, tok_gelu, tok_b2e)
                        else:
                            wait(tensor, (tok_gelu[0], sGel0 + 4 * g + 4))
                            if prev_flush_tok is not None:
                                wait(tensor, prev_flush_tok)
                        for hc in range(4 * g, 4 * g + 4):
                            for j in range(8):
                                sblk, oh = divmod(j, 2)
                                sT.inc(tensor.matmul(
                                    out=PO[j][:, 0:512],
                                    lhsT=HBS[:, 512 * hc + 128 * sblk:
                                             512 * hc + 128 * (sblk + 1)],
                                    rhs=dstw[:, (hc - 4 * g) * 1024 + 512 * oh:
                                             (hc - 4 * g) * 1024 + 512 * (oh + 1)],
                                    start=(hc == 0), stop=False))
                g_toks.append((p.sems["t"], tO0 + 32 * (g + 1)))
            tB0, vB0 = sT.n, sV.n
            with p.blockx() as blk:
                @blk.tensor
                def _(tensor):
                    for j in range(8):
                        sblk, oh = divmod(j, 2)
                        sT.inc(tensor.matmul(
                            out=PO[j][:, 0:512],
                            lhsT=p.ones_row_bf16[0:1, 0:128],
                            rhs=B2RE[0:1, 512 * oh: 512 * (oh + 1)],
                            start=False, stop=True))

                @blk.vector
                def _(vector):
                    if e == 0 and sg == 0:
                        wait(vector, tok_we)
                    for j in range(8):
                        sblk, oh = divmod(j, 2)
                        t = 4 * sg + sblk
                        wait(vector, (sT, tB0 + j + 1))
                        dst = ACC[:, 1024 * t + 512 * oh: 1024 * t + 512 * (oh + 1)]
                        wecol = WE[:, 8 * t + e:8 * t + e + 1]
                        if e == 0:
                            sV.inc(vector.tensor_scalar_mul(out=dst, in0=PO[j][:, 0:512],
                                                            scalar1=wecol))
                        else:
                            wait(vector, acc_tok[(t, oh)])
                            sV.inc(vector.scalar_tensor_tensor(
                                out=dst, in0=PO[j][:, 0:512], scalar=wecol,
                                in1=dst, op0=ALU.mult, op1=ALU.add))
                        acc_tok[(t, oh)] = sV.tok()
            prev_out_tok = sT.tok()
            prev_g_toks = g_toks
            prev_g_toks8 = g_toks
            prev_flush_tok = sV.tok()
            stPO.close()
    tok_moe = prev_flush_tok
    nc.leave_named_scope("d_moe", _sc2[0], False)

    tok_xr1 = restore_flat(p, A16F, xsp[:, 0:8192], [tok_moe])
    tok_xr2 = restore_flat(p, B16F, xsp[:, 8192:16384], [tok_moe])
    tok_hr = restore_flat(p, HR, hsp, [tok_moe])
    stFN = ExitStack()
    PFa = stFN.enter_context(nc.psum_tensor("df_pfa", [128, 128], F32))
    PFb = stFN.enter_context(nc.psum_tensor("df_pfb", [128, 128], F32))
    vF0 = sV.n
    with p.blockx() as blk:
        @blk.vector
        def _(vector):
            wait(vector, tok_xr1, tok_xr2, tok_hr)
            for c in range(NCH):
                xr = A16F if c < 4 else B16F
                sl = xr[:, (c % 4) * 2048:(c % 4) * 2048 + 2048]
                sV.inc(vector.tensor_add(out=sl, in0=sl,
                                         in1=HR[:, c * S:(c + 1) * S]))
    vF1 = sV.n
    tF0 = sT.n
    with p.blockx() as blk:
        @blk.tensor
        def _(tensor):
            wait(tensor, (sV, vF1), p.tok_idf)
            for i in range(128):
                t, c = divmod(i, 8)
                xr = A16F if c < 4 else B16F
                if i >= 2:
                    wait(tensor, (sV, vF1 + (i - 2) + 1))
                sT.inc(tensor.transpose(
                    out=(PFa if i % 2 == 0 else PFb)[:],
                    in_=xr[:, (c % 4) * 2048 + 128 * t: (c % 4) * 2048 + 128 * (t + 1)],
                    identity=p.IDF[:]))

        @blk.vector
        def _(vector):
            for i in range(128):
                t, c = divmod(i, 8)
                OUTS = OUTSa if t < 8 else OUTSb
                wait(vector, (sT, tF0 + i + 1))
                sV.inc(vector.scalar_tensor_tensor(
                    out=OUTS[:, 1024 * (t % 8) + 128 * c: 1024 * (t % 8) + 128 * (c + 1)],
                    in0=(PFa if i % 2 == 0 else PFb)[:],
                    scalar=1.0,
                    in1=ACC[:, 1024 * t + 128 * c: 1024 * t + 128 * (c + 1)],
                    op0=ALU.mult, op1=ALU.add))
    vF2 = sV.n
    stFN.close()
    ldy1 = p.sem("uld%d" % p._ldctr); p._ldctr += 1
    ldy2 = p.sem("uld%d" % p._ldctr); p._ldctr += 1
    with p.blockx() as blk:
        @blk.gpsimd
        def _(gpsimd):
            wait(gpsimd, (sV, vF1 + 64))
            ldy1.inc(gpsimd.dma_start(
                out=y_out[0:1024, :].rearrange("(t p) f -> p t f", p=128),
                in_=OUTSa[:, 0:8192].rearrange("p (t f) -> p t f", t=8)), 16)
            wait(gpsimd, (sV, vF2))
            ldy2.inc(gpsimd.dma_start(
                out=y_out[1024:2048, :].rearrange("(t p) f -> p t f", p=128),
                in_=OUTSb[:].rearrange("p (t f) -> p t f", t=8)), 16)
    p.load_toks.append(ldy1.tok())
    p.load_toks.append(ldy2.tok())
    st2.close()
    return [ldy1.tok(), ldy2.tok()]


def build_l1ffn_prog():
    if "l1ffn" in _prog_cache:
        return _prog_cache["l1ffn"]
    p = Prog()
    xin = p.din("xin", [1024, S])
    y = p.dout("y", [S, 1024])
    p.init_consts()
    _load_phase_consts(p, need=("idf",))
    X = p.sb("X", [128, NCH * S], F32)
    tok_x = emit_big_load(p, X[:].rearrange("p (c s) -> p c s", c=NCH),
                          xin.rearrange("(c p) s -> p c s", p=128))
    toks = emit_l1_ffn(p, X, deps=[tok_x], y_out=y)
    emit_final_wait(p, toks)
    nc = p.close()
    _prog_cache["l1ffn"] = (nc, p)
    return nc, p


def _prep_l1ffn(inputs):
    d = {}
    cols = lambda v, k: np.ascontiguousarray(v.reshape(k, 128).T)
    d["l1f_wqkv"] = inputs["L1_ffn_w_qkv"]
    d["l1f_wo"] = inputs["L1_ffn_w_o"]
    d["l1f_ln1w"] = cols(inputs["L1_ffn_ln1_w"], 8)
    d["l1f_ln1b"] = cols(inputs["L1_ffn_ln1_b"], 8)
    d["l1f_ln2w"] = cols(inputs["L1_ffn_ln2_w"], 8)
    d["l1f_ln2b"] = cols(inputs["L1_ffn_ln2_b"], 8)
    d["l1f_bqkv"] = cols(inputs["L1_ffn_b_qkv"], 24)
    d["l1f_bqv"] = np.ascontiguousarray(inputs["L1_ffn_b_qkv"][2048:3072].reshape(1, 1024))
    d["l1f_bo"] = cols(inputs["L1_ffn_b_o"], 8)
    d["l1f_wgate"] = np.ascontiguousarray(
        inputs["L1_ffn_w_gate"].reshape(8, 128, 8).transpose(1, 0, 2).reshape(128, 64))
    d["l1f_w1"] = inputs["L1_ffn_w1"]
    d["l1f_b1c"] = np.ascontiguousarray(
        inputs["L1_ffn_b1"].reshape(8, 32, 128).transpose(2, 0, 1).reshape(128, 256))
    d["l1f_w2"] = inputs["L1_ffn_w2"]
    d["l1f_b2r"] = np.ascontiguousarray(inputs["L1_ffn_b2"].reshape(1, 8192))
    return d


def _prep_common(inputs):
    c = {}
    c["c_idf"] = np.eye(128, dtype=np.float32)
    c["c_idb"] = np.eye(128).astype(BF)
    e415 = np.zeros((4, 15), dtype=np.float32)
    off = 0
    for e, r in enumerate([1, 2, 4, 8]):
        e415[e, off:off + r] = 1.0
        off += r
    c["c_e415"] = e415.astype(BF)
    return c


def _prep_l0ffn(inputs):
    d = {}
    d["l0f_wt"] = np.ascontiguousarray(inputs["L0_ffn_W"].T)
    d["l0f_wpt"] = np.ascontiguousarray(inputs["L0_ffn_Wp"].T)
    d["l0f_bp"] = np.ascontiguousarray(inputs["L0_ffn_bp"].reshape(2, 128).T)
    d["l0f_sim"] = np.ascontiguousarray(
        inputs["L0_ffn_sim"].reshape(2, 128, 4).transpose(1, 0, 2).reshape(128, 8))
    d["l0f_temp"] = inputs["L0_ffn_temp"].reshape(1, 1)
    wd = np.concatenate([inputs["L0_ffn_Wd0"], inputs["L0_ffn_Wd1"],
                         inputs["L0_ffn_Wd2"], inputs["L0_ffn_Wd3"]], axis=0)
    d["l0f_wdt"] = np.ascontiguousarray(
        wd.T.reshape(8, 128, 15).transpose(1, 0, 2).reshape(128, 120))
    wu = np.concatenate([inputs["L0_ffn_Wu0"], inputs["L0_ffn_Wu1"],
                         inputs["L0_ffn_Wu2"], inputs["L0_ffn_Wu3"]], axis=1)
    d["l0f_wut"] = np.ascontiguousarray(wu.T)
    return d


def _load_phase_consts(p, need=("idf", "idb", "e415")):
    if "idf" in need:
        p.IDF = p.sb("c_idf_sb", [128, 128], F32)
        p.tok_idf = emit_big_load(p, p.IDF[:], p.din("c_idf", [128, 128]))
    if "idb" in need:
        p.IDB = p.sb("c_idb_sb", [128, 128], BF16)
        p.tok_idb = emit_big_load(p, p.IDB[:], p.din("c_idb", [128, 128], BF16))
    if "e415" in need:
        p.E415 = p.sb("c_e415_sb", [4, 15], BF16)
        p.tok_e415 = emit_big_load(p, p.E415[:], p.din("c_e415", [4, 15], BF16))


_prog_cache = {}


def build_l0ffn_prog():
    if "l0ffn" in _prog_cache:
        return _prog_cache["l0ffn"]
    p = Prog()
    xin = p.din("xin", [1024, S])
    xout = p.dout("xout", [1024, S])
    p.init_consts()
    _load_phase_consts(p)
    X = p.sb("X", [128, NCH * S], F32)
    tok_x = emit_big_load(p, X[:].rearrange("p (c s) -> p c s", c=NCH),
                          xin.rearrange("(c p) s -> p c s", p=128))
    tok = emit_l0_ffn(p, X, deps=[tok_x])
    tok_st = emit_store_xT(p, X, xout, [tok])
    emit_final_wait(p, [tok_st])
    nc = p.close()
    _prog_cache["l0ffn"] = (nc, p)
    return nc, p


def build_fused_prog():
    if "fused" in _prog_cache:
        return _prog_cache["fused"]
    p = Prog()
    xin = p.din("xin", [1024, S])
    y = p.dout("y", [S, 1024])
    p.init_consts()
    _load_phase_consts(p, need=("idf", "idb", "e415"))
    X = p.sb("X", [128, NCH * S], F32)
    tok_x = emit_big_load(p, X[:].rearrange("p (c s) -> p c s", c=NCH),
                          xin.rearrange("(c p) s -> p c s", p=128))
    with p.nc.named_scope("l0_attn"):
        tok = emit_l0_attn(p, X, deps=[tok_x])
    moe_conv = emit_moe_conv(p)
    with p.nc.named_scope("l0_ffn"):
        tok = emit_l0_ffn(p, X, deps=[tok])
    with p.nc.named_scope("l1_attn"):
        tok = emit_l1_attn(p, X, deps=[tok])
    with p.nc.named_scope("l1_ffn"):
        toks = emit_l1_ffn(p, X, deps=[tok], y_out=y, moe_conv=moe_conv)
    emit_final_wait(p, toks)
    nc = p.close()
    _prog_cache["fused"] = (nc, p)
    return nc, p


def kernel(**inputs):
    emb = inputs["L0_attn_emb"]
    ncore = 8
    prep = _prep_common(inputs)
    prep.update(_prep_l0attn(inputs))
    prep.update(_prep_l0ffn(inputs))
    prep.update(_prep_l1attn(inputs))
    prep.update(_prep_l1ffn(inputs))

    nc, p = build_fused_prog()
    in_maps = []
    for i in range(ncore):
        m = {"xin": np.ascontiguousarray(emb[i].T)}
        m.update(prep)
        in_maps.append(m)
    res = run_bass_kernel_spmd(nc, in_maps, core_ids=list(range(ncore)))
    return np.stack([r["y"] for r in res.results], axis=0)


def kernel_unfused(**inputs):
    emb = inputs["L0_attn_emb"]
    ncore = 8
    prep = _prep_common(inputs)
    prep.update(_prep_l0attn(inputs))
    prep.update(_prep_l0ffn(inputs))
    prep.update(_prep_l1attn(inputs))
    prep.update(_prep_l1ffn(inputs))
    x = [np.ascontiguousarray(emb[i].T) for i in range(ncore)]
    for name, builder in [("l0attn", build_l0attn_prog), ("l0ffn", build_l0ffn_prog),
                          ("l1attn", build_l1attn_prog), ("l1ffn", build_l1ffn_prog)]:
        nc, p = builder()
        in_maps = []
        for i in range(ncore):
            m = {"xin": x[i]}
            m.update(prep)
            in_maps.append(m)
        res = run_bass_kernel_spmd(nc, in_maps, core_ids=list(range(ncore)))
        if name == "l1ffn":
            return np.stack([r["y"] for r in res.results], axis=0)
        x = [np.ascontiguousarray(r["xout"]) for r in res.results]
